# version 21
# baseline (speedup 1.0000x reference)
import numpy as np
import ml_dtypes
from contextlib import ExitStack

import concourse.bass as bass
import concourse.mybir as mybir
from concourse.bass_utils import run_bass_kernel_spmd

F32 = mybir.dt.float32
BF16 = mybir.dt.bfloat16
AF = mybir.ActivationFunctionType
ALU = mybir.AluOpType
AX = mybir.AxisListType

T = 4096
D = 1024
H = 16
DK = 64
SEG = 512
S = 8
FF = 4096
JT = 8
W_ELEMS = float(FF * D)
EPS = 1e-5
OFFS = (-2, -1, 1, 2)


class SemTok:
    __slots__ = ("h", "count", "name")

    def __init__(self, h, name):
        self.h = h
        self.count = 0
        self.name = name


class Sched:
    ENGINES = ("tensor", "vector", "scalar", "sync", "gpsimd")

    def __init__(self, nc, ctx):
        self.nc = nc
        self.ctx = ctx
        self.prog = {e: [] for e in self.ENGINES}
        self._hi = {e: {} for e in self.ENGINES}

    def sem(self, name):
        h = self.ctx.enter_context(self.nc.semaphore(name))
        return SemTok(h, name)

    def op(self, engine, build, waits=(), inc=None, inc_n=1):
        wl = []
        hi = self._hi[engine]
        for tok in waits:
            if tok is None:
                continue
            sem, val = tok
            if val <= 0:
                continue
            if hi.get(id(sem), 0) >= val:
                continue
            hi[id(sem)] = val
            wl.append((sem.h, val))
        incs = []
        out_tok = None
        if inc is not None:
            inc.count += inc_n
            incs.append((inc.h, inc_n))
            out_tok = (inc, inc.count)

        def thunk(eng, wl=wl, incs=incs, build=build):
            for (h, v) in wl:
                eng.wait_ge(h, v)
            inst = build(eng)
            for (h, n) in incs:
                inst.then_inc(h, n)

        self.prog[engine].append(thunk)
        return out_tok

    def group(self, engine, items, inc=None, inc_n=1):
        tok = None
        for i, (build, waits) in enumerate(items):
            last = i == len(items) - 1
            tok = self.op(engine, build, waits=waits,
                          inc=inc if last else None, inc_n=inc_n)
        return tok

    def emit(self):
        block = self.ctx.enter_context(self.nc.Block())
        progs = self.prog

        @block.tensor
        def _(eng):
            for f in progs["tensor"]:
                f(eng)

        @block.vector
        def _(eng):
            for f in progs["vector"]:
                f(eng)

        @block.scalar
        def _(eng):
            for f in progs["scalar"]:
                f(eng)

        @block.sync
        def _(eng):
            for f in progs["sync"]:
                f(eng)

        @block.gpsimd
        def _(eng):
            for f in progs["gpsimd"]:
                f(eng)


def build_program(nc: bass.Bass, ctx: ExitStack):
    x_h = nc.dram_tensor("x", [T, D], F32, kind="ExternalInput").ap()
    wq_h = nc.dram_tensor("wq", [D, D], F32, kind="ExternalInput").ap()
    wk_h = nc.dram_tensor("wk", [D, D], F32, kind="ExternalInput").ap()
    wv_h = nc.dram_tensor("wv", [D, D], F32, kind="ExternalInput").ap()
    wo_h = nc.dram_tensor("wo", [D, D], F32, kind="ExternalInput").ap()
    bq_h = nc.dram_tensor("bq", [D], F32, kind="ExternalInput").ap()
    bk_h = nc.dram_tensor("bk", [D], F32, kind="ExternalInput").ap()
    bv_h = nc.dram_tensor("bv", [D], F32, kind="ExternalInput").ap()
    bo_h = nc.dram_tensor("bo", [D], F32, kind="ExternalInput").ap()
    w1_h = nc.dram_tensor("w1", [FF, D], F32, kind="ExternalInput").ap()
    b1_h = nc.dram_tensor("b1", [FF], F32, kind="ExternalInput").ap()
    w2_h = nc.dram_tensor("w2", [D, FF], F32, kind="ExternalInput").ap()
    b2_h = nc.dram_tensor("b2", [D], F32, kind="ExternalInput").ap()
    out_h = nc.dram_tensor("out", [T, D], F32, kind="ExternalOutput").ap()

    xnT_h = nc.dram_tensor("xnT_sc", [S, 128, JT, SEG], BF16, kind="Internal").ap()
    qT_h = nc.dram_tensor("qT_sc", [S, 128, JT, SEG], BF16, kind="Internal").ap()
    kT_h = nc.dram_tensor("kT_sc", [S, 128, JT, SEG], BF16, kind="Internal").ap()
    v_h = nc.dram_tensor("v_sc", [S, 128, 4, 16 * 65], BF16, kind="Internal").ap()
    s_h = nc.dram_tensor("s_sc", [T, D], F32, kind="Internal").ap()
    w1t_h = nc.dram_tensor("w1t_sc", [128, 32, 8, 128], BF16, kind="Internal").ap()
    w2t_h = nc.dram_tensor("w2t_sc", [128, 32, 1024], BF16, kind="Internal").ap()
    hT_h = nc.dram_tensor("hT_sc", [S, 128, 32, SEG], BF16, kind="Internal").ap()

    id_bf_h = nc.inline_tensor(np.eye(128, dtype=ml_dtypes.bfloat16), name="id_bf_h").ap()
    id_f32_h = nc.inline_tensor(np.eye(128, dtype=np.float32), name="id_f32_h").ap()
    ones_r_h = nc.inline_tensor(np.ones((1, 128), np.float32), name="ones_r_h").ap()
    ones_c_h = nc.inline_tensor(np.ones((128, 1), np.float32), name="ones_c_h").ap()

    sb = ctx.enter_context
    S_ = Sched(nc, ctx)
    OP = S_.op
    GRP = S_.group

    sem_t = S_.sem("t")
    sem_v = S_.sem("v")
    sem_a = S_.sem("a")
    sem_ld = S_.sem("ld")
    sem_out = S_.sem("outd")
    ld_x = [S_.sem(f"ldx{i}") for i in range(2)]
    ld_w = [S_.sem(f"ldw{i}") for i in range(4)]
    ld_qa = [S_.sem(f"ldqa{i}") for i in range(2)]
    ld_ka = [S_.sem(f"ldka{i}") for i in range(2)]
    ld_va = [S_.sem(f"ldva{i}") for i in range(2)]
    ld_vd = [S_.sem(f"ldvd{i}") for i in range(2)]
    ld_xg = [S_.sem(f"ldxg{i}") for i in range(2)]
    ld_hg = [S_.sem(f"ldhg{i}") for i in range(2)]
    ld_sf = [S_.sem(f"ldsf{i}") for i in range(2)]
    ld_xf = [S_.sem(f"ldxf{i}") for i in range(2)]
    ld_wc = [S_.sem(f"ldwc{i}") for i in range(4)]
    ld_big = S_.sem("ldbig")
    sp_q = [S_.sem(f"spq{i}") for i in range(2)]
    sp_k = [S_.sem(f"spk{i}") for i in range(2)]
    sp_v = [S_.sem(f"spv{i}") for i in range(2)]
    sp_xn = [S_.sem(f"spxn{i}") for i in range(2)]
    sp_s = [S_.sem(f"sps{i}") for i in range(2)]
    sp_w1 = [S_.sem(f"spw1{i}") for i in range(2)]
    sp_w2 = [S_.sem(f"spw2{i}") for i in range(2)]
    sp_h = [S_.sem(f"sph{i}") for i in range(2)]

    id_bf = sb(nc.sbuf_tensor("id_bf", [128, 128], BF16))
    id_f32 = sb(nc.sbuf_tensor("id_f32", [128, 128], F32))
    ones_r = sb(nc.sbuf_tensor("ones_r", [1, 128], F32))
    ones_c = sb(nc.sbuf_tensor("ones_c", [128, 1], F32))
    bqc = sb(nc.sbuf_tensor("bqc", [128, JT], F32))
    bkc = sb(nc.sbuf_tensor("bkc", [128, JT], F32))
    b1c = sb(nc.sbuf_tensor("b1c", [128, 32], F32))
    brow = sb(nc.sbuf_tensor("brow", [64, 128], F32))
    bvb = sb(nc.sbuf_tensor("bvb", [128, D], F32))
    brow1 = sb(nc.sbuf_tensor("brow1", [1, D], F32))
    th1 = sb(nc.sbuf_tensor("th1", [128, 1], F32))
    th1n = sb(nc.sbuf_tensor("th1n", [128, 1], F32))
    th2 = sb(nc.sbuf_tensor("th2", [128, 1], F32))
    th2n = sb(nc.sbuf_tensor("th2n", [128, 1], F32))
    scal = sb(nc.sbuf_tensor("scal", [1, 2], F32))
    eps_sb = sb(nc.sbuf_tensor("eps_sb", [128, 1], F32))

    psA = sb(nc.psum_tensor("psA", [128, 2048], F32))
    psB = sb(nc.psum_tensor("psB", [128, 2048], F32))

    for dst, src in ((id_bf[:], id_bf_h), (id_f32[:], id_f32_h),
                     (ones_r[:], ones_r_h), (ones_c[:], ones_c_h)):
        OP("sync", lambda e, d=dst, s=src: e.dma_start(out=d, in_=s),
           inc=sem_ld, inc_n=16)
    OP("sync", lambda e: e.dma_start(out=brow[0:8, :],
                                     in_=bq_h.rearrange("(t c) -> t c", t=8)),
       inc=sem_ld, inc_n=16)
    OP("sync", lambda e: e.dma_start(out=brow[32:40, :],
                                     in_=bk_h.rearrange("(t c) -> t c", t=8)),
       inc=sem_ld, inc_n=16)
    OP("sync", lambda e: e.dma_start(out=brow1[:],
                                     in_=bv_h.rearrange("(a c) -> a c", a=1)),
       inc=sem_ld, inc_n=16)
    tok_ld7 = (sem_ld, sem_ld.count)

    tok_zA = OP("vector", lambda e: e.memset(psA[:], 0.0), inc=sem_v)
    tok_eps = OP("vector", lambda e: e.memset(eps_sb[:], EPS), inc=sem_v)
    tok_zB = OP("vector", lambda e: e.memset(psB[:], 0.0), inc=sem_v)

    t1 = OP("tensor", lambda e: e.matmul(out=psB[:, 1536:1544], lhsT=brow[0:8, :],
                                         rhs=id_f32[0:8, 0:8], start=True, stop=True),
            waits=[tok_ld7, tok_zB], inc=sem_t)
    t2 = OP("tensor", lambda e: e.matmul(out=psB[:, 1544:1552], lhsT=brow[32:40, :],
                                         rhs=id_f32[32:40, 32:40], start=True, stop=True),
            inc=sem_t)
    tok_bqc = OP("vector", lambda e: e.tensor_copy(out=bqc[:], in_=psB[:, 1536:1544]),
                 waits=[t1], inc=sem_v)
    tok_bkc = OP("vector", lambda e: e.tensor_copy(out=bkc[:], in_=psB[:, 1544:1552]),
                 waits=[t2], inc=sem_v)
    bv2 = GRP("tensor", [
        (lambda e: e.matmul(out=psB[:, 1024:1536], lhsT=ones_r[0:1, :],
                            rhs=brow1[0:1, 0:512], start=True, stop=True),
         [tok_ld7, tok_zB]),
        (lambda e: e.matmul(out=psB[:, 1536 + 0:1536 + 512], lhsT=ones_r[0:1, :],
                            rhs=brow1[0:1, 512:1024], start=True, stop=True),
         [tok_bqc, tok_bkc]),
    ], inc=sem_t)
    vb1 = OP("vector", lambda e: e.tensor_copy(out=bvb[:], in_=psB[:, 1024:2048]),
             waits=[bv2], inc=sem_v)
    tok_b1row = OP("sync", lambda e: e.dma_start(
        out=brow[0:32, :], in_=b1_h.rearrange("(t c) -> t c", t=32)),
        waits=[t2], inc=sem_ld, inc_n=16)
    t3 = OP("tensor", lambda e: e.matmul(out=psB[:, 1536:1568], lhsT=brow[0:32, :],
                                         rhs=id_f32[0:32, 0:32], start=True, stop=True),
            waits=[tok_b1row, vb1, tok_bqc, tok_bkc], inc=sem_t)
    tok_b1c = OP("vector", lambda e: e.tensor_copy(out=b1c[:], in_=psB[:, 1536:1568]),
                 waits=[t3], inc=sem_v)
    tok_biases = tok_b1c

    es_pa = ExitStack()
    kdT = es_pa.enter_context(nc.sbuf_tensor("kdT", [128, S, JT, 128], BF16))
    vd = es_pa.enter_context(nc.sbuf_tensor("vd", [128, S, 16 * 65], BF16))

    es_p1 = ExitStack()
    p1 = es_p1.enter_context
    wq_sb = p1(nc.sbuf_tensor("wq_sb", [128, JT, D], BF16))
    wk_sb = p1(nc.sbuf_tensor("wk_sb", [128, JT, D], BF16))
    wv_sb = p1(nc.sbuf_tensor("wv_sb", [128, JT, D], BF16))
    wtmp = [p1(nc.sbuf_tensor(f"wtmp{i}", [128, 1024], F32)) for i in range(3)]
    x_sb = [p1(nc.sbuf_tensor(f"x_sb{i}", [128, 2, D], F32)) for i in range(2)]
    dump = p1(nc.sbuf_tensor("dump", [128, D], BF16))
    sg = [p1(nc.sbuf_tensor(f"sg{i}", [128, 4, 4], F32)) for i in range(2)]
    nw = [p1(nc.sbuf_tensor(f"nw{i}", [128, 32], F32)) for i in range(2)]
    xn_tm = [p1(nc.sbuf_tensor(f"xn_tm{i}", [128, D], BF16)) for i in range(4)]
    xnT_st = [p1(nc.sbuf_tensor(f"xnT_st{i}", [128, JT, SEG], BF16)) for i in range(2)]
    qk_st = [p1(nc.sbuf_tensor(f"qk_st{i}", [128, JT, SEG], BF16)) for i in range(2)]
    v_st = [p1(nc.sbuf_tensor(f"v_st{i}", [128, 4, 16 * 65], BF16)) for i in range(1)]
    wacc = p1(nc.sbuf_tensor("wacc", [128, 4], F32))
    wred = p1(nc.sbuf_tensor("wred", [128, 1], F32))
    wg2 = [p1(nc.sbuf_tensor(f"wg2{i}", [128, 1024], BF16)) for i in range(1)]
    wtern = [p1(nc.sbuf_tensor(f"wtern{i}", [128, 1024], BF16)) for i in range(3)]
    wstage = [p1(nc.sbuf_tensor(f"wstage{i}", [128, 8, 128], BF16)) for i in range(2)]

    last_w_use = [None, None, None, None]
    wi = 0
    tok_wqkv = None
    for (wsb, whbm) in ((wq_sb, wq_h), (wk_sb, wk_h), (wv_sb, wv_h)):
        for dj in range(JT):
            sl = wi % 3
            wi += 1
            tokl = OP("sync", lambda e, sl=sl, whbm=whbm, dj=dj: e.dma_start(
                out=wtmp[sl][:], in_=whbm[dj * 128:(dj + 1) * 128, :]),
                waits=[last_w_use[sl]], inc=ld_w[sl], inc_n=16)
            tok_wqkv = OP("vector", lambda e, sl=sl, wsb=wsb, dj=dj: e.tensor_copy(
                out=wsb[:, dj, :], in_=wtmp[sl][:]), waits=[tokl], inc=sem_v)
            last_w_use[sl] = tok_wqkv

    proj_ps_last = {0: None, 1: None, 2: None, 3: None}
    lnps = {"prev_ev": None}
    pslot = {"i": 0}
    ln_xn_tok = [[None] * 4 for _ in range(S)]
    spq_tok = [None] * S
    spk_tok = [None] * S
    spv_tok = [None] * S
    spxn_tok = [None] * S
    kd_tok = [None] * S
    vd_tok = [None] * S
    w1t_last_sp = None

    x_use = [None, None]
    xn_use = [None, None, None, None]
    xnT_use = [None, None]
    qk_use = [None, None]
    v_use = [None, None]
    stats_use = [None, None]
    w1g_use = [None]
    wstage_use = [None, None]

    w1t_sp_tok = [None] * 32
    wscale_idx = 0
    wtern_idx = 0
    tok_acc_init = OP("vector", lambda e: e.memset(wacc[:], 0.0), inc=sem_v)
    acc_last = {1: tok_acc_init, 2: tok_acc_init}
    acc_parity = {1: 0, 2: 0}
    tok_thr = [None, None]
    tern_state = {"w1_psB_free": None, "w2_psB_free": None,
                  "tern_read": None}

    def emit_scale_tiles(n, queue="sync"):
        nonlocal wscale_idx, wi
        for _ in range(n):
            i = wscale_idx
            if i >= 64:
                return
            wscale_idx += 1
            if i < 32:
                src = w1_h[i * 128:(i + 1) * 128, :]
                widx = 1
            else:
                j = i - 32
                dt, qq = j // 4, j % 4
                src = w2_h[dt * 128:(dt + 1) * 128, qq * 1024:(qq + 1) * 1024]
                widx = 2
            sl = wi % 3
            wi += 1
            tokl = OP(queue, lambda e, sl=sl, src=src: e.dma_start(
                out=wtmp[sl][:], in_=src), waits=[last_w_use[sl]],
                inc=ld_w[sl], inc_n=16)
            tokr = OP("scalar", lambda e, sl=sl: e.activation(
                out=dump[:], in_=wtmp[sl][:], func=AF.Abs,
                accum_out=wred[:]), waits=[tokl, acc_last[widx], tok_biases],
                inc=sem_a)
            last_w_use[sl] = tokr
            par = acc_parity[widx]
            a_src = 2 * (widx - 1) + par
            a_dst = 2 * (widx - 1) + (1 - par)
            acc_parity[widx] = 1 - par
            acc_last[widx] = OP("vector",
                                lambda e, a_src=a_src, a_dst=a_dst: e.tensor_tensor(
                                    out=wacc[:, a_dst:a_dst + 1],
                                    in0=wacc[:, a_src:a_src + 1],
                                    in1=wred[:], op=ALU.add),
                                waits=[tokr], inc=sem_v)

    def finalize_scales(which):
        for widx, (th, thn) in ((1, (th1, th1n)), (2, (th2, th2n))):
            if widx != which:
                continue
            col = 2 * (widx - 1) + acc_parity[widx]
            oc = 1568 + 2 * widx
            tokm = OP("tensor", lambda e, col=col, oc=oc: e.matmul(
                out=psB[0:1, oc:oc + 1], lhsT=wacc[:, col:col + 1],
                rhs=ones_c[:, 0:1], start=True, stop=True),
                waits=[acc_last[widx], tok_b1c, tok_thr[0]], inc=sem_t)
            tokc = OP("vector", lambda e, widx=widx, oc=oc: e.tensor_scalar(
                out=scal[0:1, widx - 1:widx], in0=psB[0:1, oc:oc + 1],
                scalar1=0.5 / W_ELEMS, scalar2=0.5e-8, op0=ALU.mult, op1=ALU.max),
                waits=[tokm], inc=sem_v)
            tokb = OP("tensor", lambda e, widx=widx, oc=oc: e.matmul(
                out=psB[:, oc + 8:oc + 9], lhsT=ones_r[0:1, :],
                rhs=scal[0:1, widx - 1:widx], start=True, stop=True),
                waits=[tokc], inc=sem_t)
            tokp = OP("vector", lambda e, th=th, oc=oc: e.tensor_copy(
                out=th[:], in_=psB[:, oc + 8:oc + 9]), waits=[tokb], inc=sem_v)
            tok_thr[widx - 1] = OP("vector", lambda e, th=th, thn=thn: e.tensor_scalar(
                out=thn[:], in0=th[:], scalar1=-1.0, scalar2=None, op0=ALU.mult),
                waits=[tokp], inc=sem_v)

    tern_pend = []

    def _w1_tern_partB(ft, gsl, tokt):
        nonlocal w1t_last_sp
        ssl = ft % 2
        ev_parts = []
        for b4 in range(2):
            psl = pslot["i"] % 4
            pslot["i"] += 1
            c0 = psl * 512
            items = []
            for i4 in range(4):
                dt = b4 * 4 + i4
                items.append((
                    lambda e, gsl=gsl, dt=dt, i4=i4, c0=c0: e.matmul(
                        out=psA[:, c0 + i4 * 128:c0 + (i4 + 1) * 128],
                        lhsT=wtern[gsl][:, dt * 128:(dt + 1) * 128], rhs=id_bf[:],
                        start=True, stop=True),
                    [tokt, wstage_use[ssl], proj_ps_last[psl], tok_zA,
                     (sem_ld, 16)] if i4 == 0 else []))
            tokmm = GRP("tensor", items, inc=sem_t)
            tern_state["tern_read"] = tokmm
            toke = OP("scalar", lambda e, ssl=ssl, b4=b4, c0=c0: e.activation(
                out=wstage[ssl][:, 4 * b4:4 * b4 + 4, :], in_=psA[:, c0:c0 + 512],
                func=AF.Copy), waits=[tokmm], inc=sem_a)
            proj_ps_last[psl] = toke
            ev_parts.append(toke)
        w1t_last_sp = OP("gpsimd", lambda e, ssl=ssl, ft=ft: e.dma_start(
            out=w1t_h[:, ft, :, :], in_=wstage[ssl][:]),
            waits=ev_parts, inc=sp_w1[ssl], inc_n=16)
        w1t_sp_tok[ft] = w1t_last_sp
        wstage_use[ssl] = w1t_last_sp

    def emit_w1_tern_tiles(n):
        nonlocal wtern_idx, wi
        for _ in range(n):
            ft = wtern_idx
            if ft >= 32:
                while tern_pend:
                    _w1_tern_partB(*tern_pend.pop(0))
                return
            wtern_idx += 1
            sl = wi % 3
            wi += 1
            gsl = ft % 3
            tokl = OP("sync", lambda e, sl=sl, ft=ft: e.dma_start(
                out=wtmp[sl][:], in_=w1_h[ft * 128:(ft + 1) * 128, :]),
                waits=[last_w_use[sl]], inc=ld_w[sl], inc_n=16)
            tg2 = OP("vector", lambda e, sl=sl: e.tensor_scalar(
                out=wg2[0][:], in0=wtmp[sl][:], scalar1=th1n[:, 0:1], scalar2=None,
                op0=ALU.is_lt), waits=[tokl, tok_thr[0], w1g_use[0]], inc=sem_v)
            tokt = OP("vector", lambda e, sl=sl, gsl=gsl: e.scalar_tensor_tensor(
                out=wtern[gsl][:], in0=wtmp[sl][:], scalar=th1[:, 0:1],
                in1=wg2[0][:], op0=ALU.is_gt, op1=ALU.subtract),
                waits=[tg2, tern_state.get("tern_read")], inc=sem_v)
            last_w_use[sl] = tokt
            w1g_use[0] = tokt
            tern_pend.append((ft, gsl, tokt))
            if len(tern_pend) > 2:
                _w1_tern_partB(*tern_pend.pop(0))

    xng = [p1(nc.sbuf_tensor(f"xng{i}", [128, JT, SEG], BF16)) for i in range(2)]
    wstr = [p1(nc.sbuf_tensor(f"wstr{i}", [128, 8, 128], BF16)) for i in range(4)]
    hst = [p1(nc.sbuf_tensor(f"hst{i}", [128, SEG], BF16)) for i in range(2)]

    w2_fj = 0
    w2t_last_sp = None

    w2_pend = []

    def _w2_tern_partB(fj, gsl, tokt):
        nonlocal w2t_last_sp
        ssl = fj % 2
        ev_parts = []
        for b4 in range(2):
            psl = pslot["i"] % 4
            pslot["i"] += 1
            c0 = psl * 512
            items = []
            for i4 in range(4):
                dt = b4 * 4 + i4
                items.append((
                    lambda e, gsl=gsl, dt=dt, i4=i4, c0=c0: e.matmul(
                        out=psA[:, c0 + i4 * 128:c0 + (i4 + 1) * 128],
                        lhsT=wtern[gsl][:, dt * 128:(dt + 1) * 128], rhs=id_bf[:],
                        start=True, stop=True),
                    [tokt, tok_thr[0], tok_thr[1], wstage_use[ssl],
                     proj_ps_last[psl], tok_zA] if i4 == 0 else []))
            tokmm = GRP("tensor", items, inc=sem_t)
            tern_state["tern_read"] = tokmm
            toke = OP("scalar", lambda e, ssl=ssl, b4=b4, c0=c0: e.activation(
                out=wstage[ssl][:, 4 * b4:4 * b4 + 4, :], in_=psA[:, c0:c0 + 512],
                func=AF.Copy), waits=[tokmm], inc=sem_a)
            proj_ps_last[psl] = toke
            ev_parts.append(toke)
        w2t_last_sp = OP("gpsimd", lambda e, ssl=ssl, fj=fj: e.dma_start(
            out=w2t_h[:, fj, :], in_=wstage[ssl][:]),
            waits=ev_parts, inc=sp_w2[ssl], inc_n=16)
        wstage_use[ssl] = w2t_last_sp

    def emit_w2_tern(n):
        nonlocal w2_fj, wi
        for _ in range(n):
            fj = w2_fj
            if fj >= 32:
                while w2_pend:
                    _w2_tern_partB(*w2_pend.pop(0))
                return
            w2_fj += 1
            sl = wi % 3
            wi += 1
            tokl = OP("sync", lambda e, sl=sl, fj=fj: e.dma_start(
                out=wtmp[sl][:].rearrange("p (a b) -> p a b", a=8),
                in_=w2_h[:, fj * 128:(fj + 1) * 128].rearrange(
                    "(dt p) c -> p dt c", dt=8)),
                waits=[last_w_use[sl]], inc=ld_w[sl], inc_n=16)
            gsl = fj % 3
            tg2 = OP("vector", lambda e, sl=sl: e.tensor_scalar(
                out=wg2[0][:], in0=wtmp[sl][:], scalar1=th2n[:, 0:1], scalar2=None,
                op0=ALU.is_lt), waits=[tokl, tok_thr[1], w1g_use[0]], inc=sem_v)
            tokt = OP("vector", lambda e, sl=sl, gsl=gsl: e.scalar_tensor_tensor(
                out=wtern[gsl][:], in0=wtmp[sl][:], scalar=th2[:, 0:1],
                in1=wg2[0][:], op0=ALU.is_gt, op1=ALU.subtract),
                waits=[tg2, tern_state.get("tern_read")], inc=sem_v)
            last_w_use[sl] = tokt
            w1g_use[0] = tokt
            w2_pend.append((fj, gsl, tokt))
            if len(w2_pend) > 2:
                _w2_tern_partB(*w2_pend.pop(0))

    f1_step = [0]
    f1_xng_tok = [None]
    f1_gl = [0]
    xng_use = [None, None]
    wstr_use = [None, None, None, None]
    h_ev = [None, None, None, None]
    hst_use = [None, None]
    hsp_tok = [[None] * 32 for _ in range(S)]

    def emit_f1_steps(n):
        for _ in range(n):
            step = f1_step[0]
            if step >= 256:
                return
            f1_step[0] = step + 1
            g, ft = step // 32, step % 32
            if ft == 0:
                gl = g % 2
                f1_gl[0] = gl
                f1_xng_tok[0] = OP("sync", lambda e, g=g, gl=gl: e.dma_start(
                    out=xng[gl][:], in_=xnT_h[g]),
                    waits=[spxn_tok[g], xng_use[gl]], inc=ld_xg[gl], inc_n=16)
            gl = f1_gl[0]
            wsl = step % 4
            tok_w = OP("sync", lambda e, wsl=wsl, ft=ft: e.dma_start(
                out=wstr[wsl][:], in_=w1t_h[:, ft, :, :]),
                waits=[w1t_sp_tok[ft], wstr_use[wsl]], inc=ld_wc[wsl], inc_n=16)
            psl = step % 4
            c0 = psl * 512
            if h_ev[psl] is None:
                h_ev[psl] = (tok_zB, tok_zB, vb1, tok_thr[1])[psl]
            items = []
            for dj in range(JT):
                w0 = []
                if dj == 0:
                    w0 = [tok_w, f1_xng_tok[0], h_ev[psl]]
                items.append((
                    lambda e, wsl=wsl, dj=dj, c0=c0, gl=gl: e.matmul(
                        out=psB[:, c0:c0 + 512], lhsT=wstr[wsl][:, dj, :],
                        rhs=xng[gl][:, dj, :],
                        start=(dj == 0), stop=(dj == JT - 1)), w0))
            tokmm = GRP("tensor", items, inc=sem_t)
            wstr_use[wsl] = tokmm
            hsl = step % 2
            tok_g = OP("scalar", lambda e, ft=ft, c0=c0, hsl=hsl: e.activation(
                out=hst[hsl][:], in_=psB[:, c0:c0 + 512], func=AF.Gelu,
                bias=b1c[:, ft:ft + 1]),
                waits=[tokmm, hst_use[hsl]], inc=sem_a)
            h_ev[psl] = tok_g
            tok_sp = OP("gpsimd", lambda e, g=g, ft=ft, hsl=hsl: e.dma_start(
                out=hT_h[g][:, ft, :], in_=hst[hsl][:]),
                waits=[tok_g], inc=sp_h[hsl], inc_n=16)
            hsp_tok[g][ft] = tok_sp
            hst_use[hsl] = tok_sp
            if ft == 31:
                xng_use[g % 2] = (sem_t, sem_t.count)

    def wprep_chunk(s, sixth):
        if s == 0:
            emit_scale_tiles(1)
            if sixth == 5:
                finalize_scales(1)
        elif s == 1:
            emit_w1_tern_tiles(3 if sixth < 4 else 2)
            emit_scale_tiles(4 if sixth < 2 else 3)
        elif s == 2:
            emit_w1_tern_tiles(3 if sixth < 4 else 2)
            emit_scale_tiles(2)
            if sixth == 5:
                finalize_scales(2)
        else:
            if s == 3 and sixth == 0:
                emit_w1_tern_tiles(1)
            emit_w2_tern(2 if sixth == 0 else 1)
            emit_f1_steps(9 if sixth < 3 else 8)
            if s == 7 and sixth == 5:
                emit_w2_tern(32)
                emit_f1_steps(256)

    xnT_done = [None] * S

    def ln_stage(s):
        xsl = s % 2
        sp = s % 2
        xh_tok = []
        for hs in range(2):
            sl = hs
            r0 = s * SEG + hs * 256
            tokx = OP("sync", lambda e, sl=sl, r0=r0: e.dma_start(
                out=x_sb[sl][:],
                in_=x_h[r0:r0 + 256, :].rearrange("(n p) c -> p n c", n=2)),
                waits=[x_use[sl]], inc=ld_x[sl], inc_n=16)
            xh_tok.append(tokx)
        tok_sq = None
        tok_rd = None
        for tt in range(4):
            hs, half = tt // 2, tt % 2
            xt = x_sb[hs][:, half, :]
            tok_sq = OP("scalar", lambda e, xt=xt, sp=sp, tt=tt: e.activation(
                out=dump[:], in_=xt, func=AF.Square,
                accum_out=sg[sp][:, tt, 1:2]),
                waits=[xh_tok[hs], stats_use[sp], tok_biases], inc=sem_a)
            tok_rd = OP("vector", lambda e, xt=xt, sp=sp, tt=tt: e.reduce_sum(
                out=sg[sp][:, tt, 0:1], in_=xt, axis=AX.X),
                waits=[xh_tok[hs], stats_use[sp]], inc=sem_v)
        m4 = OP("vector", lambda e, sp=sp: e.tensor_scalar(
            out=nw[sp][:, 0:4], in0=sg[sp][:, :, 0], scalar1=1.0 / D,
            scalar2=None, op0=ALU.mult), waits=[tok_rd, tok_sq], inc=sem_v)
        e4 = OP("vector", lambda e, sp=sp: e.tensor_scalar(
            out=nw[sp][:, 4:8], in0=sg[sp][:, :, 1], scalar1=1.0 / D,
            scalar2=None, op0=ALU.mult), inc=sem_v)
        mm4 = OP("vector", lambda e, sp=sp: e.tensor_tensor(
            out=nw[sp][:, 8:12], in0=nw[sp][:, 0:4], in1=nw[sp][:, 0:4],
            op=ALU.mult), waits=[m4], inc=sem_v)
        v4 = OP("vector", lambda e, sp=sp: e.tensor_tensor(
            out=sg[sp][:, :, 2], in0=nw[sp][:, 4:8], in1=nw[sp][:, 8:12],
            op=ALU.subtract), waits=[e4, mm4], inc=sem_v)
        vh4 = OP("vector", lambda e, sp=sp: e.tensor_scalar(
            out=nw[sp][:, 16:20], in0=sg[sp][:, :, 2], scalar1=0.5,
            scalar2=0.5 * EPS, op0=ALU.mult, op1=ALU.add), waits=[v4], inc=sem_v)
        td4 = OP("vector", lambda e, sp=sp: e.tensor_scalar(
            out=nw[sp][:, 20:24], in0=sg[sp][:, :, 2], scalar1=0.5,
            scalar2=0.5 + 0.5 * EPS, op0=ALU.mult, op1=ALU.add), inc=sem_v)
        tok_y = OP("vector", lambda e, sp=sp: e.reciprocal(
            out=nw[sp][:, 24:28], in_=nw[sp][:, 20:24]), waits=[td4], inc=sem_v)
        yc = [24, 28]
        for it in range(4):
            ca, cb = yc[it % 2], yc[(it + 1) % 2]
            t1 = OP("vector", lambda e, sp=sp, ca=ca: e.tensor_tensor(
                out=nw[sp][:, 8:12], in0=nw[sp][:, ca:ca + 4],
                in1=nw[sp][:, ca:ca + 4], op=ALU.mult), waits=[tok_y], inc=sem_v)
            t2 = OP("vector", lambda e, sp=sp: e.tensor_tensor(
                out=nw[sp][:, 12:16], in0=nw[sp][:, 8:12], in1=nw[sp][:, 16:20],
                op=ALU.mult), waits=[t1, vh4], inc=sem_v)
            t3 = OP("vector", lambda e, sp=sp: e.tensor_scalar(
                out=nw[sp][:, 8:12], in0=nw[sp][:, 12:16], scalar1=-1.0,
                scalar2=1.5, op0=ALU.mult, op1=ALU.add), waits=[t2], inc=sem_v)
            tok_y = OP("vector", lambda e, sp=sp, ca=ca, cb=cb: e.tensor_tensor(
                out=nw[sp][:, cb:cb + 4], in0=nw[sp][:, ca:ca + 4],
                in1=nw[sp][:, 8:12], op=ALU.mult), waits=[t3], inc=sem_v)
        yfin = yc[0]
        ng4 = OP("vector", lambda e, sp=sp, yfin=yfin: e.tensor_tensor(
            out=nw[sp][:, 12:16], in0=nw[sp][:, 0:4], in1=nw[sp][:, yfin:yfin + 4],
            op=ALU.mult), waits=[tok_y], inc=sem_v)
        tok_nm = OP("vector", lambda e, sp=sp: e.tensor_scalar(
            out=nw[sp][:, 8:12], in0=nw[sp][:, 12:16], scalar1=-1.0,
            scalar2=None, op0=ALU.mult), waits=[ng4], inc=sem_v)
        for tt in range(4):
            hs, half = tt // 2, tt % 2
            xt = x_sb[hs][:, half, :]
            xsl2 = tt
            tok_xn = OP("scalar", lambda e, xt=xt, sp=sp, tt=tt, xsl2=xsl2, yfin=yfin: e.activation(
                out=xn_tm[xsl2][:], in_=xt, func=AF.Identity,
                scale=nw[sp][:, yfin + tt:yfin + tt + 1],
                bias=nw[sp][:, 8 + tt:9 + tt]),
                waits=[tok_nm, xn_use[xsl2]], inc=sem_a)
            ln_xn_tok[s][tt] = tok_xn
            stats_use[sp] = tok_xn

    def ln_transpose_tiles(s, tts):
        xsl = s % 2
        for tt in tts:
            xsl2 = tt
            tok_xn = ln_xn_tok[s][tt]
            for g4 in range(2):
                psl = pslot["i"] % 4
                pslot["i"] += 1
                c0 = psl * 512
                items = []
                for j4 in range(4):
                    dj = g4 * 4 + j4
                    items.append((
                        lambda e, dj=dj, j4=j4, xsl2=xsl2, c0=c0: e.matmul(
                            out=psA[:, c0 + j4 * 128:c0 + (j4 + 1) * 128],
                            lhsT=xn_tm[xsl2][:, dj * 128:(dj + 1) * 128], rhs=id_bf[:],
                            start=True, stop=True),
                        [tok_xn, xnT_use[xsl], tok_zA, proj_ps_last[psl],
                         spxn_tok[s - 2] if s >= 2 else None] if j4 == 0 else []))
                tokmm = GRP("tensor", items, inc=sem_t)
                lnps["prev_ev"] = OP("scalar",
                                     lambda e, xsl=xsl, g4=g4, tt=tt, c0=c0: e.activation(
                                         out=xnT_st[xsl][:, g4 * 4:(g4 + 1) * 4,
                                                         tt * 128:(tt + 1) * 128],
                                         in_=psA[:, c0:c0 + 512], func=AF.Copy),
                                     waits=[tokmm], inc=sem_a)
                proj_ps_last[psl] = lnps["prev_ev"]
            xn_use[tt] = (sem_t, sem_t.count)
            if tt == 1:
                x_use[0] = lnps["prev_ev"]
            if tt == 3:
                x_use[1] = lnps["prev_ev"]
        if tts and tts[-1] == 3:
            tok_xnT = lnps["prev_ev"]
            xnT_done[s] = tok_xnT
            spxn_tok[s] = OP("gpsimd", lambda e, xsl=xsl, s=s: e.dma_start(
                out=xnT_h[s], in_=xnT_st[xsl][:]), waits=[tok_xnT],
                inc=sp_xn[xsl], inc_n=16)

    def proj_stage(s):
        xsl = s % 2
        tok_xnT = xnT_done[s]
        for which in range(2):
            wsb = (wq_sb, wk_sb)[which]
            bcol = (bqc, bkc)[which]
            sp_sem = (sp_q, sp_k)[which]
            sp_arr = (spq_tok, spk_tok)[which]
            dst_h = (qT_h, kT_h)[which]
            qsl = which
            ev_toks = []
            for jt in range(JT):
                psl = pslot["i"] % 4
                pslot["i"] += 1
                c0 = psl * 512
                items = []
                for dj in range(JT):
                    w0 = [tok_xnT, tok_wqkv, qk_use[qsl], tok_zA,
                          proj_ps_last[psl]] if dj == 0 else []
                    items.append((
                        lambda e, wsb=wsb, dj=dj, jt=jt, c0=c0, xsl=xsl: e.matmul(
                            out=psA[:, c0:c0 + 512],
                            lhsT=wsb[:, dj, jt * 128:(jt + 1) * 128],
                            rhs=xnT_st[xsl][:, dj, :],
                            start=(dj == 0), stop=(dj == JT - 1)), w0))
                tokmm = GRP("tensor", items, inc=sem_t)
                tok_ev = OP("scalar",
                            lambda e, qsl=qsl, jt=jt, c0=c0, bcol=bcol: e.activation(
                                out=qk_st[qsl][:, jt, :], in_=psA[:, c0:c0 + 512],
                                func=AF.Identity, bias=bcol[:, jt:jt + 1]),
                            waits=[tokmm], inc=sem_a)
                proj_ps_last[psl] = tok_ev
                ev_toks.append(tok_ev)
                if jt == 3:
                    wprep_chunk(s, which * 2)
            extra = None
            if which == 1:
                kd_tok[s] = OP("scalar", lambda e, qsl=qsl, s=s: e.activation(
                    out=kdT[:, s, :, :], in_=qk_st[qsl][:, :, 0:512:4], func=AF.Copy),
                    waits=[ev_toks[-1]], inc=sem_a)
                extra = kd_tok[s]
            sp_arr[s] = OP("gpsimd", lambda e, qsl=qsl, s=s, dst_h=dst_h: e.dma_start(
                out=dst_h[s], in_=qk_st[qsl][:]),
                waits=[ev_toks[-1], extra], inc=sp_sem[qsl], inc_n=16)
            qk_use[qsl] = sp_arr[s]
            if s + 1 < S:
                ln_transpose_tiles(s + 1, [2 * which, 2 * which + 1])
            wprep_chunk(s, which * 2 + 1)

        vsl = 0
        ev_last = None
        for tt in range(4):
            for halfj in range(2):
                psl = pslot["i"] % 4
                pslot["i"] += 1
                c0 = psl * 512
                items = []
                for dj in range(JT):
                    w0 = [tok_xnT, tok_wqkv, v_use[vsl], vb1, tok_zA,
                          proj_ps_last[psl]] if dj == 0 else []
                    items.append((
                        lambda e, dj=dj, tt=tt, halfj=halfj, c0=c0, xsl=xsl: e.matmul(
                            out=psA[:, c0:c0 + 512],
                            lhsT=xnT_st[xsl][:, dj, tt * 128:(tt + 1) * 128],
                            rhs=wv_sb[:, dj, halfj * 512:(halfj + 1) * 512],
                            start=(dj == 0), stop=(dj == JT - 1)), w0))
                tokmm = GRP("tensor", items, inc=sem_t)
                ev_last = OP("vector",
                             lambda e, vsl=vsl, tt=tt, halfj=halfj, c0=c0: e.tensor_tensor(
                                 out=v_st[vsl][:, tt, :].rearrange(
                                     "p (h c) -> p h c", c=65)[:, halfj * 8:(halfj + 1) * 8, 0:64],
                                 in0=psA[:, c0:c0 + 512].rearrange("p (h c) -> p h c", c=64),
                                 in1=bvb[:, halfj * 512:(halfj + 1) * 512].rearrange(
                                     "p (h c) -> p h c", c=64),
                                 op=ALU.add), waits=[tokmm], inc=sem_v)
                proj_ps_last[psl] = ev_last
                if tt == 1 and halfj == 1:
                    wprep_chunk(s, 4)
        tok_ones = OP("vector", lambda e, vsl=vsl: e.memset(
            v_st[vsl][:].rearrange("p t (h c) -> p t h c", c=65)[:, :, :, 64:65], 1.0),
            waits=[ev_last], inc=sem_v)
        spv_tok[s] = OP("gpsimd", lambda e, vsl=vsl, s=s: e.dma_start(
            out=v_h[s], in_=v_st[vsl][:]), waits=[tok_ones], inc=sp_v[vsl], inc_n=16)
        v_use[vsl] = spv_tok[s]
        gsl = s % 2
        tok_vd0 = OP("sync", lambda e, s=s: e.dma_start(
            out=vd[:, s, :], in_=v_h[s][0:128:4].rearrange("p t c -> t p c")),
            waits=[spv_tok[s], vd_tok[s - 2] if s >= 2 else None],
            inc=ld_vd[gsl], inc_n=16)
        vd_tok[s] = OP("vector", lambda e, s=s: e.memset(
            vd[:, s, :].rearrange("p (h c) -> p h c", c=65)[:, :, 64:65], 4.0),
            waits=[tok_vd0], inc=sem_v)
        if s + 1 < S:
            ln_transpose_tiles(s + 1, [])
        wprep_chunk(s, 5)

        xnT_use[xsl] = (sem_t, sem_t.count)


    ln_stage(0)
    ln_transpose_tiles(0, [0, 1, 2, 3])
    emit_scale_tiles(26)
    for s in range(S):
        if s + 1 < S:
            ln_stage(s + 1)
        proj_stage(s)

    p1_end = [(sem_a, sem_a.count), (sem_v, sem_v.count)]
    p1_all = p1_end + [(sem_t, sem_t.count)]
    for _sp in (sp_q, sp_k, sp_v, sp_xn, sp_w1, sp_w2, sp_h):
        p1_all += [(_sp[0], _sp[0].count), (_sp[1], _sp[1].count)]
    es_p1.close()

    es_a = ExitStack()
    pa = es_a.enter_context
    wo_sb = pa(nc.sbuf_tensor("wo_sb", [128, JT, D], BF16))
    wtmp_a = [pa(nc.sbuf_tensor(f"wtmpa{i}", [128, 1024], F32)) for i in range(2)]
    qT_a = [pa(nc.sbuf_tensor(f"qTa{i}", [128, JT, SEG], BF16)) for i in range(2)]
    kT_a = [pa(nc.sbuf_tensor(f"kTa{i}", [128, JT, SEG], BF16)) for i in range(2)]
    vp_a = [pa(nc.sbuf_tensor(f"vpa{i}", [128, 4, 16 * 65], BF16)) for i in range(2)]
    E_sb = [pa(nc.sbuf_tensor(f"E{i}", [128, 2048], BF16)) for i in range(2)]
    ao = [pa(nc.sbuf_tensor(f"ao{i}", [128, 4, D], BF16)) for i in range(2)]
    aoT = [pa(nc.sbuf_tensor(f"aoT{i}", [128, JT, SEG], BF16)) for i in range(2)]
    bob = pa(nc.sbuf_tensor("bob", [128, D], F32))
    r_sb = [pa(nc.sbuf_tensor(f"r{i}", [128, 2, 5], F32)) for i in range(2)]
    tmp_sb = [pa(nc.sbuf_tensor(f"tmp{i}", [128, 640], F32)) for i in range(2)]
    s_st = [pa(nc.sbuf_tensor(f"sst{i}", [128, D], F32)) for i in range(2)]

    last_wa_use = [None, None]
    tok_wo = None
    for dj in range(JT):
        sl = dj % 2
        tokl = OP("sync", lambda e, sl=sl, dj=dj: e.dma_start(
            out=wtmp_a[sl][:], in_=wo_h[dj * 128:(dj + 1) * 128, :]),
            waits=[last_wa_use[sl], last_w_use[sl]] + p1_all,
            inc=ld_w[sl], inc_n=16)
        tok_wo = OP("vector", lambda e, sl=sl, dj=dj: e.tensor_copy(
            out=wo_sb[:, dj, :], in_=wtmp_a[sl][:]), waits=[tokl], inc=sem_v)
        last_wa_use[sl] = tok_wo

    tok_bo = OP("sync", lambda e: e.dma_start(
        out=brow1[:], in_=bo_h.rearrange("(a c) -> a c", a=1)),
        waits=p1_all, inc=sem_ld, inc_n=16)
    bo2 = GRP("tensor", [
        (lambda e: e.matmul(out=psB[:, 1024:1536], lhsT=ones_r[0:1, :],
                            rhs=brow1[0:1, 0:512], start=True, stop=True),
         [tok_bo] + p1_all),
        (lambda e: e.matmul(out=psB[:, 1536:2048], lhsT=ones_r[0:1, :],
                            rhs=brow1[0:1, 512:1024], start=True, stop=True), []),
    ], inc=sem_t)
    vb2 = OP("vector", lambda e: e.tensor_copy(out=bob[:], in_=psB[:, 1024:2048]),
             waits=[bo2], inc=sem_v)

    qa_tok = [None] * S
    ka_tok = [None] * S
    va_tok = [None] * S
    qa_use = [None, None]
    ka_use = [None, None]
    va_use = [None, None]

    def prefetch_seg(s):
        sl = s % 2
        qa_tok[s] = OP("sync", lambda e, s=s, sl=sl: e.dma_start(
            out=qT_a[sl][:], in_=qT_h[s]),
            waits=[spq_tok[s], qa_use[sl]] + p1_all, inc=ld_qa[sl], inc_n=16)
        ka_tok[s] = OP("sync", lambda e, s=s, sl=sl: e.dma_start(
            out=kT_a[sl][:], in_=kT_h[s]),
            waits=[spk_tok[s], ka_use[sl]] + p1_all, inc=ld_ka[sl], inc_n=16)
        va_tok[s] = OP("sync", lambda e, s=s, sl=sl: e.dma_start(
            out=vp_a[sl][:], in_=v_h[s]),
            waits=[spv_tok[s], va_use[sl]] + p1_all, inc=ld_va[sl], inc_n=16)

    prefetch_seg(0)
    prefetch_seg(1)

    slot_done = {}
    loc_free = {}
    cross_free = {}
    exp_a_tok = {}
    exp_b_tok = {}
    av_tok = {}
    ao_read_tok = [None, None]
    aoT_read_tok = [None, None]
    sst_use = [None, None]
    red_last = None
    u_global = 0
    g_slot = 0
    pend = None

    for s in range(S):
        sl = s % 2
        aslot = s % 2
        valid = [o for o in OFFS if 0 <= s + o < S]
        nv = len(valid)

        def stage1(hp, qt, u, g, s=s, sl=sl, valid=valid, nv=nv):
            ps = psA if (g % 2 == 0) else psB
            q0 = qt * 128
            w0 = [loc_free.get(g - 2), qa_tok[s], ka_tok[s], tok_wo, tok_zA, tok_zB]
            if g < 2:
                w0 += p1_end + [vb2]
            items = []
            for kc in range(4):
                for hx in range(2):
                    pb = 64 * hx
                    items.append((
                        lambda e, ps=ps, kc=kc, hx=hx, pb=pb, sl=sl, hp=hp, q0=q0: e.matmul(
                            out=ps[:, hx * 1024 + kc * 128:hx * 1024 + (kc + 1) * 128],
                            lhsT=kT_a[sl][pb:pb + 64, hp, kc * 128:(kc + 1) * 128],
                            rhs=qT_a[sl][pb:pb + 64, hp, q0:q0 + 128],
                            start=True, stop=True),
                        w0 if (kc == 0 and hx == 0) else []))
            sc_loc = GRP("tensor", items, inc=sem_t)
            items = []
            for j, o in enumerate(valid):
                so = s + o
                for hx in range(2):
                    pb = 64 * hx
                    items.append((
                        lambda e, ps=ps, j=j, so=so, hx=hx, pb=pb, sl=sl, hp=hp, q0=q0: e.matmul(
                            out=ps[:, hx * 1024 + 512 + j * 128:hx * 1024 + 512 + (j + 1) * 128],
                            lhsT=kdT[pb:pb + 64, so, hp, :],
                            rhs=qT_a[sl][pb:pb + 64, hp, q0:q0 + 128],
                            start=True, stop=True),
                        [kd_tok[so], cross_free.get(g - 2)] if (j == 0 and hx == 0)
                        else [kd_tok[so]]))
            sc_cr = GRP("tensor", items, inc=sem_t)
            exp_a_tok[u] = OP("scalar", lambda e, ps=ps, u=u: e.activation(
                out=E_sb[u % 2][:].rearrange("p (h c) -> p h c", h=2)[:, :, 0:512],
                in_=ps.rearrange("p (h c) -> p h c", h=2)[:, :, 0:512],
                func=AF.Exp, scale=0.125),
                waits=[sc_loc, av_tok.get(u - 2)], inc=sem_a)
            loc_free[g] = exp_a_tok[u]
            exp_b_tok[u] = OP("scalar", lambda e, ps=ps, u=u, nv=nv: e.activation(
                out=E_sb[u % 2][:].rearrange("p (h c) -> p h c", h=2)[:, :, 512:512 + 128 * nv],
                in_=ps.rearrange("p (h c) -> p h c", h=2)[:, :, 512:512 + 128 * nv],
                func=AF.Exp, scale=0.125),
                waits=[sc_cr], inc=sem_a)

        def stage2(hp, qt, u, g, s=s, sl=sl, aslot=aslot, valid=valid, nv=nv):
            nonlocal red_last
            ps = psA if (g % 2 == 0) else psB
            items = []
            for hx in range(2):
                vcol = 65 * (2 * hp + hx)
                base = hx * 1024 + 512
                for kc in range(4):
                    items.append((
                        lambda e, ps=ps, u=u, hx=hx, base=base, kc=kc, sl=sl, vcol=vcol: e.matmul(
                            out=ps[:, base:base + 65],
                            lhsT=E_sb[u % 2][:, hx * 1024 + kc * 128:hx * 1024 + (kc + 1) * 128],
                            rhs=vp_a[sl][:, kc, vcol:vcol + 65],
                            start=(kc == 0), stop=(kc == 3)),
                        [exp_a_tok[u], exp_b_tok[u], va_tok[s]]
                        if (hx == 0 and kc == 0) else []))
                for j, o in enumerate(valid):
                    so = s + o
                    blk = base + 72 * (1 + j)
                    items.append((
                        lambda e, ps=ps, u=u, hx=hx, j=j, so=so, blk=blk, vcol=vcol: e.matmul(
                            out=ps[:, blk:blk + 65],
                            lhsT=E_sb[u % 2][:, hx * 1024 + 512 + j * 128:hx * 1024 + 512 + (j + 1) * 128],
                            rhs=vd[:, so, vcol:vcol + 65],
                            start=True, stop=True),
                        [vd_tok[so]]))
            av_tok[u] = GRP("tensor", items, inc=sem_t)
            rsl = u % 2
            nc1 = 1 + nv
            tok_rc = OP("vector", lambda e, ps=ps, rsl=rsl, nc1=nc1: e.reciprocal(
                out=r_sb[rsl][:, :, 0:nc1],
                in_=ps.rearrange("p (h c) -> p h c", h=2)[:, :, 576:576 + 72 * (nc1 - 1) + 1:72]),
                waits=[av_tok[u]], inc=sem_v)
            tok_mul = OP("vector", lambda e, ps=ps, rsl=rsl, nc1=nc1: e.tensor_tensor(
                out=tmp_sb[rsl][:].rearrange("p (x c j) -> p x c j", x=2, j=64)[:, :, 0:nc1, :],
                in0=ps.rearrange("p (h q) -> p h q", h=2)[:, :, 512:512 + 72 * nc1]
                      .rearrange("p h (c j) -> p h c j", j=72)[:, :, :, 0:64],
                in1=r_sb[rsl][:, :, 0:nc1].broadcast_to([128, 2, nc1, 64]),
                op=ALU.mult), waits=[tok_rc], inc=sem_v)
            slot_done[g] = tok_mul
            cross_free[g] = tok_mul

            def _red(e, rsl=rsl, nc1=nc1, aslot=aslot, qt=qt, hp=hp):
                with nc.allow_low_precision(reason="bf16 attnout operand"):
                    return e.tensor_reduce(
                        out=ao[aslot][:, qt, 128 * hp:128 * hp + 128].rearrange(
                            "p (h j) -> p h j", h=2),
                        in_=tmp_sb[rsl][:].rearrange(
                            "p (x c j) -> p x j c", x=2, j=64)[:, :, :, 0:nc1],
                        axis=AX.X, op=ALU.add)
            red_last = OP("vector", _red, waits=[ao_read_tok[aslot], tok_mul],
                          inc=sem_v)

        for hp in range(8):
            for qt in range(4):
                u = u_global
                u_global += 1
                g = g_slot
                g_slot += 1
                stage1(hp, qt, u, g)
                if pend is not None:
                    pend()
                pend = (lambda hp=hp, qt=qt, u=u, g=g: stage2(hp, qt, u, g))
        pend()
        pend = None

        pt_evac = None
        pt_last_mm = None
        for half4 in range(2):
            g = g_slot
            g_slot += 1
            ps = psA if (g % 2 == 0) else psB
            mm_toks = []
            first = True
            for jt_ in range(4 * half4, 4 * half4 + 4):
                items = []
                for tt in range(4):
                    w0 = []
                    if first:
                        w0 = [loc_free.get(g - 2), cross_free.get(g - 2),
                              slot_done.get(g - 2), red_last, aoT_read_tok[aslot]]
                        first = False
                    items.append((
                        lambda e, ps=ps, aslot=aslot, tt=tt, jt_=jt_: e.matmul(
                            out=ps[:, (jt_ % 4) * 512 + tt * 128:(jt_ % 4) * 512 + (tt + 1) * 128],
                            lhsT=ao[aslot][:, tt, jt_ * 128:(jt_ + 1) * 128],
                            rhs=id_bf[:], start=True, stop=True), w0))
                mm_toks.append(GRP("tensor", items, inc=sem_t))
            ev = None
            for i, jt_ in enumerate(range(4 * half4, 4 * half4 + 4)):
                ev = OP("scalar", lambda e, ps=ps, aslot=aslot, jt_=jt_: e.activation(
                    out=aoT[aslot][:, jt_, :],
                    in_=ps[:, (jt_ % 4) * 512:(jt_ % 4) * 512 + 512],
                    func=AF.Identity), waits=[mm_toks[i]], inc=sem_a)
            slot_done[g] = ev
            loc_free[g] = ev
            cross_free[g] = ev
            pt_evac = ev
            pt_last_mm = mm_toks[-1]
        ao_read_tok[aslot] = pt_last_mm

        o_last_mm = None
        for tg in range(2):
            g = g_slot
            g_slot += 1
            ps = psA if (g % 2 == 0) else psB
            dve_ev = None
            for ti, tt in enumerate((2 * tg, 2 * tg + 1)):
                c0 = ti * 1024
                items = []
                for halfj in range(2):
                    for jt_ in range(JT):
                        w0 = []
                        if halfj == 0 and jt_ == 0 and ti == 0:
                            w0 = [loc_free.get(g - 2), cross_free.get(g - 2),
                                  slot_done.get(g - 2), pt_evac]
                        items.append((
                            lambda e, ps=ps, aslot=aslot, tt=tt, jt_=jt_, halfj=halfj, c0=c0: e.matmul(
                                out=ps[:, c0 + halfj * 512:c0 + (halfj + 1) * 512],
                                lhsT=aoT[aslot][:, jt_, tt * 128:(tt + 1) * 128],
                                rhs=wo_sb[:, jt_, halfj * 512:(halfj + 1) * 512],
                                start=(jt_ == 0), stop=(jt_ == JT - 1)), w0))
                o_last_mm = GRP("tensor", items, inc=sem_t)
                ssl = tt % 2
                tok_oe = OP("vector", lambda e, ps=ps, c0=c0, ssl=ssl: e.tensor_tensor(
                    out=s_st[ssl][:], in0=ps[:, c0:c0 + 1024], in1=bob[:], op=ALU.add),
                    waits=[o_last_mm, vb2, sst_use[ssl]], inc=sem_v)
                dve_ev = tok_oe
                r0 = s * SEG + tt * 128
                sst_use[ssl] = OP("gpsimd", lambda e, ssl=ssl, r0=r0: e.dma_start(
                    out=s_h[r0:r0 + 128, :], in_=s_st[ssl][:]),
                    waits=[tok_oe], inc=sp_s[ssl], inc_n=16)
            slot_done[g] = dve_ev
            loc_free[g] = dve_ev
            cross_free[g] = dve_ev
        aoT_read_tok[aslot] = o_last_mm
        qa_use[sl] = ka_use[sl] = va_use[sl] = (sem_t, sem_t.count)
        if s + 2 < S:
            prefetch_seg(s + 2)

    sp_s_final = [(sp_s[0], sp_s[0].count), (sp_s[1], sp_s[1].count)]
    a_end = [(sem_a, sem_a.count), (sem_v, sem_v.count)]
    a_all = a_end + [(sem_t, sem_t.count)]
    es_a.close()
    es_pa.close()

    es_f2 = ExitStack()
    f2 = es_f2.enter_context
    w2t_sb = f2(nc.sbuf_tensor("w2t_sb", [128, 32, 1024], BF16))
    hT_g = [f2(nc.sbuf_tensor(f"hTg{i}", [128, 32, SEG], BF16)) for i in range(2)]
    s_f = [f2(nc.sbuf_tensor(f"s_f{i}", [128, D], F32)) for i in range(2)]
    x_f = [f2(nc.sbuf_tensor(f"x_f{i}", [128, D], F32)) for i in range(2)]
    gp = [f2(nc.sbuf_tensor(f"gp{i}", [128, D], F32)) for i in range(2)]
    ga = [f2(nc.sbuf_tensor(f"ga{i}", [128, D], F32)) for i in range(2)]
    ot = [f2(nc.sbuf_tensor(f"ot{i}", [128, D], F32)) for i in range(2)]

    b2b = f2(nc.sbuf_tensor("b2b", [128, D], F32))
    tok_b2r = OP("sync", lambda e: e.dma_start(
        out=brow1[:], in_=b2_h.rearrange("(a c) -> a c", a=1)),
        waits=a_all, inc=sem_ld, inc_n=16)
    b22 = GRP("tensor", [
        (lambda e: e.matmul(out=psB[:, 1024:1536], lhsT=ones_r[0:1, :],
                            rhs=brow1[0:1, 0:512], start=True, stop=True),
         [tok_b2r] + a_all),
        (lambda e: e.matmul(out=psB[:, 1536:2048], lhsT=ones_r[0:1, :],
                            rhs=brow1[0:1, 512:1024], start=True, stop=True), []),
    ], inc=sem_t)
    vb3 = OP("vector", lambda e: e.tensor_copy(out=b2b[:], in_=psB[:, 1024:2048]),
             waits=[b22], inc=sem_v)

    tok_w2t_c = []
    for ch in range(4):
        tok_w2t_c.append(OP("sync", lambda e, ch=ch: e.dma_start(
            out=w2t_sb[:, ch * 8:(ch + 1) * 8, :], in_=w2t_h[:, ch * 8:(ch + 1) * 8, :]),
            waits=([w2t_last_sp,
                    (sp_w2[0], sp_w2[0].count), (sp_w2[1], sp_w2[1].count)]
                   + a_all) if ch == 0 else [],
            inc=ld_big, inc_n=16))
    tok_w2t = tok_w2t_c[3]

    hg_use = [None, None]
    sf_use = [None, None]
    xf_use = [None, None]
    gp_use = [None, None]
    ga_use = [None, None]
    ot_use = [None, None]
    o_ev = [None, None]

    for g in range(S):
        gl = g % 2
        tok_hg = OP("sync", lambda e, g=g, gl=gl: e.dma_start(
            out=hT_g[gl][:], in_=hT_h[g]),
            waits=[hsp_tok[g][30], hsp_tok[g][31], hg_use[gl]] + a_all,
            inc=ld_hg[gl], inc_n=16)
        for tt in range(4):
            ssl = tt % 2
            r0 = g * SEG + tt * 128
            tok_sf = OP("sync", lambda e, ssl=ssl, r0=r0: e.dma_start(
                out=s_f[ssl][:], in_=s_h[r0:r0 + 128, :]),
                waits=[sf_use[ssl]] + sp_s_final + a_all,
                inc=ld_sf[ssl], inc_n=16)
            tok_xf = OP("sync", lambda e, ssl=ssl, r0=r0: e.dma_start(
                out=x_f[ssl][:], in_=x_h[r0:r0 + 128, :]),
                waits=[xf_use[ssl]] + a_all, inc=ld_xf[ssl], inc_n=16)
            c0 = (tt % 2) * 1024
            items = []
            for fj in range(32):
                for halfj in range(2):
                    w0 = [tok_w2t_c[0], tok_hg, o_ev[tt % 2]] if (fj == 0 and halfj == 0) else []
                    if fj % 8 == 0 and halfj == 0:
                        w0 = w0 + [tok_w2t_c[fj // 8]]
                    if fj == 0 and halfj == 0 and g == 0 and tt < 2:
                        w0 = w0 + a_end
                    items.append((
                        lambda e, fj=fj, halfj=halfj, c0=c0, gl=gl, tt=tt: e.matmul(
                            out=psA[:, c0 + halfj * 512:c0 + (halfj + 1) * 512],
                            lhsT=hT_g[gl][:, fj, tt * 128:(tt + 1) * 128],
                            rhs=w2t_sb[:, fj, halfj * 512:(halfj + 1) * 512],
                            start=(fj == 0), stop=(fj == 31)), w0))
            tok_f2 = GRP("tensor", items, inc=sem_t)
            tok_gp = OP("vector", lambda e, c0=c0, ssl=ssl: e.tensor_tensor(
                out=gp[ssl][:], in0=psA[:, c0:c0 + 1024], in1=b2b[:], op=ALU.add),
                waits=[tok_f2, vb3, ga_use[ssl]], inc=sem_v)
            o_ev[tt % 2] = tok_gp
            tok_ga = OP("scalar", lambda e, ssl=ssl: e.activation(
                out=ga[ssl][:], in_=gp[ssl][:], func=AF.Silu), waits=[tok_gp],
                inc=sem_a)
            tok_m = OP("vector", lambda e, ssl=ssl: e.tensor_tensor(
                out=gp[ssl][:], in0=ga[ssl][:], in1=s_f[ssl][:], op=ALU.mult),
                waits=[tok_ga, tok_sf, ot_use[ssl]], inc=sem_v)
            tok_o = OP("vector", lambda e, ssl=ssl: e.tensor_tensor(
                out=ot[ssl][:], in0=gp[ssl][:], in1=x_f[ssl][:], op=ALU.add),
                waits=[tok_xf, tok_m], inc=sem_v)
            ga_use[ssl] = tok_o
            sf_use[ssl] = tok_o
            xf_use[ssl] = tok_o
            tok_st = OP("gpsimd", lambda e, ssl=ssl, r0=r0: e.dma_start(
                out=out_h[r0:r0 + 128, :], in_=ot[ssl][:]),
                waits=[tok_o], inc=sem_out, inc_n=16)
            ot_use[ssl] = tok_st
        hg_use[gl] = (sem_t, sem_t.count)

    OP("gpsimd", lambda e: e.engine_nop(), waits=[(sem_out, sem_out.count)])

    es_f2.close()
    S_.emit()
    return nc


def _build():
    nc = bass.Bass("TRN2", target_bir_lowering=False,
                   detect_race_conditions=False)
    ctx = ExitStack()
    with ctx:
        build_program(nc, ctx)
    return nc


def kernel(L0_attn_x, L0_attn_Wq, L0_attn_bq, L0_attn_Wk, L0_attn_bk,
           L0_attn_Wv, L0_attn_bv, L0_attn_Wo, L0_attn_bo,
           L0_ffn_W1, L0_ffn_b1, L0_ffn_W2, L0_ffn_b2):
    x = np.ascontiguousarray(np.asarray(L0_attn_x, dtype=np.float32))
    base = {
        "wq": np.ascontiguousarray(np.asarray(L0_attn_Wq, np.float32)),
        "wk": np.ascontiguousarray(np.asarray(L0_attn_Wk, np.float32)),
        "wv": np.ascontiguousarray(np.asarray(L0_attn_Wv, np.float32)),
        "wo": np.ascontiguousarray(np.asarray(L0_attn_Wo, np.float32)),
        "bq": np.ascontiguousarray(np.asarray(L0_attn_bq, np.float32)),
        "bk": np.ascontiguousarray(np.asarray(L0_attn_bk, np.float32)),
        "bv": np.ascontiguousarray(np.asarray(L0_attn_bv, np.float32)),
        "bo": np.ascontiguousarray(np.asarray(L0_attn_bo, np.float32)),
        "w1": np.ascontiguousarray(np.asarray(L0_ffn_W1, np.float32)),
        "b1": np.ascontiguousarray(np.asarray(L0_ffn_b1, np.float32)),
        "w2": np.ascontiguousarray(np.asarray(L0_ffn_W2, np.float32)),
        "b2": np.ascontiguousarray(np.asarray(L0_ffn_b2, np.float32)),
    }
    nc = _build()
    in_maps = [dict(base, x=np.ascontiguousarray(x[b])) for b in range(8)]
    res = run_bass_kernel_spmd(nc, in_maps, core_ids=list(range(8)))
    return np.stack([res.results[b]["out"] for b in range(8)], axis=0)


# version 22
# speedup vs baseline: 1.0162x; 1.0162x over previous
import numpy as np
import ml_dtypes
from contextlib import ExitStack

import concourse.bass as bass
import concourse.mybir as mybir
from concourse.bass_utils import run_bass_kernel_spmd

F32 = mybir.dt.float32
BF16 = mybir.dt.bfloat16
AF = mybir.ActivationFunctionType
ALU = mybir.AluOpType
AX = mybir.AxisListType

T = 4096
D = 1024
H = 16
DK = 64
SEG = 512
S = 8
FF = 4096
JT = 8
W_ELEMS = float(FF * D)
EPS = 1e-5
OFFS = (-2, -1, 1, 2)


class SemTok:
    __slots__ = ("h", "count", "name")

    def __init__(self, h, name):
        self.h = h
        self.count = 0
        self.name = name


class Sched:
    ENGINES = ("tensor", "vector", "scalar", "sync", "gpsimd")

    def __init__(self, nc, ctx):
        self.nc = nc
        self.ctx = ctx
        self.prog = {e: [] for e in self.ENGINES}
        self._hi = {e: {} for e in self.ENGINES}

    def sem(self, name):
        h = self.ctx.enter_context(self.nc.semaphore(name))
        return SemTok(h, name)

    def op(self, engine, build, waits=(), inc=None, inc_n=1):
        wl = []
        hi = self._hi[engine]
        for tok in waits:
            if tok is None:
                continue
            sem, val = tok
            if val <= 0:
                continue
            if hi.get(id(sem), 0) >= val:
                continue
            hi[id(sem)] = val
            wl.append((sem.h, val))
        incs = []
        out_tok = None
        if inc is not None:
            inc.count += inc_n
            incs.append((inc.h, inc_n))
            out_tok = (inc, inc.count)

        def thunk(eng, wl=wl, incs=incs, build=build):
            for (h, v) in wl:
                eng.wait_ge(h, v)
            inst = build(eng)
            for (h, n) in incs:
                inst.then_inc(h, n)

        self.prog[engine].append(thunk)
        return out_tok

    def group(self, engine, items, inc=None, inc_n=1):
        tok = None
        for i, (build, waits) in enumerate(items):
            last = i == len(items) - 1
            tok = self.op(engine, build, waits=waits,
                          inc=inc if last else None, inc_n=inc_n)
        return tok

    def emit(self):
        block = self.ctx.enter_context(self.nc.Block())
        progs = self.prog

        @block.tensor
        def _(eng):
            for f in progs["tensor"]:
                f(eng)

        @block.vector
        def _(eng):
            for f in progs["vector"]:
                f(eng)

        @block.scalar
        def _(eng):
            for f in progs["scalar"]:
                f(eng)

        @block.sync
        def _(eng):
            for f in progs["sync"]:
                f(eng)

        @block.gpsimd
        def _(eng):
            for f in progs["gpsimd"]:
                f(eng)


def build_program(nc: bass.Bass, ctx: ExitStack):
    x_h = nc.dram_tensor("x", [T, D], F32, kind="ExternalInput").ap()
    wq_h = nc.dram_tensor("wq", [D, D], F32, kind="ExternalInput").ap()
    wk_h = nc.dram_tensor("wk", [D, D], F32, kind="ExternalInput").ap()
    wv_h = nc.dram_tensor("wv", [D, D], F32, kind="ExternalInput").ap()
    wo_h = nc.dram_tensor("wo", [D, D], F32, kind="ExternalInput").ap()
    bq_h = nc.dram_tensor("bq", [D], F32, kind="ExternalInput").ap()
    bk_h = nc.dram_tensor("bk", [D], F32, kind="ExternalInput").ap()
    bv_h = nc.dram_tensor("bv", [D], F32, kind="ExternalInput").ap()
    bo_h = nc.dram_tensor("bo", [D], F32, kind="ExternalInput").ap()
    w1_h = nc.dram_tensor("w1", [FF, D], F32, kind="ExternalInput").ap()
    b1_h = nc.dram_tensor("b1", [FF], F32, kind="ExternalInput").ap()
    w2_h = nc.dram_tensor("w2", [D, FF], F32, kind="ExternalInput").ap()
    b2_h = nc.dram_tensor("b2", [D], F32, kind="ExternalInput").ap()
    out_h = nc.dram_tensor("out", [T, D], F32, kind="ExternalOutput").ap()

    xnT_h = nc.dram_tensor("xnT_sc", [S, 128, JT, SEG], BF16, kind="Internal").ap()
    qT_h = nc.dram_tensor("qT_sc", [S, 128, JT, SEG], BF16, kind="Internal").ap()
    kT_h = nc.dram_tensor("kT_sc", [S, 128, JT, SEG], BF16, kind="Internal").ap()
    v_h = nc.dram_tensor("v_sc", [S, 128, 4, 16 * 65], BF16, kind="Internal").ap()
    s_h = nc.dram_tensor("s_sc", [T, D], F32, kind="Internal").ap()
    w1t_h = nc.dram_tensor("w1t_sc", [128, 32, 8, 128], BF16, kind="Internal").ap()
    w2t_h = nc.dram_tensor("w2t_sc", [128, 32, 1024], BF16, kind="Internal").ap()
    hT_h = nc.dram_tensor("hT_sc", [S, 128, 32, SEG], BF16, kind="Internal").ap()

    id_bf_h = nc.inline_tensor(np.eye(128, dtype=ml_dtypes.bfloat16), name="id_bf_h").ap()
    id_f32_h = nc.inline_tensor(np.eye(128, dtype=np.float32), name="id_f32_h").ap()
    ones_r_h = nc.inline_tensor(np.ones((1, 128), np.float32), name="ones_r_h").ap()
    ones_c_h = nc.inline_tensor(np.ones((128, 1), np.float32), name="ones_c_h").ap()

    sb = ctx.enter_context
    S_ = Sched(nc, ctx)
    OP = S_.op
    GRP = S_.group

    sem_t = S_.sem("t")
    sem_v = S_.sem("v")
    sem_a = S_.sem("a")
    sem_ld = S_.sem("ld")
    sem_out = S_.sem("outd")
    ld_x = [S_.sem(f"ldx{i}") for i in range(2)]
    ld_w = [S_.sem(f"ldw{i}") for i in range(4)]
    ld_qa = [S_.sem(f"ldqa{i}") for i in range(2)]
    ld_ka = [S_.sem(f"ldka{i}") for i in range(2)]
    ld_va = [S_.sem(f"ldva{i}") for i in range(2)]
    ld_vd = [S_.sem(f"ldvd{i}") for i in range(2)]
    ld_xg = [S_.sem(f"ldxg{i}") for i in range(2)]
    ld_hg = [S_.sem(f"ldhg{i}") for i in range(2)]
    ld_sf = [S_.sem(f"ldsf{i}") for i in range(2)]
    ld_xf = [S_.sem(f"ldxf{i}") for i in range(2)]
    ld_wc = [S_.sem(f"ldwc{i}") for i in range(4)]
    ld_big = S_.sem("ldbig")
    sp_q = [S_.sem(f"spq{i}") for i in range(2)]
    sp_k = [S_.sem(f"spk{i}") for i in range(2)]
    sp_v = [S_.sem(f"spv{i}") for i in range(2)]
    sp_xn = [S_.sem(f"spxn{i}") for i in range(2)]
    sp_s = [S_.sem(f"sps{i}") for i in range(2)]
    sp_w1 = [S_.sem(f"spw1{i}") for i in range(2)]
    sp_w2 = [S_.sem(f"spw2{i}") for i in range(2)]
    sp_h = [S_.sem(f"sph{i}") for i in range(2)]

    id_bf = sb(nc.sbuf_tensor("id_bf", [128, 128], BF16))
    id_f32 = sb(nc.sbuf_tensor("id_f32", [128, 128], F32))
    ones_r = sb(nc.sbuf_tensor("ones_r", [1, 128], F32))
    ones_c = sb(nc.sbuf_tensor("ones_c", [128, 1], F32))
    bqc = sb(nc.sbuf_tensor("bqc", [128, JT], F32))
    bkc = sb(nc.sbuf_tensor("bkc", [128, JT], F32))
    b1c = sb(nc.sbuf_tensor("b1c", [128, 32], F32))
    brow = sb(nc.sbuf_tensor("brow", [64, 128], F32))
    bvb = sb(nc.sbuf_tensor("bvb", [128, D], F32))
    brow1 = sb(nc.sbuf_tensor("brow1", [1, D], F32))
    th1 = sb(nc.sbuf_tensor("th1", [128, 1], F32))
    th1n = sb(nc.sbuf_tensor("th1n", [128, 1], F32))
    th2 = sb(nc.sbuf_tensor("th2", [128, 1], F32))
    th2n = sb(nc.sbuf_tensor("th2n", [128, 1], F32))
    scal = sb(nc.sbuf_tensor("scal", [1, 2], F32))
    eps_sb = sb(nc.sbuf_tensor("eps_sb", [128, 1], F32))

    psA = sb(nc.psum_tensor("psA", [128, 2048], F32))
    psB = sb(nc.psum_tensor("psB", [128, 2048], F32))

    for dst, src in ((id_bf[:], id_bf_h), (id_f32[:], id_f32_h),
                     (ones_r[:], ones_r_h), (ones_c[:], ones_c_h)):
        OP("sync", lambda e, d=dst, s=src: e.dma_start(out=d, in_=s),
           inc=sem_ld, inc_n=16)
    OP("sync", lambda e: e.dma_start(out=brow[0:8, :],
                                     in_=bq_h.rearrange("(t c) -> t c", t=8)),
       inc=sem_ld, inc_n=16)
    OP("sync", lambda e: e.dma_start(out=brow[32:40, :],
                                     in_=bk_h.rearrange("(t c) -> t c", t=8)),
       inc=sem_ld, inc_n=16)
    OP("sync", lambda e: e.dma_start(out=brow1[:],
                                     in_=bv_h.rearrange("(a c) -> a c", a=1)),
       inc=sem_ld, inc_n=16)
    tok_ld7 = (sem_ld, sem_ld.count)

    tok_zA = OP("vector", lambda e: e.memset(psA[:], 0.0), inc=sem_v)
    tok_eps = OP("vector", lambda e: e.memset(eps_sb[:], EPS), inc=sem_v)
    tok_zB = OP("vector", lambda e: e.memset(psB[:], 0.0), inc=sem_v)

    t1 = OP("tensor", lambda e: e.matmul(out=psB[:, 1536:1544], lhsT=brow[0:8, :],
                                         rhs=id_f32[0:8, 0:8], start=True, stop=True),
            waits=[tok_ld7, tok_zB], inc=sem_t)
    t2 = OP("tensor", lambda e: e.matmul(out=psB[:, 1544:1552], lhsT=brow[32:40, :],
                                         rhs=id_f32[32:40, 32:40], start=True, stop=True),
            inc=sem_t)
    tok_bqc = OP("vector", lambda e: e.tensor_copy(out=bqc[:], in_=psB[:, 1536:1544]),
                 waits=[t1], inc=sem_v)
    tok_bkc = OP("vector", lambda e: e.tensor_copy(out=bkc[:], in_=psB[:, 1544:1552]),
                 waits=[t2], inc=sem_v)
    bv2 = GRP("tensor", [
        (lambda e: e.matmul(out=psB[:, 1024:1536], lhsT=ones_r[0:1, :],
                            rhs=brow1[0:1, 0:512], start=True, stop=True),
         [tok_ld7, tok_zB]),
        (lambda e: e.matmul(out=psB[:, 1536 + 0:1536 + 512], lhsT=ones_r[0:1, :],
                            rhs=brow1[0:1, 512:1024], start=True, stop=True),
         [tok_bqc, tok_bkc]),
    ], inc=sem_t)
    vb1 = OP("vector", lambda e: e.tensor_copy(out=bvb[:], in_=psB[:, 1024:2048]),
             waits=[bv2], inc=sem_v)
    tok_b1row = OP("sync", lambda e: e.dma_start(
        out=brow[0:32, :], in_=b1_h.rearrange("(t c) -> t c", t=32)),
        waits=[t2], inc=sem_ld, inc_n=16)
    t3 = OP("tensor", lambda e: e.matmul(out=psB[:, 1536:1568], lhsT=brow[0:32, :],
                                         rhs=id_f32[0:32, 0:32], start=True, stop=True),
            waits=[tok_b1row, vb1, tok_bqc, tok_bkc], inc=sem_t)
    tok_b1c = OP("vector", lambda e: e.tensor_copy(out=b1c[:], in_=psB[:, 1536:1568]),
                 waits=[t3], inc=sem_v)
    tok_biases = tok_b1c

    es_pa = ExitStack()
    kdT = es_pa.enter_context(nc.sbuf_tensor("kdT", [128, S, JT, 128], BF16))
    vd = es_pa.enter_context(nc.sbuf_tensor("vd", [128, S, 16 * 65], BF16))

    es_p1 = ExitStack()
    p1 = es_p1.enter_context
    wq_sb = p1(nc.sbuf_tensor("wq_sb", [128, JT, D], BF16))
    wk_sb = p1(nc.sbuf_tensor("wk_sb", [128, JT, D], BF16))
    wv_sb = p1(nc.sbuf_tensor("wv_sb", [128, JT, D], BF16))
    wtmp = [p1(nc.sbuf_tensor(f"wtmp{i}", [128, 1024], F32)) for i in range(3)]
    x_sb = [p1(nc.sbuf_tensor(f"x_sb{i}", [128, 2, D], F32)) for i in range(2)]
    dump = p1(nc.sbuf_tensor("dump", [128, D], BF16))
    sg = [p1(nc.sbuf_tensor(f"sg{i}", [128, 4, 4], F32)) for i in range(2)]
    nw = [p1(nc.sbuf_tensor(f"nw{i}", [128, 32], F32)) for i in range(2)]
    xn_tm = [p1(nc.sbuf_tensor(f"xn_tm{i}", [128, D], BF16)) for i in range(4)]
    xnT_st = [p1(nc.sbuf_tensor(f"xnT_st{i}", [128, JT, SEG], BF16)) for i in range(2)]
    qk_st = [p1(nc.sbuf_tensor(f"qk_st{i}", [128, JT, SEG], BF16)) for i in range(2)]
    v_st = [p1(nc.sbuf_tensor(f"v_st{i}", [128, 4, 16 * 65], BF16)) for i in range(1)]
    wacc = p1(nc.sbuf_tensor("wacc", [128, 4], F32))
    wred = p1(nc.sbuf_tensor("wred", [128, 1], F32))
    wg2 = [p1(nc.sbuf_tensor(f"wg2{i}", [128, 1024], BF16)) for i in range(1)]
    wtern = [p1(nc.sbuf_tensor(f"wtern{i}", [128, 1024], BF16)) for i in range(3)]
    wstage = [p1(nc.sbuf_tensor(f"wstage{i}", [128, 8, 128], BF16)) for i in range(2)]

    last_w_use = [None, None, None, None]
    wi = 0
    tok_wqkv = None
    for (wsb, whbm) in ((wq_sb, wq_h), (wk_sb, wk_h), (wv_sb, wv_h)):
        for dj in range(JT):
            sl = wi % 3
            wi += 1
            tokl = OP("sync", lambda e, sl=sl, whbm=whbm, dj=dj: e.dma_start(
                out=wtmp[sl][:], in_=whbm[dj * 128:(dj + 1) * 128, :]),
                waits=[last_w_use[sl]], inc=ld_w[sl], inc_n=16)
            tok_wqkv = OP("vector", lambda e, sl=sl, wsb=wsb, dj=dj: e.tensor_copy(
                out=wsb[:, dj, :], in_=wtmp[sl][:]), waits=[tokl], inc=sem_v)
            last_w_use[sl] = tok_wqkv

    proj_ps_last = {0: None, 1: None, 2: None, 3: None}
    lnps = {"prev_ev": None}
    pslot = {"i": 0}
    ln_xn_tok = [[None] * 4 for _ in range(S)]
    spq_tok = [None] * S
    spk_tok = [None] * S
    spv_tok = [None] * S
    spxn_tok = [None] * S
    kd_tok = [None] * S
    vd_tok = [None] * S
    w1t_last_sp = None

    x_use = [None, None]
    xn_use = [None, None, None, None]
    xnT_use = [None, None]
    qk_use = [None, None]
    v_use = [None, None]
    stats_use = [None, None]
    w1g_use = [None]
    wstage_use = [None, None]

    w1t_sp_tok = [None] * 32
    wscale_idx = 0
    wtern_idx = 0
    tok_acc_init = OP("vector", lambda e: e.memset(wacc[:], 0.0), inc=sem_v)
    acc_last = {1: tok_acc_init, 2: tok_acc_init}
    acc_parity = {1: 0, 2: 0}
    tok_thr = [None, None]
    tern_state = {"w1_psB_free": None, "w2_psB_free": None,
                  "tern_read": None}

    def emit_scale_tiles(n, queue="sync"):
        nonlocal wscale_idx, wi
        for _ in range(n):
            i = wscale_idx
            if i >= 64:
                return
            wscale_idx += 1
            if i < 32:
                src = w1_h[i * 128:(i + 1) * 128, :]
                widx = 1
            else:
                j = i - 32
                dt, qq = j // 4, j % 4
                src = w2_h[dt * 128:(dt + 1) * 128, qq * 1024:(qq + 1) * 1024]
                widx = 2
            sl = wi % 3
            wi += 1
            tokl = OP(queue, lambda e, sl=sl, src=src: e.dma_start(
                out=wtmp[sl][:], in_=src), waits=[last_w_use[sl]],
                inc=ld_w[sl], inc_n=16)
            tokr = OP("scalar", lambda e, sl=sl: e.activation(
                out=dump[:], in_=wtmp[sl][:], func=AF.Abs,
                accum_out=wred[:]), waits=[tokl, acc_last[widx], tok_biases],
                inc=sem_a)
            last_w_use[sl] = tokr
            par = acc_parity[widx]
            a_src = 2 * (widx - 1) + par
            a_dst = 2 * (widx - 1) + (1 - par)
            acc_parity[widx] = 1 - par
            acc_last[widx] = OP("vector",
                                lambda e, a_src=a_src, a_dst=a_dst: e.tensor_tensor(
                                    out=wacc[:, a_dst:a_dst + 1],
                                    in0=wacc[:, a_src:a_src + 1],
                                    in1=wred[:], op=ALU.add),
                                waits=[tokr], inc=sem_v)

    def finalize_scales(which):
        for widx, (th, thn) in ((1, (th1, th1n)), (2, (th2, th2n))):
            if widx != which:
                continue
            col = 2 * (widx - 1) + acc_parity[widx]
            oc = 1568 + 2 * widx
            tokm = OP("tensor", lambda e, col=col, oc=oc: e.matmul(
                out=psB[0:1, oc:oc + 1], lhsT=wacc[:, col:col + 1],
                rhs=ones_c[:, 0:1], start=True, stop=True),
                waits=[acc_last[widx], tok_b1c, tok_thr[0]], inc=sem_t)
            tokc = OP("vector", lambda e, widx=widx, oc=oc: e.tensor_scalar(
                out=scal[0:1, widx - 1:widx], in0=psB[0:1, oc:oc + 1],
                scalar1=0.5 / W_ELEMS, scalar2=0.5e-8, op0=ALU.mult, op1=ALU.max),
                waits=[tokm], inc=sem_v)
            tokb = OP("tensor", lambda e, widx=widx, oc=oc: e.matmul(
                out=psB[:, oc + 8:oc + 9], lhsT=ones_r[0:1, :],
                rhs=scal[0:1, widx - 1:widx], start=True, stop=True),
                waits=[tokc], inc=sem_t)
            tokp = OP("vector", lambda e, th=th, oc=oc: e.tensor_copy(
                out=th[:], in_=psB[:, oc + 8:oc + 9]), waits=[tokb], inc=sem_v)
            tok_thr[widx - 1] = OP("vector", lambda e, th=th, thn=thn: e.tensor_scalar(
                out=thn[:], in0=th[:], scalar1=-1.0, scalar2=None, op0=ALU.mult),
                waits=[tokp], inc=sem_v)

    tern_pend = []

    def _w1_tern_partB(ft, gsl, tokt):
        nonlocal w1t_last_sp
        ssl = ft % 2
        ev_parts = []
        for b4 in range(2):
            psl = pslot["i"] % 4
            pslot["i"] += 1
            c0 = psl * 512
            items = []
            for i4 in range(4):
                dt = b4 * 4 + i4
                items.append((
                    lambda e, gsl=gsl, dt=dt, i4=i4, c0=c0: e.matmul(
                        out=psA[:, c0 + i4 * 128:c0 + (i4 + 1) * 128],
                        lhsT=wtern[gsl][:, dt * 128:(dt + 1) * 128], rhs=id_bf[:],
                        start=True, stop=True),
                    [tokt, wstage_use[ssl], proj_ps_last[psl], tok_zA,
                     (sem_ld, 16)] if i4 == 0 else []))
            tokmm = GRP("tensor", items, inc=sem_t)
            tern_state["tern_read"] = tokmm
            toke = OP("scalar", lambda e, ssl=ssl, b4=b4, c0=c0: e.activation(
                out=wstage[ssl][:, 4 * b4:4 * b4 + 4, :], in_=psA[:, c0:c0 + 512],
                func=AF.Copy), waits=[tokmm], inc=sem_a)
            proj_ps_last[psl] = toke
            ev_parts.append(toke)
        w1t_last_sp = OP("gpsimd", lambda e, ssl=ssl, ft=ft: e.dma_start(
            out=w1t_h[:, ft, :, :], in_=wstage[ssl][:]),
            waits=ev_parts, inc=sp_w1[ssl], inc_n=16)
        w1t_sp_tok[ft] = w1t_last_sp
        wstage_use[ssl] = w1t_last_sp

    def emit_w1_tern_tiles(n):
        nonlocal wtern_idx, wi
        for _ in range(n):
            ft = wtern_idx
            if ft >= 32:
                while tern_pend:
                    _w1_tern_partB(*tern_pend.pop(0))
                return
            wtern_idx += 1
            sl = wi % 3
            wi += 1
            gsl = ft % 3
            tokl = OP("sync", lambda e, sl=sl, ft=ft: e.dma_start(
                out=wtmp[sl][:], in_=w1_h[ft * 128:(ft + 1) * 128, :]),
                waits=[last_w_use[sl]], inc=ld_w[sl], inc_n=16)
            tg2 = OP("vector", lambda e, sl=sl: e.tensor_scalar(
                out=wg2[0][:], in0=wtmp[sl][:], scalar1=th1n[:, 0:1], scalar2=None,
                op0=ALU.is_lt), waits=[tokl, tok_thr[0], w1g_use[0]], inc=sem_v)
            tokt = OP("vector", lambda e, sl=sl, gsl=gsl: e.scalar_tensor_tensor(
                out=wtern[gsl][:], in0=wtmp[sl][:], scalar=th1[:, 0:1],
                in1=wg2[0][:], op0=ALU.is_gt, op1=ALU.subtract),
                waits=[tg2, tern_state.get("tern_read")], inc=sem_v)
            last_w_use[sl] = tokt
            w1g_use[0] = tokt
            tern_pend.append((ft, gsl, tokt))
            if len(tern_pend) > 2:
                _w1_tern_partB(*tern_pend.pop(0))

    xng = [p1(nc.sbuf_tensor(f"xng{i}", [128, JT, SEG], BF16)) for i in range(2)]
    wstr = [p1(nc.sbuf_tensor(f"wstr{i}", [128, 8, 128], BF16)) for i in range(4)]
    hst = [p1(nc.sbuf_tensor(f"hst{i}", [128, SEG], BF16)) for i in range(2)]

    w2_fj = 0
    w2t_last_sp = None

    w2_pend = []

    def _w2_tern_partB(fj, gsl, tokt):
        nonlocal w2t_last_sp
        ssl = fj % 2
        ev_parts = []
        for b4 in range(2):
            psl = pslot["i"] % 4
            pslot["i"] += 1
            c0 = psl * 512
            items = []
            for i4 in range(4):
                dt = b4 * 4 + i4
                items.append((
                    lambda e, gsl=gsl, dt=dt, i4=i4, c0=c0: e.matmul(
                        out=psA[:, c0 + i4 * 128:c0 + (i4 + 1) * 128],
                        lhsT=wtern[gsl][:, dt * 128:(dt + 1) * 128], rhs=id_bf[:],
                        start=True, stop=True),
                    [tokt, tok_thr[0], tok_thr[1], wstage_use[ssl],
                     proj_ps_last[psl], tok_zA] if i4 == 0 else []))
            tokmm = GRP("tensor", items, inc=sem_t)
            tern_state["tern_read"] = tokmm
            toke = OP("scalar", lambda e, ssl=ssl, b4=b4, c0=c0: e.activation(
                out=wstage[ssl][:, 4 * b4:4 * b4 + 4, :], in_=psA[:, c0:c0 + 512],
                func=AF.Copy), waits=[tokmm], inc=sem_a)
            proj_ps_last[psl] = toke
            ev_parts.append(toke)
        w2t_last_sp = OP("gpsimd", lambda e, ssl=ssl, fj=fj: e.dma_start(
            out=w2t_h[:, fj, :], in_=wstage[ssl][:]),
            waits=ev_parts, inc=sp_w2[ssl], inc_n=16)
        wstage_use[ssl] = w2t_last_sp

    def emit_w2_tern(n):
        nonlocal w2_fj, wi
        for _ in range(n):
            fj = w2_fj
            if fj >= 32:
                while w2_pend:
                    _w2_tern_partB(*w2_pend.pop(0))
                return
            w2_fj += 1
            sl = wi % 3
            wi += 1
            tokl = OP("sync", lambda e, sl=sl, fj=fj: e.dma_start(
                out=wtmp[sl][:].rearrange("p (a b) -> p a b", a=8),
                in_=w2_h[:, fj * 128:(fj + 1) * 128].rearrange(
                    "(dt p) c -> p dt c", dt=8)),
                waits=[last_w_use[sl]], inc=ld_w[sl], inc_n=16)
            gsl = fj % 3
            tg2 = OP("vector", lambda e, sl=sl: e.tensor_scalar(
                out=wg2[0][:], in0=wtmp[sl][:], scalar1=th2n[:, 0:1], scalar2=None,
                op0=ALU.is_lt), waits=[tokl, tok_thr[1], w1g_use[0]], inc=sem_v)
            tokt = OP("vector", lambda e, sl=sl, gsl=gsl: e.scalar_tensor_tensor(
                out=wtern[gsl][:], in0=wtmp[sl][:], scalar=th2[:, 0:1],
                in1=wg2[0][:], op0=ALU.is_gt, op1=ALU.subtract),
                waits=[tg2, tern_state.get("tern_read")], inc=sem_v)
            last_w_use[sl] = tokt
            w1g_use[0] = tokt
            w2_pend.append((fj, gsl, tokt))
            if len(w2_pend) > 2:
                _w2_tern_partB(*w2_pend.pop(0))

    f1_step = [0]
    f1_xng_tok = [None]
    f1_gl = [0]
    xng_use = [None, None]
    wstr_use = [None, None, None, None]
    h_ev = [None, None, None, None]
    hst_use = [None, None]
    hsp_tok = [[None] * 32 for _ in range(S)]

    def emit_f1_steps(n):
        for _ in range(n):
            step = f1_step[0]
            if step >= 256:
                return
            f1_step[0] = step + 1
            g, ft = step // 32, step % 32
            if ft == 0:
                gl = g % 2
                f1_gl[0] = gl
                f1_xng_tok[0] = OP("sync", lambda e, g=g, gl=gl: e.dma_start(
                    out=xng[gl][:], in_=xnT_h[g]),
                    waits=[spxn_tok[g], xng_use[gl]], inc=ld_xg[gl], inc_n=16)
            gl = f1_gl[0]
            wsl = step % 4
            tok_w = OP("sync", lambda e, wsl=wsl, ft=ft: e.dma_start(
                out=wstr[wsl][:], in_=w1t_h[:, ft, :, :]),
                waits=[w1t_sp_tok[ft], wstr_use[wsl]], inc=ld_wc[wsl], inc_n=16)
            psl = step % 4
            c0 = psl * 512
            if h_ev[psl] is None:
                h_ev[psl] = (tok_zB, tok_zB, vb1, tok_thr[1])[psl]
            items = []
            for dj in range(JT):
                w0 = []
                if dj == 0:
                    w0 = [tok_w, f1_xng_tok[0], h_ev[psl]]
                items.append((
                    lambda e, wsl=wsl, dj=dj, c0=c0, gl=gl: e.matmul(
                        out=psB[:, c0:c0 + 512], lhsT=wstr[wsl][:, dj, :],
                        rhs=xng[gl][:, dj, :],
                        start=(dj == 0), stop=(dj == JT - 1)), w0))
            tokmm = GRP("tensor", items, inc=sem_t)
            wstr_use[wsl] = tokmm
            hsl = step % 2
            tok_g = OP("scalar", lambda e, ft=ft, c0=c0, hsl=hsl: e.activation(
                out=hst[hsl][:], in_=psB[:, c0:c0 + 512], func=AF.Gelu,
                bias=b1c[:, ft:ft + 1]),
                waits=[tokmm, hst_use[hsl]], inc=sem_a)
            h_ev[psl] = tok_g
            tok_sp = OP("gpsimd", lambda e, g=g, ft=ft, hsl=hsl: e.dma_start(
                out=hT_h[g][:, ft, :], in_=hst[hsl][:]),
                waits=[tok_g], inc=sp_h[hsl], inc_n=16)
            hsp_tok[g][ft] = tok_sp
            hst_use[hsl] = tok_sp
            if ft == 31:
                xng_use[g % 2] = (sem_t, sem_t.count)

    def wprep_chunk(s, sixth):
        if s == 0:
            emit_scale_tiles(1)
            if sixth == 5:
                finalize_scales(1)
        elif s == 1:
            emit_w1_tern_tiles(3 if sixth < 4 else 2)
            emit_scale_tiles(4 if sixth < 2 else 3)
        elif s == 2:
            emit_w1_tern_tiles(3 if sixth < 4 else 2)
            emit_scale_tiles(2)
            if sixth == 5:
                finalize_scales(2)
        else:
            if s == 3 and sixth == 0:
                emit_w1_tern_tiles(1)
            emit_w2_tern(2 if sixth == 0 else 1)
            emit_f1_steps(9 if sixth < 3 else 8)
            if s == 7 and sixth == 5:
                emit_w2_tern(32)
                emit_f1_steps(256)

    xnT_done = [None] * S

    def ln_stage(s):
        xsl = s % 2
        sp = s % 2
        xh_tok = []
        for hs in range(2):
            sl = hs
            r0 = s * SEG + hs * 256
            tokx = OP("sync", lambda e, sl=sl, r0=r0: e.dma_start(
                out=x_sb[sl][:],
                in_=x_h[r0:r0 + 256, :].rearrange("(n p) c -> p n c", n=2)),
                waits=[x_use[sl]], inc=ld_x[sl], inc_n=16)
            xh_tok.append(tokx)
        tok_sq = None
        tok_rd = None
        for tt in range(4):
            hs, half = tt // 2, tt % 2
            xt = x_sb[hs][:, half, :]
            tok_sq = OP("scalar", lambda e, xt=xt, sp=sp, tt=tt: e.activation(
                out=dump[:], in_=xt, func=AF.Square,
                accum_out=sg[sp][:, tt, 1:2]),
                waits=[xh_tok[hs], stats_use[sp], tok_biases], inc=sem_a)
            tok_rd = OP("vector", lambda e, xt=xt, sp=sp, tt=tt: e.reduce_sum(
                out=sg[sp][:, tt, 0:1], in_=xt, axis=AX.X),
                waits=[xh_tok[hs], stats_use[sp]], inc=sem_v)
        m4 = OP("vector", lambda e, sp=sp: e.tensor_scalar(
            out=nw[sp][:, 0:4], in0=sg[sp][:, :, 0], scalar1=1.0 / D,
            scalar2=None, op0=ALU.mult), waits=[tok_rd, tok_sq], inc=sem_v)
        e4 = OP("vector", lambda e, sp=sp: e.tensor_scalar(
            out=nw[sp][:, 4:8], in0=sg[sp][:, :, 1], scalar1=1.0 / D,
            scalar2=None, op0=ALU.mult), inc=sem_v)
        mm4 = OP("vector", lambda e, sp=sp: e.tensor_tensor(
            out=nw[sp][:, 8:12], in0=nw[sp][:, 0:4], in1=nw[sp][:, 0:4],
            op=ALU.mult), waits=[m4], inc=sem_v)
        v4 = OP("vector", lambda e, sp=sp: e.tensor_tensor(
            out=sg[sp][:, :, 2], in0=nw[sp][:, 4:8], in1=nw[sp][:, 8:12],
            op=ALU.subtract), waits=[e4, mm4], inc=sem_v)
        vh4 = OP("vector", lambda e, sp=sp: e.tensor_scalar(
            out=nw[sp][:, 16:20], in0=sg[sp][:, :, 2], scalar1=0.5,
            scalar2=0.5 * EPS, op0=ALU.mult, op1=ALU.add), waits=[v4], inc=sem_v)
        td4 = OP("vector", lambda e, sp=sp: e.tensor_scalar(
            out=nw[sp][:, 20:24], in0=sg[sp][:, :, 2], scalar1=0.5,
            scalar2=0.5 + 0.5 * EPS, op0=ALU.mult, op1=ALU.add), inc=sem_v)
        tok_y = OP("vector", lambda e, sp=sp: e.reciprocal(
            out=nw[sp][:, 24:28], in_=nw[sp][:, 20:24]), waits=[td4], inc=sem_v)
        yc = [24, 28]
        for it in range(4):
            ca, cb = yc[it % 2], yc[(it + 1) % 2]
            t1 = OP("vector", lambda e, sp=sp, ca=ca: e.tensor_tensor(
                out=nw[sp][:, 8:12], in0=nw[sp][:, ca:ca + 4],
                in1=nw[sp][:, ca:ca + 4], op=ALU.mult), waits=[tok_y], inc=sem_v)
            t2 = OP("vector", lambda e, sp=sp: e.tensor_tensor(
                out=nw[sp][:, 12:16], in0=nw[sp][:, 8:12], in1=nw[sp][:, 16:20],
                op=ALU.mult), waits=[t1, vh4], inc=sem_v)
            t3 = OP("vector", lambda e, sp=sp: e.tensor_scalar(
                out=nw[sp][:, 8:12], in0=nw[sp][:, 12:16], scalar1=-1.0,
                scalar2=1.5, op0=ALU.mult, op1=ALU.add), waits=[t2], inc=sem_v)
            tok_y = OP("vector", lambda e, sp=sp, ca=ca, cb=cb: e.tensor_tensor(
                out=nw[sp][:, cb:cb + 4], in0=nw[sp][:, ca:ca + 4],
                in1=nw[sp][:, 8:12], op=ALU.mult), waits=[t3], inc=sem_v)
        yfin = yc[0]
        ng4 = OP("vector", lambda e, sp=sp, yfin=yfin: e.tensor_tensor(
            out=nw[sp][:, 12:16], in0=nw[sp][:, 0:4], in1=nw[sp][:, yfin:yfin + 4],
            op=ALU.mult), waits=[tok_y], inc=sem_v)
        tok_nm = OP("vector", lambda e, sp=sp: e.tensor_scalar(
            out=nw[sp][:, 8:12], in0=nw[sp][:, 12:16], scalar1=-1.0,
            scalar2=None, op0=ALU.mult), waits=[ng4], inc=sem_v)
        for tt in range(4):
            hs, half = tt // 2, tt % 2
            xt = x_sb[hs][:, half, :]
            xsl2 = tt
            tok_xn = OP("scalar", lambda e, xt=xt, sp=sp, tt=tt, xsl2=xsl2, yfin=yfin: e.activation(
                out=xn_tm[xsl2][:], in_=xt, func=AF.Identity,
                scale=nw[sp][:, yfin + tt:yfin + tt + 1],
                bias=nw[sp][:, 8 + tt:9 + tt]),
                waits=[tok_nm, xn_use[xsl2]], inc=sem_a)
            ln_xn_tok[s][tt] = tok_xn
            stats_use[sp] = tok_xn

    def ln_transpose_tiles(s, tts):
        xsl = s % 2
        for tt in tts:
            xsl2 = tt
            tok_xn = ln_xn_tok[s][tt]
            for g4 in range(2):
                psl = pslot["i"] % 4
                pslot["i"] += 1
                c0 = psl * 512
                items = []
                for j4 in range(4):
                    dj = g4 * 4 + j4
                    items.append((
                        lambda e, dj=dj, j4=j4, xsl2=xsl2, c0=c0: e.matmul(
                            out=psA[:, c0 + j4 * 128:c0 + (j4 + 1) * 128],
                            lhsT=xn_tm[xsl2][:, dj * 128:(dj + 1) * 128], rhs=id_bf[:],
                            start=True, stop=True),
                        [tok_xn, xnT_use[xsl], tok_zA, proj_ps_last[psl],
                         spxn_tok[s - 2] if s >= 2 else None] if j4 == 0 else []))
                tokmm = GRP("tensor", items, inc=sem_t)
                lnps["prev_ev"] = OP("scalar",
                                     lambda e, xsl=xsl, g4=g4, tt=tt, c0=c0: e.activation(
                                         out=xnT_st[xsl][:, g4 * 4:(g4 + 1) * 4,
                                                         tt * 128:(tt + 1) * 128],
                                         in_=psA[:, c0:c0 + 512], func=AF.Copy),
                                     waits=[tokmm], inc=sem_a)
                proj_ps_last[psl] = lnps["prev_ev"]
            xn_use[tt] = (sem_t, sem_t.count)
            if tt == 1:
                x_use[0] = lnps["prev_ev"]
            if tt == 3:
                x_use[1] = lnps["prev_ev"]
        if tts and tts[-1] == 3:
            tok_xnT = lnps["prev_ev"]
            xnT_done[s] = tok_xnT
            spxn_tok[s] = OP("gpsimd", lambda e, xsl=xsl, s=s: e.dma_start(
                out=xnT_h[s], in_=xnT_st[xsl][:]), waits=[tok_xnT],
                inc=sp_xn[xsl], inc_n=16)

    def proj_stage(s):
        xsl = s % 2
        tok_xnT = xnT_done[s]
        for which in range(2):
            wsb = (wq_sb, wk_sb)[which]
            bcol = (bqc, bkc)[which]
            sp_sem = (sp_q, sp_k)[which]
            sp_arr = (spq_tok, spk_tok)[which]
            dst_h = (qT_h, kT_h)[which]
            qsl = which
            ev_toks = []
            for jt in range(JT):
                psl = pslot["i"] % 4
                pslot["i"] += 1
                c0 = psl * 512
                items = []
                for dj in range(JT):
                    w0 = [tok_xnT, tok_wqkv, qk_use[qsl], tok_zA,
                          proj_ps_last[psl]] if dj == 0 else []
                    items.append((
                        lambda e, wsb=wsb, dj=dj, jt=jt, c0=c0, xsl=xsl: e.matmul(
                            out=psA[:, c0:c0 + 512],
                            lhsT=wsb[:, dj, jt * 128:(jt + 1) * 128],
                            rhs=xnT_st[xsl][:, dj, :],
                            start=(dj == 0), stop=(dj == JT - 1)), w0))
                tokmm = GRP("tensor", items, inc=sem_t)
                tok_ev = OP("vector",
                            lambda e, qsl=qsl, jt=jt, c0=c0, bcol=bcol: e.tensor_scalar(
                                out=qk_st[qsl][:, jt, :], in0=psA[:, c0:c0 + 512],
                                scalar1=bcol[:, jt:jt + 1], scalar2=None,
                                op0=ALU.add),
                            waits=[tokmm], inc=sem_v)
                proj_ps_last[psl] = tok_ev
                ev_toks.append(tok_ev)
                if jt == 3:
                    wprep_chunk(s, which * 2)
            extra = None
            if which == 1:
                kd_tok[s] = OP("vector", lambda e, qsl=qsl, s=s: e.tensor_copy(
                    out=kdT[:, s, :, :], in_=qk_st[qsl][:, :, 0:512:4]),
                    waits=[ev_toks[-1]], inc=sem_v)
                extra = kd_tok[s]
            sp_arr[s] = OP("gpsimd", lambda e, qsl=qsl, s=s, dst_h=dst_h: e.dma_start(
                out=dst_h[s], in_=qk_st[qsl][:]),
                waits=[ev_toks[-1], extra], inc=sp_sem[qsl], inc_n=16)
            qk_use[qsl] = sp_arr[s]
            if s + 1 < S:
                ln_transpose_tiles(s + 1, [2 * which, 2 * which + 1])
            wprep_chunk(s, which * 2 + 1)

        vsl = 0
        ev_last = None
        for tt in range(4):
            for halfj in range(2):
                psl = pslot["i"] % 4
                pslot["i"] += 1
                c0 = psl * 512
                items = []
                for dj in range(JT):
                    w0 = [tok_xnT, tok_wqkv, v_use[vsl], vb1, tok_zA,
                          proj_ps_last[psl]] if dj == 0 else []
                    items.append((
                        lambda e, dj=dj, tt=tt, halfj=halfj, c0=c0, xsl=xsl: e.matmul(
                            out=psA[:, c0:c0 + 512],
                            lhsT=xnT_st[xsl][:, dj, tt * 128:(tt + 1) * 128],
                            rhs=wv_sb[:, dj, halfj * 512:(halfj + 1) * 512],
                            start=(dj == 0), stop=(dj == JT - 1)), w0))
                tokmm = GRP("tensor", items, inc=sem_t)
                ev_last = OP("vector",
                             lambda e, vsl=vsl, tt=tt, halfj=halfj, c0=c0: e.tensor_tensor(
                                 out=v_st[vsl][:, tt, :].rearrange(
                                     "p (h c) -> p h c", c=65)[:, halfj * 8:(halfj + 1) * 8, 0:64],
                                 in0=psA[:, c0:c0 + 512].rearrange("p (h c) -> p h c", c=64),
                                 in1=bvb[:, halfj * 512:(halfj + 1) * 512].rearrange(
                                     "p (h c) -> p h c", c=64),
                                 op=ALU.add), waits=[tokmm], inc=sem_v)
                proj_ps_last[psl] = ev_last
                if tt == 1 and halfj == 1:
                    wprep_chunk(s, 4)
        tok_ones = OP("vector", lambda e, vsl=vsl: e.memset(
            v_st[vsl][:].rearrange("p t (h c) -> p t h c", c=65)[:, :, :, 64:65], 1.0),
            waits=[ev_last], inc=sem_v)
        spv_tok[s] = OP("gpsimd", lambda e, vsl=vsl, s=s: e.dma_start(
            out=v_h[s], in_=v_st[vsl][:]), waits=[tok_ones], inc=sp_v[vsl], inc_n=16)
        v_use[vsl] = spv_tok[s]
        gsl = s % 2
        tok_vd0 = OP("sync", lambda e, s=s: e.dma_start(
            out=vd[:, s, :], in_=v_h[s][0:128:4].rearrange("p t c -> t p c")),
            waits=[spv_tok[s], vd_tok[s - 2] if s >= 2 else None],
            inc=ld_vd[gsl], inc_n=16)
        vd_tok[s] = OP("vector", lambda e, s=s: e.memset(
            vd[:, s, :].rearrange("p (h c) -> p h c", c=65)[:, :, 64:65], 4.0),
            waits=[tok_vd0], inc=sem_v)
        if s + 1 < S:
            ln_transpose_tiles(s + 1, [])
        wprep_chunk(s, 5)

        xnT_use[xsl] = (sem_t, sem_t.count)


    ln_stage(0)
    ln_transpose_tiles(0, [0, 1, 2, 3])
    emit_scale_tiles(26)
    for s in range(S):
        if s + 1 < S:
            ln_stage(s + 1)
        proj_stage(s)

    p1_end = [(sem_a, sem_a.count), (sem_v, sem_v.count)]
    p1_all = p1_end + [(sem_t, sem_t.count)]
    for _sp in (sp_q, sp_k, sp_v, sp_xn, sp_w1, sp_w2, sp_h):
        p1_all += [(_sp[0], _sp[0].count), (_sp[1], _sp[1].count)]
    es_p1.close()

    es_a = ExitStack()
    pa = es_a.enter_context
    wo_sb = pa(nc.sbuf_tensor("wo_sb", [128, JT, D], BF16))
    wtmp_a = [pa(nc.sbuf_tensor(f"wtmpa{i}", [128, 1024], F32)) for i in range(2)]
    qT_a = [pa(nc.sbuf_tensor(f"qTa{i}", [128, JT, SEG], BF16)) for i in range(2)]
    kT_a = [pa(nc.sbuf_tensor(f"kTa{i}", [128, JT, SEG], BF16)) for i in range(2)]
    vp_a = [pa(nc.sbuf_tensor(f"vpa{i}", [128, 4, 16 * 65], BF16)) for i in range(2)]
    E_sb = [pa(nc.sbuf_tensor(f"E{i}", [128, 2048], BF16)) for i in range(2)]
    ao = [pa(nc.sbuf_tensor(f"ao{i}", [128, 4, D], BF16)) for i in range(2)]
    aoT = [pa(nc.sbuf_tensor(f"aoT{i}", [128, JT, SEG], BF16)) for i in range(2)]
    bob = pa(nc.sbuf_tensor("bob", [128, D], F32))
    r_sb = [pa(nc.sbuf_tensor(f"r{i}", [128, 2, 5], F32)) for i in range(2)]
    tmp_sb = [pa(nc.sbuf_tensor(f"tmp{i}", [128, 640], F32)) for i in range(2)]
    s_st = [pa(nc.sbuf_tensor(f"sst{i}", [128, D], F32)) for i in range(2)]

    last_wa_use = [None, None]
    tok_wo = None
    for dj in range(JT):
        sl = dj % 2
        tokl = OP("sync", lambda e, sl=sl, dj=dj: e.dma_start(
            out=wtmp_a[sl][:], in_=wo_h[dj * 128:(dj + 1) * 128, :]),
            waits=[last_wa_use[sl], last_w_use[sl]] + p1_all,
            inc=ld_w[sl], inc_n=16)
        tok_wo = OP("vector", lambda e, sl=sl, dj=dj: e.tensor_copy(
            out=wo_sb[:, dj, :], in_=wtmp_a[sl][:]), waits=[tokl], inc=sem_v)
        last_wa_use[sl] = tok_wo

    tok_bo = OP("sync", lambda e: e.dma_start(
        out=brow1[:], in_=bo_h.rearrange("(a c) -> a c", a=1)),
        waits=p1_all, inc=sem_ld, inc_n=16)
    bo2 = GRP("tensor", [
        (lambda e: e.matmul(out=psB[:, 1024:1536], lhsT=ones_r[0:1, :],
                            rhs=brow1[0:1, 0:512], start=True, stop=True),
         [tok_bo] + p1_all),
        (lambda e: e.matmul(out=psB[:, 1536:2048], lhsT=ones_r[0:1, :],
                            rhs=brow1[0:1, 512:1024], start=True, stop=True), []),
    ], inc=sem_t)
    vb2 = OP("vector", lambda e: e.tensor_copy(out=bob[:], in_=psB[:, 1024:2048]),
             waits=[bo2], inc=sem_v)

    qa_tok = [None] * S
    ka_tok = [None] * S
    va_tok = [None] * S
    qa_use = [None, None]
    ka_use = [None, None]
    va_use = [None, None]

    def prefetch_seg(s):
        sl = s % 2
        qa_tok[s] = OP("sync", lambda e, s=s, sl=sl: e.dma_start(
            out=qT_a[sl][:], in_=qT_h[s]),
            waits=[spq_tok[s], qa_use[sl]] + p1_all, inc=ld_qa[sl], inc_n=16)
        ka_tok[s] = OP("sync", lambda e, s=s, sl=sl: e.dma_start(
            out=kT_a[sl][:], in_=kT_h[s]),
            waits=[spk_tok[s], ka_use[sl]] + p1_all, inc=ld_ka[sl], inc_n=16)
        va_tok[s] = OP("sync", lambda e, s=s, sl=sl: e.dma_start(
            out=vp_a[sl][:], in_=v_h[s]),
            waits=[spv_tok[s], va_use[sl]] + p1_all, inc=ld_va[sl], inc_n=16)

    prefetch_seg(0)
    prefetch_seg(1)

    slot_done = {}
    loc_free = {}
    cross_free = {}
    exp_a_tok = {}
    exp_b_tok = {}
    av_tok = {}
    ao_read_tok = [None, None]
    aoT_read_tok = [None, None]
    sst_use = [None, None]
    red_last = None
    u_global = 0
    g_slot = 0
    pend = None

    for s in range(S):
        sl = s % 2
        aslot = s % 2
        valid = [o for o in OFFS if 0 <= s + o < S]
        nv = len(valid)

        def stage1(hp, qt, u, g, s=s, sl=sl, valid=valid, nv=nv):
            ps = psA if (g % 2 == 0) else psB
            q0 = qt * 128
            w0 = [loc_free.get(g - 2), qa_tok[s], ka_tok[s], tok_wo, tok_zA, tok_zB]
            if g < 2:
                w0 += p1_end + [vb2]
            items = []
            for kc in range(4):
                for hx in range(2):
                    pb = 64 * hx
                    items.append((
                        lambda e, ps=ps, kc=kc, hx=hx, pb=pb, sl=sl, hp=hp, q0=q0: e.matmul(
                            out=ps[:, hx * 1024 + kc * 128:hx * 1024 + (kc + 1) * 128],
                            lhsT=kT_a[sl][pb:pb + 64, hp, kc * 128:(kc + 1) * 128],
                            rhs=qT_a[sl][pb:pb + 64, hp, q0:q0 + 128],
                            start=True, stop=True),
                        w0 if (kc == 0 and hx == 0) else []))
            sc_loc = GRP("tensor", items, inc=sem_t)
            items = []
            for j, o in enumerate(valid):
                so = s + o
                for hx in range(2):
                    pb = 64 * hx
                    items.append((
                        lambda e, ps=ps, j=j, so=so, hx=hx, pb=pb, sl=sl, hp=hp, q0=q0: e.matmul(
                            out=ps[:, hx * 1024 + 512 + j * 128:hx * 1024 + 512 + (j + 1) * 128],
                            lhsT=kdT[pb:pb + 64, so, hp, :],
                            rhs=qT_a[sl][pb:pb + 64, hp, q0:q0 + 128],
                            start=True, stop=True),
                        [kd_tok[so], cross_free.get(g - 2)] if (j == 0 and hx == 0)
                        else [kd_tok[so]]))
            sc_cr = GRP("tensor", items, inc=sem_t)
            exp_a_tok[u] = OP("scalar", lambda e, ps=ps, u=u: e.activation(
                out=E_sb[u % 2][:].rearrange("p (h c) -> p h c", h=2)[:, :, 0:512],
                in_=ps.rearrange("p (h c) -> p h c", h=2)[:, :, 0:512],
                func=AF.Exp, scale=0.125),
                waits=[sc_loc, av_tok.get(u - 2)], inc=sem_a)
            loc_free[g] = exp_a_tok[u]
            exp_b_tok[u] = OP("scalar", lambda e, ps=ps, u=u, nv=nv: e.activation(
                out=E_sb[u % 2][:].rearrange("p (h c) -> p h c", h=2)[:, :, 512:512 + 128 * nv],
                in_=ps.rearrange("p (h c) -> p h c", h=2)[:, :, 512:512 + 128 * nv],
                func=AF.Exp, scale=0.125),
                waits=[sc_cr], inc=sem_a)

        def stage2(hp, qt, u, g, s=s, sl=sl, aslot=aslot, valid=valid, nv=nv):
            nonlocal red_last
            ps = psA if (g % 2 == 0) else psB
            items = []
            for hx in range(2):
                vcol = 65 * (2 * hp + hx)
                base = hx * 1024 + 512
                for kc in range(4):
                    items.append((
                        lambda e, ps=ps, u=u, hx=hx, base=base, kc=kc, sl=sl, vcol=vcol: e.matmul(
                            out=ps[:, base:base + 65],
                            lhsT=E_sb[u % 2][:, hx * 1024 + kc * 128:hx * 1024 + (kc + 1) * 128],
                            rhs=vp_a[sl][:, kc, vcol:vcol + 65],
                            start=(kc == 0), stop=(kc == 3)),
                        [exp_a_tok[u], exp_b_tok[u], va_tok[s]]
                        if (hx == 0 and kc == 0) else []))
                for j, o in enumerate(valid):
                    so = s + o
                    blk = base + 72 * (1 + j)
                    items.append((
                        lambda e, ps=ps, u=u, hx=hx, j=j, so=so, blk=blk, vcol=vcol: e.matmul(
                            out=ps[:, blk:blk + 65],
                            lhsT=E_sb[u % 2][:, hx * 1024 + 512 + j * 128:hx * 1024 + 512 + (j + 1) * 128],
                            rhs=vd[:, so, vcol:vcol + 65],
                            start=True, stop=True),
                        [vd_tok[so]]))
            av_tok[u] = GRP("tensor", items, inc=sem_t)
            rsl = u % 2
            nc1 = 1 + nv
            tok_rc = OP("vector", lambda e, ps=ps, rsl=rsl, nc1=nc1: e.reciprocal(
                out=r_sb[rsl][:, :, 0:nc1],
                in_=ps.rearrange("p (h c) -> p h c", h=2)[:, :, 576:576 + 72 * (nc1 - 1) + 1:72]),
                waits=[av_tok[u]], inc=sem_v)
            tok_mul = OP("vector", lambda e, ps=ps, rsl=rsl, nc1=nc1: e.tensor_tensor(
                out=tmp_sb[rsl][:].rearrange("p (x c j) -> p x c j", x=2, j=64)[:, :, 0:nc1, :],
                in0=ps.rearrange("p (h q) -> p h q", h=2)[:, :, 512:512 + 72 * nc1]
                      .rearrange("p h (c j) -> p h c j", j=72)[:, :, :, 0:64],
                in1=r_sb[rsl][:, :, 0:nc1].broadcast_to([128, 2, nc1, 64]),
                op=ALU.mult), waits=[tok_rc], inc=sem_v)
            slot_done[g] = tok_mul
            cross_free[g] = tok_mul

            def _red(e, rsl=rsl, nc1=nc1, aslot=aslot, qt=qt, hp=hp):
                with nc.allow_low_precision(reason="bf16 attnout operand"):
                    return e.tensor_reduce(
                        out=ao[aslot][:, qt, 128 * hp:128 * hp + 128].rearrange(
                            "p (h j) -> p h j", h=2),
                        in_=tmp_sb[rsl][:].rearrange(
                            "p (x c j) -> p x j c", x=2, j=64)[:, :, :, 0:nc1],
                        axis=AX.X, op=ALU.add)
            red_last = OP("vector", _red, waits=[ao_read_tok[aslot], tok_mul],
                          inc=sem_v)

        for hp in range(8):
            for qt in range(4):
                u = u_global
                u_global += 1
                g = g_slot
                g_slot += 1
                stage1(hp, qt, u, g)
                if pend is not None:
                    pend()
                pend = (lambda hp=hp, qt=qt, u=u, g=g: stage2(hp, qt, u, g))
        pend()
        pend = None

        pt_evac = None
        pt_last_mm = None
        for half4 in range(2):
            g = g_slot
            g_slot += 1
            ps = psA if (g % 2 == 0) else psB
            mm_toks = []
            first = True
            for jt_ in range(4 * half4, 4 * half4 + 4):
                items = []
                for tt in range(4):
                    w0 = []
                    if first:
                        w0 = [loc_free.get(g - 2), cross_free.get(g - 2),
                              slot_done.get(g - 2), red_last, aoT_read_tok[aslot]]
                        first = False
                    items.append((
                        lambda e, ps=ps, aslot=aslot, tt=tt, jt_=jt_: e.matmul(
                            out=ps[:, (jt_ % 4) * 512 + tt * 128:(jt_ % 4) * 512 + (tt + 1) * 128],
                            lhsT=ao[aslot][:, tt, jt_ * 128:(jt_ + 1) * 128],
                            rhs=id_bf[:], start=True, stop=True), w0))
                mm_toks.append(GRP("tensor", items, inc=sem_t))
            ev = None
            for i, jt_ in enumerate(range(4 * half4, 4 * half4 + 4)):
                ev = OP("scalar", lambda e, ps=ps, aslot=aslot, jt_=jt_: e.activation(
                    out=aoT[aslot][:, jt_, :],
                    in_=ps[:, (jt_ % 4) * 512:(jt_ % 4) * 512 + 512],
                    func=AF.Identity), waits=[mm_toks[i]], inc=sem_a)
            slot_done[g] = ev
            loc_free[g] = ev
            cross_free[g] = ev
            pt_evac = ev
            pt_last_mm = mm_toks[-1]
        ao_read_tok[aslot] = pt_last_mm

        o_last_mm = None
        for tg in range(2):
            g = g_slot
            g_slot += 1
            ps = psA if (g % 2 == 0) else psB
            dve_ev = None
            for ti, tt in enumerate((2 * tg, 2 * tg + 1)):
                c0 = ti * 1024
                items = []
                for halfj in range(2):
                    for jt_ in range(JT):
                        w0 = []
                        if halfj == 0 and jt_ == 0 and ti == 0:
                            w0 = [loc_free.get(g - 2), cross_free.get(g - 2),
                                  slot_done.get(g - 2), pt_evac]
                        items.append((
                            lambda e, ps=ps, aslot=aslot, tt=tt, jt_=jt_, halfj=halfj, c0=c0: e.matmul(
                                out=ps[:, c0 + halfj * 512:c0 + (halfj + 1) * 512],
                                lhsT=aoT[aslot][:, jt_, tt * 128:(tt + 1) * 128],
                                rhs=wo_sb[:, jt_, halfj * 512:(halfj + 1) * 512],
                                start=(jt_ == 0), stop=(jt_ == JT - 1)), w0))
                o_last_mm = GRP("tensor", items, inc=sem_t)
                ssl = tt % 2
                tok_oe = OP("vector", lambda e, ps=ps, c0=c0, ssl=ssl: e.tensor_tensor(
                    out=s_st[ssl][:], in0=ps[:, c0:c0 + 1024], in1=bob[:], op=ALU.add),
                    waits=[o_last_mm, vb2, sst_use[ssl]], inc=sem_v)
                dve_ev = tok_oe
                r0 = s * SEG + tt * 128
                sst_use[ssl] = OP("gpsimd", lambda e, ssl=ssl, r0=r0: e.dma_start(
                    out=s_h[r0:r0 + 128, :], in_=s_st[ssl][:]),
                    waits=[tok_oe], inc=sp_s[ssl], inc_n=16)
            slot_done[g] = dve_ev
            loc_free[g] = dve_ev
            cross_free[g] = dve_ev
        aoT_read_tok[aslot] = o_last_mm
        qa_use[sl] = ka_use[sl] = va_use[sl] = (sem_t, sem_t.count)
        if s + 2 < S:
            prefetch_seg(s + 2)

    sp_s_final = [(sp_s[0], sp_s[0].count), (sp_s[1], sp_s[1].count)]
    a_end = [(sem_a, sem_a.count), (sem_v, sem_v.count)]
    a_all = a_end + [(sem_t, sem_t.count)]
    es_a.close()
    es_pa.close()

    es_f2 = ExitStack()
    f2 = es_f2.enter_context
    w2t_sb = f2(nc.sbuf_tensor("w2t_sb", [128, 32, 1024], BF16))
    hT_g = [f2(nc.sbuf_tensor(f"hTg{i}", [128, 32, SEG], BF16)) for i in range(2)]
    s_f = [f2(nc.sbuf_tensor(f"s_f{i}", [128, D], F32)) for i in range(2)]
    x_f = [f2(nc.sbuf_tensor(f"x_f{i}", [128, D], F32)) for i in range(2)]
    gp = [f2(nc.sbuf_tensor(f"gp{i}", [128, D], F32)) for i in range(2)]
    ga = [f2(nc.sbuf_tensor(f"ga{i}", [128, D], F32)) for i in range(2)]
    ot = [f2(nc.sbuf_tensor(f"ot{i}", [128, D], F32)) for i in range(2)]

    b2b = f2(nc.sbuf_tensor("b2b", [128, D], F32))
    tok_b2r = OP("sync", lambda e: e.dma_start(
        out=brow1[:], in_=b2_h.rearrange("(a c) -> a c", a=1)),
        waits=a_all, inc=sem_ld, inc_n=16)
    b22 = GRP("tensor", [
        (lambda e: e.matmul(out=psB[:, 1024:1536], lhsT=ones_r[0:1, :],
                            rhs=brow1[0:1, 0:512], start=True, stop=True),
         [tok_b2r] + a_all),
        (lambda e: e.matmul(out=psB[:, 1536:2048], lhsT=ones_r[0:1, :],
                            rhs=brow1[0:1, 512:1024], start=True, stop=True), []),
    ], inc=sem_t)
    vb3 = OP("vector", lambda e: e.tensor_copy(out=b2b[:], in_=psB[:, 1024:2048]),
             waits=[b22], inc=sem_v)

    tok_w2t_c = []
    for ch in range(4):
        tok_w2t_c.append(OP("sync", lambda e, ch=ch: e.dma_start(
            out=w2t_sb[:, ch * 8:(ch + 1) * 8, :], in_=w2t_h[:, ch * 8:(ch + 1) * 8, :]),
            waits=([w2t_last_sp,
                    (sp_w2[0], sp_w2[0].count), (sp_w2[1], sp_w2[1].count)]
                   + a_all) if ch == 0 else [],
            inc=ld_big, inc_n=16))
    tok_w2t = tok_w2t_c[3]

    hg_use = [None, None]
    sf_use = [None, None]
    xf_use = [None, None]
    gp_use = [None, None]
    ga_use = [None, None]
    ot_use = [None, None]
    o_ev = [None, None]

    for g in range(S):
        gl = g % 2
        tok_hg_c = []
        for ch in range(4):
            tok_hg_c.append(OP("sync", lambda e, g=g, gl=gl, ch=ch: e.dma_start(
                out=hT_g[gl][:, ch * 8:(ch + 1) * 8, :],
                in_=hT_h[g][:, ch * 8:(ch + 1) * 8, :]),
                waits=([hsp_tok[g][30], hsp_tok[g][31], hg_use[gl]] + a_all)
                      if ch == 0 else [],
                inc=ld_hg[gl], inc_n=16))
        tok_hg = tok_hg_c[0]
        for tt in range(4):
            ssl = tt % 2
            r0 = g * SEG + tt * 128
            tok_sf = OP("sync", lambda e, ssl=ssl, r0=r0: e.dma_start(
                out=s_f[ssl][:], in_=s_h[r0:r0 + 128, :]),
                waits=[sf_use[ssl]] + sp_s_final + a_all,
                inc=ld_sf[ssl], inc_n=16)
            tok_xf = OP("sync", lambda e, ssl=ssl, r0=r0: e.dma_start(
                out=x_f[ssl][:], in_=x_h[r0:r0 + 128, :]),
                waits=[xf_use[ssl]] + a_all, inc=ld_xf[ssl], inc_n=16)
            c0 = (tt % 2) * 1024
            items = []
            for fj in range(32):
                for halfj in range(2):
                    w0 = [o_ev[tt % 2]] if (fj == 0 and halfj == 0) else []
                    if fj % 8 == 0 and halfj == 0:
                        w0 = w0 + [tok_w2t_c[fj // 8], tok_hg_c[fj // 8]]
                    if fj == 0 and halfj == 0 and g == 0 and tt < 2:
                        w0 = w0 + a_end
                    items.append((
                        lambda e, fj=fj, halfj=halfj, c0=c0, gl=gl, tt=tt: e.matmul(
                            out=psA[:, c0 + halfj * 512:c0 + (halfj + 1) * 512],
                            lhsT=hT_g[gl][:, fj, tt * 128:(tt + 1) * 128],
                            rhs=w2t_sb[:, fj, halfj * 512:(halfj + 1) * 512],
                            start=(fj == 0), stop=(fj == 31)), w0))
            tok_f2 = GRP("tensor", items, inc=sem_t)
            tok_gp = OP("vector", lambda e, c0=c0, ssl=ssl: e.tensor_tensor(
                out=gp[ssl][:], in0=psA[:, c0:c0 + 1024], in1=b2b[:], op=ALU.add),
                waits=[tok_f2, vb3, ga_use[ssl]], inc=sem_v)
            o_ev[tt % 2] = tok_gp
            tok_ga = OP("scalar", lambda e, ssl=ssl: e.activation(
                out=ga[ssl][:], in_=gp[ssl][:], func=AF.Silu), waits=[tok_gp],
                inc=sem_a)
            tok_m = OP("vector", lambda e, ssl=ssl: e.tensor_tensor(
                out=gp[ssl][:], in0=ga[ssl][:], in1=s_f[ssl][:], op=ALU.mult),
                waits=[tok_ga, tok_sf, ot_use[ssl]], inc=sem_v)
            tok_o = OP("vector", lambda e, ssl=ssl: e.tensor_tensor(
                out=ot[ssl][:], in0=gp[ssl][:], in1=x_f[ssl][:], op=ALU.add),
                waits=[tok_xf, tok_m], inc=sem_v)
            ga_use[ssl] = tok_o
            sf_use[ssl] = tok_o
            xf_use[ssl] = tok_o
            tok_st = OP("gpsimd", lambda e, ssl=ssl, r0=r0: e.dma_start(
                out=out_h[r0:r0 + 128, :], in_=ot[ssl][:]),
                waits=[tok_o], inc=sem_out, inc_n=16)
            ot_use[ssl] = tok_st
        hg_use[gl] = (sem_t, sem_t.count)

    OP("gpsimd", lambda e: e.engine_nop(), waits=[(sem_out, sem_out.count)])

    es_f2.close()
    S_.emit()
    return nc


def _build():
    nc = bass.Bass("TRN2", target_bir_lowering=False,
                   detect_race_conditions=False)
    ctx = ExitStack()
    with ctx:
        build_program(nc, ctx)
    return nc


def kernel(L0_attn_x, L0_attn_Wq, L0_attn_bq, L0_attn_Wk, L0_attn_bk,
           L0_attn_Wv, L0_attn_bv, L0_attn_Wo, L0_attn_bo,
           L0_ffn_W1, L0_ffn_b1, L0_ffn_W2, L0_ffn_b2):
    x = np.ascontiguousarray(np.asarray(L0_attn_x, dtype=np.float32))
    base = {
        "wq": np.ascontiguousarray(np.asarray(L0_attn_Wq, np.float32)),
        "wk": np.ascontiguousarray(np.asarray(L0_attn_Wk, np.float32)),
        "wv": np.ascontiguousarray(np.asarray(L0_attn_Wv, np.float32)),
        "wo": np.ascontiguousarray(np.asarray(L0_attn_Wo, np.float32)),
        "bq": np.ascontiguousarray(np.asarray(L0_attn_bq, np.float32)),
        "bk": np.ascontiguousarray(np.asarray(L0_attn_bk, np.float32)),
        "bv": np.ascontiguousarray(np.asarray(L0_attn_bv, np.float32)),
        "bo": np.ascontiguousarray(np.asarray(L0_attn_bo, np.float32)),
        "w1": np.ascontiguousarray(np.asarray(L0_ffn_W1, np.float32)),
        "b1": np.ascontiguousarray(np.asarray(L0_ffn_b1, np.float32)),
        "w2": np.ascontiguousarray(np.asarray(L0_ffn_W2, np.float32)),
        "b2": np.ascontiguousarray(np.asarray(L0_ffn_b2, np.float32)),
    }
    nc = _build()
    in_maps = [dict(base, x=np.ascontiguousarray(x[b])) for b in range(8)]
    res = run_bass_kernel_spmd(nc, in_maps, core_ids=list(range(8)))
    return np.stack([res.results[b]["out"] for b in range(8)], axis=0)


# version 23
# speedup vs baseline: 1.0257x; 1.0094x over previous
import numpy as np
import ml_dtypes
from contextlib import ExitStack

import concourse.bass as bass
import concourse.mybir as mybir
from concourse.bass_utils import run_bass_kernel_spmd

F32 = mybir.dt.float32
BF16 = mybir.dt.bfloat16
AF = mybir.ActivationFunctionType
ALU = mybir.AluOpType
AX = mybir.AxisListType

T = 4096
D = 1024
H = 16
DK = 64
SEG = 512
S = 8
FF = 4096
JT = 8
W_ELEMS = float(FF * D)
EPS = 1e-5
OFFS = (-2, -1, 1, 2)


class SemTok:
    __slots__ = ("h", "count", "name")

    def __init__(self, h, name):
        self.h = h
        self.count = 0
        self.name = name


class Sched:
    ENGINES = ("tensor", "vector", "scalar", "sync", "gpsimd")

    def __init__(self, nc, ctx):
        self.nc = nc
        self.ctx = ctx
        self.prog = {e: [] for e in self.ENGINES}
        self._hi = {e: {} for e in self.ENGINES}

    def sem(self, name):
        h = self.ctx.enter_context(self.nc.semaphore(name))
        return SemTok(h, name)

    def op(self, engine, build, waits=(), inc=None, inc_n=1):
        wl = []
        hi = self._hi[engine]
        for tok in waits:
            if tok is None:
                continue
            sem, val = tok
            if val <= 0:
                continue
            if hi.get(id(sem), 0) >= val:
                continue
            hi[id(sem)] = val
            wl.append((sem.h, val))
        incs = []
        out_tok = None
        if inc is not None:
            inc.count += inc_n
            incs.append((inc.h, inc_n))
            out_tok = (inc, inc.count)

        def thunk(eng, wl=wl, incs=incs, build=build):
            for (h, v) in wl:
                eng.wait_ge(h, v)
            inst = build(eng)
            for (h, n) in incs:
                inst.then_inc(h, n)

        self.prog[engine].append(thunk)
        return out_tok

    def group(self, engine, items, inc=None, inc_n=1):
        tok = None
        for i, (build, waits) in enumerate(items):
            last = i == len(items) - 1
            tok = self.op(engine, build, waits=waits,
                          inc=inc if last else None, inc_n=inc_n)
        return tok

    def emit(self):
        block = self.ctx.enter_context(self.nc.Block())
        progs = self.prog

        @block.tensor
        def _(eng):
            for f in progs["tensor"]:
                f(eng)

        @block.vector
        def _(eng):
            for f in progs["vector"]:
                f(eng)

        @block.scalar
        def _(eng):
            for f in progs["scalar"]:
                f(eng)

        @block.sync
        def _(eng):
            for f in progs["sync"]:
                f(eng)

        @block.gpsimd
        def _(eng):
            for f in progs["gpsimd"]:
                f(eng)


def build_program(nc: bass.Bass, ctx: ExitStack):
    x_h = nc.dram_tensor("x", [T, D], F32, kind="ExternalInput").ap()
    wq_h = nc.dram_tensor("wq", [D, D], F32, kind="ExternalInput").ap()
    wk_h = nc.dram_tensor("wk", [D, D], F32, kind="ExternalInput").ap()
    wv_h = nc.dram_tensor("wv", [D, D], F32, kind="ExternalInput").ap()
    wo_h = nc.dram_tensor("wo", [D, D], F32, kind="ExternalInput").ap()
    bq_h = nc.dram_tensor("bq", [D], F32, kind="ExternalInput").ap()
    bk_h = nc.dram_tensor("bk", [D], F32, kind="ExternalInput").ap()
    bv_h = nc.dram_tensor("bv", [D], F32, kind="ExternalInput").ap()
    bo_h = nc.dram_tensor("bo", [D], F32, kind="ExternalInput").ap()
    w1_h = nc.dram_tensor("w1", [FF, D], F32, kind="ExternalInput").ap()
    b1_h = nc.dram_tensor("b1", [FF], F32, kind="ExternalInput").ap()
    w2_h = nc.dram_tensor("w2", [D, FF], F32, kind="ExternalInput").ap()
    b2_h = nc.dram_tensor("b2", [D], F32, kind="ExternalInput").ap()
    out_h = nc.dram_tensor("out", [T, D], F32, kind="ExternalOutput").ap()

    xnT_h = nc.dram_tensor("xnT_sc", [S, 128, JT, SEG], BF16, kind="Internal").ap()
    qT_h = nc.dram_tensor("qT_sc", [S, 128, JT, SEG], BF16, kind="Internal").ap()
    kT_h = nc.dram_tensor("kT_sc", [S, 128, JT, SEG], BF16, kind="Internal").ap()
    v_h = nc.dram_tensor("v_sc", [S, 128, 4, 16 * 65], BF16, kind="Internal").ap()
    s_h = nc.dram_tensor("s_sc", [T, D], F32, kind="Internal").ap()
    w1t_h = nc.dram_tensor("w1t_sc", [128, 32, 8, 128], BF16, kind="Internal").ap()
    w2t_h = nc.dram_tensor("w2t_sc", [128, 32, 1024], BF16, kind="Internal").ap()
    hT_h = nc.dram_tensor("hT_sc", [S, 128, 32, SEG], BF16, kind="Internal").ap()

    id_bf_h = nc.inline_tensor(np.eye(128, dtype=ml_dtypes.bfloat16), name="id_bf_h").ap()
    id_f32_h = nc.inline_tensor(np.eye(128, dtype=np.float32), name="id_f32_h").ap()
    ones_r_h = nc.inline_tensor(np.ones((1, 128), np.float32), name="ones_r_h").ap()
    ones_c_h = nc.inline_tensor(np.ones((128, 1), np.float32), name="ones_c_h").ap()

    sb = ctx.enter_context
    S_ = Sched(nc, ctx)
    OP = S_.op
    GRP = S_.group

    sem_t = S_.sem("t")
    sem_v = S_.sem("v")
    sem_a = S_.sem("a")
    sem_ld = S_.sem("ld")
    sem_out = S_.sem("outd")
    ld_x = [S_.sem(f"ldx{i}") for i in range(2)]
    ld_w = [S_.sem(f"ldw{i}") for i in range(4)]
    ld_qa = [S_.sem(f"ldqa{i}") for i in range(2)]
    ld_ka = [S_.sem(f"ldka{i}") for i in range(2)]
    ld_va = [S_.sem(f"ldva{i}") for i in range(2)]
    ld_vd = [S_.sem(f"ldvd{i}") for i in range(2)]
    ld_xg = [S_.sem(f"ldxg{i}") for i in range(2)]
    ld_hg = [S_.sem(f"ldhg{i}") for i in range(2)]
    ld_sf = [S_.sem(f"ldsf{i}") for i in range(2)]
    ld_xf = [S_.sem(f"ldxf{i}") for i in range(2)]
    ld_wc = [S_.sem(f"ldwc{i}") for i in range(4)]
    ld_big = S_.sem("ldbig")
    sp_q = [S_.sem(f"spq{i}") for i in range(2)]
    sp_k = [S_.sem(f"spk{i}") for i in range(2)]
    sp_v = [S_.sem(f"spv{i}") for i in range(2)]
    sp_xn = [S_.sem(f"spxn{i}") for i in range(2)]
    sp_s = [S_.sem(f"sps{i}") for i in range(2)]
    sp_w1 = [S_.sem(f"spw1{i}") for i in range(2)]
    sp_w2 = [S_.sem(f"spw2{i}") for i in range(2)]
    sp_h = [S_.sem(f"sph{i}") for i in range(2)]

    id_bf = sb(nc.sbuf_tensor("id_bf", [128, 128], BF16))
    id_f32 = sb(nc.sbuf_tensor("id_f32", [128, 128], F32))
    ones_r = sb(nc.sbuf_tensor("ones_r", [1, 128], F32))
    ones_c = sb(nc.sbuf_tensor("ones_c", [128, 1], F32))
    bqc = sb(nc.sbuf_tensor("bqc", [128, JT], F32))
    bkc = sb(nc.sbuf_tensor("bkc", [128, JT], F32))
    b1c = sb(nc.sbuf_tensor("b1c", [128, 32], F32))
    brow = sb(nc.sbuf_tensor("brow", [64, 128], F32))
    bvb = sb(nc.sbuf_tensor("bvb", [128, D], F32))
    brow1 = sb(nc.sbuf_tensor("brow1", [1, D], F32))
    th1 = sb(nc.sbuf_tensor("th1", [128, 1], F32))
    th1n = sb(nc.sbuf_tensor("th1n", [128, 1], F32))
    th2 = sb(nc.sbuf_tensor("th2", [128, 1], F32))
    th2n = sb(nc.sbuf_tensor("th2n", [128, 1], F32))
    scal = sb(nc.sbuf_tensor("scal", [1, 2], F32))
    eps_sb = sb(nc.sbuf_tensor("eps_sb", [128, 1], F32))

    psA = sb(nc.psum_tensor("psA", [128, 2048], F32))
    psB = sb(nc.psum_tensor("psB", [128, 2048], F32))

    for dst, src in ((id_bf[:], id_bf_h), (id_f32[:], id_f32_h),
                     (ones_r[:], ones_r_h), (ones_c[:], ones_c_h)):
        OP("sync", lambda e, d=dst, s=src: e.dma_start(out=d, in_=s),
           inc=sem_ld, inc_n=16)
    OP("sync", lambda e: e.dma_start(out=brow[0:8, :],
                                     in_=bq_h.rearrange("(t c) -> t c", t=8)),
       inc=sem_ld, inc_n=16)
    OP("sync", lambda e: e.dma_start(out=brow[32:40, :],
                                     in_=bk_h.rearrange("(t c) -> t c", t=8)),
       inc=sem_ld, inc_n=16)
    OP("sync", lambda e: e.dma_start(out=brow1[:],
                                     in_=bv_h.rearrange("(a c) -> a c", a=1)),
       inc=sem_ld, inc_n=16)
    tok_ld7 = (sem_ld, sem_ld.count)

    tok_zA = OP("vector", lambda e: e.memset(psA[:], 0.0), inc=sem_v)
    tok_eps = OP("vector", lambda e: e.memset(eps_sb[:], EPS), inc=sem_v)
    tok_zB = OP("vector", lambda e: e.memset(psB[:], 0.0), inc=sem_v)

    t1 = OP("tensor", lambda e: e.matmul(out=psB[:, 1536:1544], lhsT=brow[0:8, :],
                                         rhs=id_f32[0:8, 0:8], start=True, stop=True),
            waits=[tok_ld7, tok_zB], inc=sem_t)
    t2 = OP("tensor", lambda e: e.matmul(out=psB[:, 1544:1552], lhsT=brow[32:40, :],
                                         rhs=id_f32[32:40, 32:40], start=True, stop=True),
            inc=sem_t)
    tok_bqc = OP("vector", lambda e: e.tensor_copy(out=bqc[:], in_=psB[:, 1536:1544]),
                 waits=[t1], inc=sem_v)
    tok_bkc = OP("vector", lambda e: e.tensor_copy(out=bkc[:], in_=psB[:, 1544:1552]),
                 waits=[t2], inc=sem_v)
    bv2 = GRP("tensor", [
        (lambda e: e.matmul(out=psB[:, 1024:1536], lhsT=ones_r[0:1, :],
                            rhs=brow1[0:1, 0:512], start=True, stop=True),
         [tok_ld7, tok_zB]),
        (lambda e: e.matmul(out=psB[:, 1536 + 0:1536 + 512], lhsT=ones_r[0:1, :],
                            rhs=brow1[0:1, 512:1024], start=True, stop=True),
         [tok_bqc, tok_bkc]),
    ], inc=sem_t)
    vb1 = OP("vector", lambda e: e.tensor_copy(out=bvb[:], in_=psB[:, 1024:2048]),
             waits=[bv2], inc=sem_v)
    tok_b1row = OP("sync", lambda e: e.dma_start(
        out=brow[0:32, :], in_=b1_h.rearrange("(t c) -> t c", t=32)),
        waits=[t2], inc=sem_ld, inc_n=16)
    t3 = OP("tensor", lambda e: e.matmul(out=psB[:, 1536:1568], lhsT=brow[0:32, :],
                                         rhs=id_f32[0:32, 0:32], start=True, stop=True),
            waits=[tok_b1row, vb1, tok_bqc, tok_bkc], inc=sem_t)
    tok_b1c = OP("vector", lambda e: e.tensor_copy(out=b1c[:], in_=psB[:, 1536:1568]),
                 waits=[t3], inc=sem_v)
    tok_biases = tok_b1c

    es_pa = ExitStack()
    kdT = es_pa.enter_context(nc.sbuf_tensor("kdT", [128, S, JT, 128], BF16))
    vd = es_pa.enter_context(nc.sbuf_tensor("vd", [128, S, 16 * 65], BF16))

    es_p1 = ExitStack()
    p1 = es_p1.enter_context
    wq_sb = p1(nc.sbuf_tensor("wq_sb", [128, JT, D], BF16))
    wk_sb = p1(nc.sbuf_tensor("wk_sb", [128, JT, D], BF16))
    wv_sb = p1(nc.sbuf_tensor("wv_sb", [128, JT, D], BF16))
    wtmp = [p1(nc.sbuf_tensor(f"wtmp{i}", [128, 1024], F32)) for i in range(3)]
    x_sb = [p1(nc.sbuf_tensor(f"x_sb{i}", [128, 2, D], F32)) for i in range(2)]
    dump = p1(nc.sbuf_tensor("dump", [128, D], BF16))
    sg = [p1(nc.sbuf_tensor(f"sg{i}", [128, 4, 4], F32)) for i in range(2)]
    nw = [p1(nc.sbuf_tensor(f"nw{i}", [128, 32], F32)) for i in range(2)]
    xn_tm = [p1(nc.sbuf_tensor(f"xn_tm{i}", [128, D], BF16)) for i in range(4)]
    xnT_st = [p1(nc.sbuf_tensor(f"xnT_st{i}", [128, JT, SEG], BF16)) for i in range(2)]
    qk_st = [p1(nc.sbuf_tensor(f"qk_st{i}", [128, JT, SEG], BF16)) for i in range(2)]
    v_st = [p1(nc.sbuf_tensor(f"v_st{i}", [128, 4, 16 * 65], BF16)) for i in range(1)]
    wacc = p1(nc.sbuf_tensor("wacc", [128, 4], F32))
    wred = p1(nc.sbuf_tensor("wred", [128, 1], F32))
    wg2 = [p1(nc.sbuf_tensor(f"wg2{i}", [128, 1024], BF16)) for i in range(1)]
    wtern = [p1(nc.sbuf_tensor(f"wtern{i}", [128, 1024], BF16)) for i in range(3)]
    wstage = [p1(nc.sbuf_tensor(f"wstage{i}", [128, 8, 128], BF16)) for i in range(2)]

    last_w_use = [None, None, None, None]
    wi = 0
    tok_wqkv = None
    for (wsb, whbm) in ((wq_sb, wq_h), (wk_sb, wk_h), (wv_sb, wv_h)):
        for dj in range(JT):
            sl = wi % 3
            wi += 1
            tokl = OP("sync", lambda e, sl=sl, whbm=whbm, dj=dj: e.dma_start(
                out=wtmp[sl][:], in_=whbm[dj * 128:(dj + 1) * 128, :]),
                waits=[last_w_use[sl]], inc=ld_w[sl], inc_n=16)
            tok_wqkv = OP("vector", lambda e, sl=sl, wsb=wsb, dj=dj: e.tensor_copy(
                out=wsb[:, dj, :], in_=wtmp[sl][:]), waits=[tokl], inc=sem_v)
            last_w_use[sl] = tok_wqkv

    proj_ps_last = {0: None, 1: None, 2: None, 3: None}
    lnps = {"prev_ev": None}
    pslot = {"i": 0}
    ln_xn_tok = [[None] * 4 for _ in range(S)]
    spq_tok = [None] * S
    spk_tok = [None] * S
    spv_tok = [None] * S
    spxn_tok = [None] * S
    kd_tok = [None] * S
    vd_tok = [None] * S
    w1t_last_sp = None

    x_use = [None, None]
    xn_use = [None, None, None, None]
    xnT_use = [None, None]
    qk_use = [None, None]
    v_use = [None, None]
    stats_use = [None, None]
    w1g_use = [None]
    wstage_use = [None, None]

    w1t_sp_tok = [None] * 32
    wscale_idx = 0
    wtern_idx = 0
    tok_acc_init = OP("vector", lambda e: e.memset(wacc[:], 0.0), inc=sem_v)
    acc_last = {1: tok_acc_init, 2: tok_acc_init}
    acc_parity = {1: 0, 2: 0}
    tok_thr = [None, None]
    tern_state = {"w1_psB_free": None, "w2_psB_free": None,
                  "tern_read": None}

    def emit_scale_tiles(n, queue="sync"):
        nonlocal wscale_idx, wi
        for _ in range(n):
            i = wscale_idx
            if i >= 64:
                return
            wscale_idx += 1
            if i < 32:
                src = w1_h[i * 128:(i + 1) * 128, :]
                widx = 1
            else:
                j = i - 32
                dt, qq = j // 4, j % 4
                src = w2_h[dt * 128:(dt + 1) * 128, qq * 1024:(qq + 1) * 1024]
                widx = 2
            sl = wi % 3
            wi += 1
            tokl = OP(queue, lambda e, sl=sl, src=src: e.dma_start(
                out=wtmp[sl][:], in_=src), waits=[last_w_use[sl]],
                inc=ld_w[sl], inc_n=16)
            tokr = OP("scalar", lambda e, sl=sl: e.activation(
                out=dump[:], in_=wtmp[sl][:], func=AF.Abs,
                accum_out=wred[:]), waits=[tokl, acc_last[widx], tok_biases],
                inc=sem_a)
            last_w_use[sl] = tokr
            par = acc_parity[widx]
            a_src = 2 * (widx - 1) + par
            a_dst = 2 * (widx - 1) + (1 - par)
            acc_parity[widx] = 1 - par
            acc_last[widx] = OP("vector",
                                lambda e, a_src=a_src, a_dst=a_dst: e.tensor_tensor(
                                    out=wacc[:, a_dst:a_dst + 1],
                                    in0=wacc[:, a_src:a_src + 1],
                                    in1=wred[:], op=ALU.add),
                                waits=[tokr], inc=sem_v)

    def finalize_scales(which):
        for widx, (th, thn) in ((1, (th1, th1n)), (2, (th2, th2n))):
            if widx != which:
                continue
            col = 2 * (widx - 1) + acc_parity[widx]
            oc = 1568 + 2 * widx
            tokm = OP("tensor", lambda e, col=col, oc=oc: e.matmul(
                out=psB[0:1, oc:oc + 1], lhsT=wacc[:, col:col + 1],
                rhs=ones_c[:, 0:1], start=True, stop=True),
                waits=[acc_last[widx], tok_b1c, tok_thr[0]], inc=sem_t)
            tokc = OP("vector", lambda e, widx=widx, oc=oc: e.tensor_scalar(
                out=scal[0:1, widx - 1:widx], in0=psB[0:1, oc:oc + 1],
                scalar1=0.5 / W_ELEMS, scalar2=0.5e-8, op0=ALU.mult, op1=ALU.max),
                waits=[tokm], inc=sem_v)
            tokb = OP("tensor", lambda e, widx=widx, oc=oc: e.matmul(
                out=psB[:, oc + 8:oc + 9], lhsT=ones_r[0:1, :],
                rhs=scal[0:1, widx - 1:widx], start=True, stop=True),
                waits=[tokc], inc=sem_t)
            tokp = OP("vector", lambda e, th=th, oc=oc: e.tensor_copy(
                out=th[:], in_=psB[:, oc + 8:oc + 9]), waits=[tokb], inc=sem_v)
            tok_thr[widx - 1] = OP("vector", lambda e, th=th, thn=thn: e.tensor_scalar(
                out=thn[:], in0=th[:], scalar1=-1.0, scalar2=None, op0=ALU.mult),
                waits=[tokp], inc=sem_v)

    tern_pend = []

    def _w1_tern_partB(ft, gsl, tokt):
        nonlocal w1t_last_sp
        ssl = ft % 2
        ev_parts = []
        for b4 in range(2):
            psl = pslot["i"] % 4
            pslot["i"] += 1
            c0 = psl * 512
            items = []
            for i4 in range(4):
                dt = b4 * 4 + i4
                items.append((
                    lambda e, gsl=gsl, dt=dt, i4=i4, c0=c0: e.matmul(
                        out=psA[:, c0 + i4 * 128:c0 + (i4 + 1) * 128],
                        lhsT=wtern[gsl][:, dt * 128:(dt + 1) * 128], rhs=id_bf[:],
                        start=True, stop=True),
                    [tokt, wstage_use[ssl], proj_ps_last[psl], tok_zA,
                     (sem_ld, 16)] if i4 == 0 else []))
            tokmm = GRP("tensor", items, inc=sem_t)
            tern_state["tern_read"] = tokmm
            toke = OP("scalar", lambda e, ssl=ssl, b4=b4, c0=c0: e.activation(
                out=wstage[ssl][:, 4 * b4:4 * b4 + 4, :], in_=psA[:, c0:c0 + 512],
                func=AF.Copy), waits=[tokmm], inc=sem_a)
            proj_ps_last[psl] = toke
            ev_parts.append(toke)
        w1t_last_sp = OP("gpsimd", lambda e, ssl=ssl, ft=ft: e.dma_start(
            out=w1t_h[:, ft, :, :], in_=wstage[ssl][:]),
            waits=ev_parts, inc=sp_w1[ssl], inc_n=16)
        w1t_sp_tok[ft] = w1t_last_sp
        wstage_use[ssl] = w1t_last_sp

    def emit_w1_tern_tiles(n):
        nonlocal wtern_idx, wi
        for _ in range(n):
            ft = wtern_idx
            if ft >= 32:
                while tern_pend:
                    _w1_tern_partB(*tern_pend.pop(0))
                return
            wtern_idx += 1
            sl = wi % 3
            wi += 1
            gsl = ft % 3
            tokl = OP("sync", lambda e, sl=sl, ft=ft: e.dma_start(
                out=wtmp[sl][:], in_=w1_h[ft * 128:(ft + 1) * 128, :]),
                waits=[last_w_use[sl]], inc=ld_w[sl], inc_n=16)
            tg2 = OP("vector", lambda e, sl=sl: e.tensor_scalar(
                out=wg2[0][:], in0=wtmp[sl][:], scalar1=th1n[:, 0:1], scalar2=None,
                op0=ALU.is_lt), waits=[tokl, tok_thr[0], w1g_use[0]], inc=sem_v)
            tokt = OP("vector", lambda e, sl=sl, gsl=gsl: e.scalar_tensor_tensor(
                out=wtern[gsl][:], in0=wtmp[sl][:], scalar=th1[:, 0:1],
                in1=wg2[0][:], op0=ALU.is_gt, op1=ALU.subtract),
                waits=[tg2, tern_state.get("tern_read")], inc=sem_v)
            last_w_use[sl] = tokt
            w1g_use[0] = tokt
            tern_pend.append((ft, gsl, tokt))
            if len(tern_pend) > 2:
                _w1_tern_partB(*tern_pend.pop(0))

    xng = [p1(nc.sbuf_tensor(f"xng{i}", [128, JT, SEG], BF16)) for i in range(2)]
    wstr = [p1(nc.sbuf_tensor(f"wstr{i}", [128, 8, 128], BF16)) for i in range(4)]
    hst = [p1(nc.sbuf_tensor(f"hst{i}", [128, SEG], BF16)) for i in range(2)]

    w2_fj = 0
    w2t_last_sp = None

    w2_pend = []

    def _w2_tern_partB(fj, gsl, tokt):
        nonlocal w2t_last_sp
        ssl = fj % 2
        ev_parts = []
        for b4 in range(2):
            psl = pslot["i"] % 4
            pslot["i"] += 1
            c0 = psl * 512
            items = []
            for i4 in range(4):
                dt = b4 * 4 + i4
                items.append((
                    lambda e, gsl=gsl, dt=dt, i4=i4, c0=c0: e.matmul(
                        out=psA[:, c0 + i4 * 128:c0 + (i4 + 1) * 128],
                        lhsT=wtern[gsl][:, dt * 128:(dt + 1) * 128], rhs=id_bf[:],
                        start=True, stop=True),
                    [tokt, tok_thr[0], tok_thr[1], wstage_use[ssl],
                     proj_ps_last[psl], tok_zA] if i4 == 0 else []))
            tokmm = GRP("tensor", items, inc=sem_t)
            tern_state["tern_read"] = tokmm
            toke = OP("scalar", lambda e, ssl=ssl, b4=b4, c0=c0: e.activation(
                out=wstage[ssl][:, 4 * b4:4 * b4 + 4, :], in_=psA[:, c0:c0 + 512],
                func=AF.Copy), waits=[tokmm], inc=sem_a)
            proj_ps_last[psl] = toke
            ev_parts.append(toke)
        w2t_last_sp = OP("gpsimd", lambda e, ssl=ssl, fj=fj: e.dma_start(
            out=w2t_h[:, fj, :], in_=wstage[ssl][:]),
            waits=ev_parts, inc=sp_w2[ssl], inc_n=16)
        wstage_use[ssl] = w2t_last_sp

    def emit_w2_tern(n):
        nonlocal w2_fj, wi
        for _ in range(n):
            fj = w2_fj
            if fj >= 32:
                while w2_pend:
                    _w2_tern_partB(*w2_pend.pop(0))
                return
            w2_fj += 1
            sl = wi % 3
            wi += 1
            tokl = OP("sync", lambda e, sl=sl, fj=fj: e.dma_start(
                out=wtmp[sl][:].rearrange("p (a b) -> p a b", a=8),
                in_=w2_h[:, fj * 128:(fj + 1) * 128].rearrange(
                    "(dt p) c -> p dt c", dt=8)),
                waits=[last_w_use[sl]], inc=ld_w[sl], inc_n=16)
            gsl = fj % 3
            tg2 = OP("vector", lambda e, sl=sl: e.tensor_scalar(
                out=wg2[0][:], in0=wtmp[sl][:], scalar1=th2n[:, 0:1], scalar2=None,
                op0=ALU.is_lt), waits=[tokl, tok_thr[1], w1g_use[0]], inc=sem_v)
            tokt = OP("vector", lambda e, sl=sl, gsl=gsl: e.scalar_tensor_tensor(
                out=wtern[gsl][:], in0=wtmp[sl][:], scalar=th2[:, 0:1],
                in1=wg2[0][:], op0=ALU.is_gt, op1=ALU.subtract),
                waits=[tg2, tern_state.get("tern_read")], inc=sem_v)
            last_w_use[sl] = tokt
            w1g_use[0] = tokt
            w2_pend.append((fj, gsl, tokt))
            if len(w2_pend) > 2:
                _w2_tern_partB(*w2_pend.pop(0))

    f1_step = [0]
    f1_xng_tok = [None]
    f1_gl = [0]
    xng_use = [None, None]
    wstr_use = [None, None, None, None]
    h_ev = [None, None, None, None]
    hst_use = [None, None]
    hsp_tok = [[None] * 32 for _ in range(S)]

    def emit_f1_steps(n):
        for _ in range(n):
            step = f1_step[0]
            if step >= 256:
                return
            f1_step[0] = step + 1
            g, ft = step // 32, step % 32
            if ft == 0:
                gl = g % 2
                f1_gl[0] = gl
                f1_xng_tok[0] = OP("sync", lambda e, g=g, gl=gl: e.dma_start(
                    out=xng[gl][:], in_=xnT_h[g]),
                    waits=[spxn_tok[g], xng_use[gl]], inc=ld_xg[gl], inc_n=16)
            gl = f1_gl[0]
            wsl = step % 4
            tok_w = OP("sync", lambda e, wsl=wsl, ft=ft: e.dma_start(
                out=wstr[wsl][:], in_=w1t_h[:, ft, :, :]),
                waits=[w1t_sp_tok[ft], wstr_use[wsl]], inc=ld_wc[wsl], inc_n=16)
            psl = step % 4
            c0 = psl * 512
            if h_ev[psl] is None:
                h_ev[psl] = (tok_zB, tok_zB, vb1, tok_thr[1])[psl]
            items = []
            for dj in range(JT):
                w0 = []
                if dj == 0:
                    w0 = [tok_w, f1_xng_tok[0], h_ev[psl]]
                items.append((
                    lambda e, wsl=wsl, dj=dj, c0=c0, gl=gl: e.matmul(
                        out=psB[:, c0:c0 + 512], lhsT=wstr[wsl][:, dj, :],
                        rhs=xng[gl][:, dj, :],
                        start=(dj == 0), stop=(dj == JT - 1)), w0))
            tokmm = GRP("tensor", items, inc=sem_t)
            wstr_use[wsl] = tokmm
            hsl = step % 2
            tok_g = OP("scalar", lambda e, ft=ft, c0=c0, hsl=hsl: e.activation(
                out=hst[hsl][:], in_=psB[:, c0:c0 + 512], func=AF.Gelu,
                bias=b1c[:, ft:ft + 1]),
                waits=[tokmm, hst_use[hsl]], inc=sem_a)
            h_ev[psl] = tok_g
            tok_sp = OP("gpsimd", lambda e, g=g, ft=ft, hsl=hsl: e.dma_start(
                out=hT_h[g][:, ft, :], in_=hst[hsl][:]),
                waits=[tok_g], inc=sp_h[hsl], inc_n=16)
            hsp_tok[g][ft] = tok_sp
            hst_use[hsl] = tok_sp
            if ft == 31:
                xng_use[g % 2] = (sem_t, sem_t.count)

    def wprep_chunk(s, sixth):
        if s == 0:
            emit_scale_tiles(1)
            if sixth == 5:
                finalize_scales(1)
        elif s == 1:
            emit_w1_tern_tiles(3 if sixth < 4 else 2)
            emit_scale_tiles(4 if sixth < 2 else 3)
        elif s == 2:
            emit_w1_tern_tiles(3 if sixth < 4 else 2)
            emit_scale_tiles(2)
            if sixth == 5:
                finalize_scales(2)
        else:
            if s == 3 and sixth == 0:
                emit_w1_tern_tiles(1)
            emit_w2_tern(2 if sixth == 0 else 1)
            emit_f1_steps(9 if sixth < 3 else 8)
            if s == 7 and sixth == 5:
                emit_w2_tern(32)
                emit_f1_steps(256)

    xnT_done = [None] * S

    def ln_stage(s):
        xsl = s % 2
        sp = s % 2
        xh_tok = []
        for hs in range(2):
            sl = hs
            r0 = s * SEG + hs * 256
            tokx = OP("sync", lambda e, sl=sl, r0=r0: e.dma_start(
                out=x_sb[sl][:],
                in_=x_h[r0:r0 + 256, :].rearrange("(n p) c -> p n c", n=2)),
                waits=[x_use[sl]], inc=ld_x[sl], inc_n=16)
            xh_tok.append(tokx)
        tok_sq = None
        tok_rd = None
        for tt in range(4):
            hs, half = tt // 2, tt % 2
            xt = x_sb[hs][:, half, :]
            tok_sq = OP("scalar", lambda e, xt=xt, sp=sp, tt=tt: e.activation(
                out=dump[:], in_=xt, func=AF.Square,
                accum_out=sg[sp][:, tt, 1:2]),
                waits=[xh_tok[hs], stats_use[sp], tok_biases], inc=sem_a)
            tok_rd = OP("vector", lambda e, xt=xt, sp=sp, tt=tt: e.reduce_sum(
                out=sg[sp][:, tt, 0:1], in_=xt, axis=AX.X),
                waits=[xh_tok[hs], stats_use[sp]], inc=sem_v)
        m4 = OP("vector", lambda e, sp=sp: e.tensor_scalar(
            out=nw[sp][:, 0:4], in0=sg[sp][:, :, 0], scalar1=1.0 / D,
            scalar2=None, op0=ALU.mult), waits=[tok_rd, tok_sq], inc=sem_v)
        e4 = OP("vector", lambda e, sp=sp: e.tensor_scalar(
            out=nw[sp][:, 4:8], in0=sg[sp][:, :, 1], scalar1=1.0 / D,
            scalar2=None, op0=ALU.mult), inc=sem_v)
        mm4 = OP("vector", lambda e, sp=sp: e.tensor_tensor(
            out=nw[sp][:, 8:12], in0=nw[sp][:, 0:4], in1=nw[sp][:, 0:4],
            op=ALU.mult), waits=[m4], inc=sem_v)
        v4 = OP("vector", lambda e, sp=sp: e.tensor_tensor(
            out=sg[sp][:, :, 2], in0=nw[sp][:, 4:8], in1=nw[sp][:, 8:12],
            op=ALU.subtract), waits=[e4, mm4], inc=sem_v)
        vh4 = OP("vector", lambda e, sp=sp: e.tensor_scalar(
            out=nw[sp][:, 16:20], in0=sg[sp][:, :, 2], scalar1=0.5,
            scalar2=0.5 * EPS, op0=ALU.mult, op1=ALU.add), waits=[v4], inc=sem_v)
        td4 = OP("vector", lambda e, sp=sp: e.tensor_scalar(
            out=nw[sp][:, 20:24], in0=sg[sp][:, :, 2], scalar1=0.5,
            scalar2=0.5 + 0.5 * EPS, op0=ALU.mult, op1=ALU.add), inc=sem_v)
        tok_y = OP("vector", lambda e, sp=sp: e.reciprocal(
            out=nw[sp][:, 24:28], in_=nw[sp][:, 20:24]), waits=[td4], inc=sem_v)
        yc = [24, 28]
        for it in range(4):
            ca, cb = yc[it % 2], yc[(it + 1) % 2]
            t1 = OP("vector", lambda e, sp=sp, ca=ca: e.tensor_tensor(
                out=nw[sp][:, 8:12], in0=nw[sp][:, ca:ca + 4],
                in1=nw[sp][:, ca:ca + 4], op=ALU.mult), waits=[tok_y], inc=sem_v)
            t2 = OP("vector", lambda e, sp=sp: e.tensor_tensor(
                out=nw[sp][:, 12:16], in0=nw[sp][:, 8:12], in1=nw[sp][:, 16:20],
                op=ALU.mult), waits=[t1, vh4], inc=sem_v)
            t3 = OP("vector", lambda e, sp=sp: e.tensor_scalar(
                out=nw[sp][:, 8:12], in0=nw[sp][:, 12:16], scalar1=-1.0,
                scalar2=1.5, op0=ALU.mult, op1=ALU.add), waits=[t2], inc=sem_v)
            tok_y = OP("vector", lambda e, sp=sp, ca=ca, cb=cb: e.tensor_tensor(
                out=nw[sp][:, cb:cb + 4], in0=nw[sp][:, ca:ca + 4],
                in1=nw[sp][:, 8:12], op=ALU.mult), waits=[t3], inc=sem_v)
        yfin = yc[0]
        ng4 = OP("vector", lambda e, sp=sp, yfin=yfin: e.tensor_tensor(
            out=nw[sp][:, 12:16], in0=nw[sp][:, 0:4], in1=nw[sp][:, yfin:yfin + 4],
            op=ALU.mult), waits=[tok_y], inc=sem_v)
        tok_nm = OP("vector", lambda e, sp=sp: e.tensor_scalar(
            out=nw[sp][:, 8:12], in0=nw[sp][:, 12:16], scalar1=-1.0,
            scalar2=None, op0=ALU.mult), waits=[ng4], inc=sem_v)
        for tt in range(4):
            hs, half = tt // 2, tt % 2
            xt = x_sb[hs][:, half, :]
            xsl2 = tt
            tok_xn = OP("scalar", lambda e, xt=xt, sp=sp, tt=tt, xsl2=xsl2, yfin=yfin: e.activation(
                out=xn_tm[xsl2][:], in_=xt, func=AF.Identity,
                scale=nw[sp][:, yfin + tt:yfin + tt + 1],
                bias=nw[sp][:, 8 + tt:9 + tt]),
                waits=[tok_nm, xn_use[xsl2]], inc=sem_a)
            ln_xn_tok[s][tt] = tok_xn
            stats_use[sp] = tok_xn

    def ln_transpose_tiles(s, tts):
        xsl = s % 2
        for tt in tts:
            xsl2 = tt
            tok_xn = ln_xn_tok[s][tt]
            for g4 in range(2):
                psl = pslot["i"] % 4
                pslot["i"] += 1
                c0 = psl * 512
                items = []
                for j4 in range(4):
                    dj = g4 * 4 + j4
                    items.append((
                        lambda e, dj=dj, j4=j4, xsl2=xsl2, c0=c0: e.matmul(
                            out=psA[:, c0 + j4 * 128:c0 + (j4 + 1) * 128],
                            lhsT=xn_tm[xsl2][:, dj * 128:(dj + 1) * 128], rhs=id_bf[:],
                            start=True, stop=True),
                        [tok_xn, xnT_use[xsl], tok_zA, proj_ps_last[psl],
                         spxn_tok[s - 2] if s >= 2 else None] if j4 == 0 else []))
                tokmm = GRP("tensor", items, inc=sem_t)
                lnps["prev_ev"] = OP("scalar",
                                     lambda e, xsl=xsl, g4=g4, tt=tt, c0=c0: e.activation(
                                         out=xnT_st[xsl][:, g4 * 4:(g4 + 1) * 4,
                                                         tt * 128:(tt + 1) * 128],
                                         in_=psA[:, c0:c0 + 512], func=AF.Copy),
                                     waits=[tokmm], inc=sem_a)
                proj_ps_last[psl] = lnps["prev_ev"]
            xn_use[tt] = (sem_t, sem_t.count)
            if tt == 1:
                x_use[0] = lnps["prev_ev"]
            if tt == 3:
                x_use[1] = lnps["prev_ev"]
        if tts and tts[-1] == 3:
            tok_xnT = lnps["prev_ev"]
            xnT_done[s] = tok_xnT
            spxn_tok[s] = OP("gpsimd", lambda e, xsl=xsl, s=s: e.dma_start(
                out=xnT_h[s], in_=xnT_st[xsl][:]), waits=[tok_xnT],
                inc=sp_xn[xsl], inc_n=16)

    def proj_stage(s):
        xsl = s % 2
        tok_xnT = xnT_done[s]
        for which in range(2):
            wsb = (wq_sb, wk_sb)[which]
            bcol = (bqc, bkc)[which]
            sp_sem = (sp_q, sp_k)[which]
            sp_arr = (spq_tok, spk_tok)[which]
            dst_h = (qT_h, kT_h)[which]
            qsl = which
            ev_toks = []
            for jt in range(JT):
                psl = pslot["i"] % 4
                pslot["i"] += 1
                c0 = psl * 512
                items = []
                for dj in range(JT):
                    w0 = [tok_xnT, tok_wqkv, qk_use[qsl], tok_zA,
                          proj_ps_last[psl]] if dj == 0 else []
                    items.append((
                        lambda e, wsb=wsb, dj=dj, jt=jt, c0=c0, xsl=xsl: e.matmul(
                            out=psA[:, c0:c0 + 512],
                            lhsT=wsb[:, dj, jt * 128:(jt + 1) * 128],
                            rhs=xnT_st[xsl][:, dj, :],
                            start=(dj == 0), stop=(dj == JT - 1)), w0))
                tokmm = GRP("tensor", items, inc=sem_t)
                tok_ev = OP("vector",
                            lambda e, qsl=qsl, jt=jt, c0=c0, bcol=bcol: e.tensor_scalar(
                                out=qk_st[qsl][:, jt, :], in0=psA[:, c0:c0 + 512],
                                scalar1=bcol[:, jt:jt + 1], scalar2=None,
                                op0=ALU.add),
                            waits=[tokmm], inc=sem_v)
                proj_ps_last[psl] = tok_ev
                ev_toks.append(tok_ev)
                if jt == 3:
                    wprep_chunk(s, which * 2)
            extra = None
            if which == 1:
                kd_tok[s] = OP("vector", lambda e, qsl=qsl, s=s: e.tensor_copy(
                    out=kdT[:, s, :, :], in_=qk_st[qsl][:, :, 0:512:4]),
                    waits=[ev_toks[-1]], inc=sem_v)
                extra = kd_tok[s]
            sp_arr[s] = OP("gpsimd", lambda e, qsl=qsl, s=s, dst_h=dst_h: e.dma_start(
                out=dst_h[s], in_=qk_st[qsl][:]),
                waits=[ev_toks[-1], extra], inc=sp_sem[qsl], inc_n=16)
            qk_use[qsl] = sp_arr[s]
            if s + 1 < S:
                ln_transpose_tiles(s + 1, [2 * which, 2 * which + 1])
            wprep_chunk(s, which * 2 + 1)

        vsl = 0
        ev_last = None
        for tt in range(4):
            for halfj in range(2):
                psl = pslot["i"] % 4
                pslot["i"] += 1
                c0 = psl * 512
                items = []
                for dj in range(JT):
                    w0 = [tok_xnT, tok_wqkv, v_use[vsl], vb1, tok_zA,
                          proj_ps_last[psl]] if dj == 0 else []
                    items.append((
                        lambda e, dj=dj, tt=tt, halfj=halfj, c0=c0, xsl=xsl: e.matmul(
                            out=psA[:, c0:c0 + 512],
                            lhsT=xnT_st[xsl][:, dj, tt * 128:(tt + 1) * 128],
                            rhs=wv_sb[:, dj, halfj * 512:(halfj + 1) * 512],
                            start=(dj == 0), stop=(dj == JT - 1)), w0))
                tokmm = GRP("tensor", items, inc=sem_t)
                ev_last = OP("vector",
                             lambda e, vsl=vsl, tt=tt, halfj=halfj, c0=c0: e.tensor_tensor(
                                 out=v_st[vsl][:, tt, :].rearrange(
                                     "p (h c) -> p h c", c=65)[:, halfj * 8:(halfj + 1) * 8, 0:64],
                                 in0=psA[:, c0:c0 + 512].rearrange("p (h c) -> p h c", c=64),
                                 in1=bvb[:, halfj * 512:(halfj + 1) * 512].rearrange(
                                     "p (h c) -> p h c", c=64),
                                 op=ALU.add), waits=[tokmm], inc=sem_v)
                proj_ps_last[psl] = ev_last
                if tt == 1 and halfj == 1:
                    wprep_chunk(s, 4)
        tok_ones = OP("vector", lambda e, vsl=vsl: e.memset(
            v_st[vsl][:].rearrange("p t (h c) -> p t h c", c=65)[:, :, :, 64:65], 1.0),
            waits=[ev_last], inc=sem_v)
        spv_tok[s] = OP("gpsimd", lambda e, vsl=vsl, s=s: e.dma_start(
            out=v_h[s], in_=v_st[vsl][:]), waits=[tok_ones], inc=sp_v[vsl], inc_n=16)
        v_use[vsl] = spv_tok[s]
        gsl = s % 2
        tok_vd0 = OP("sync", lambda e, s=s: e.dma_start(
            out=vd[:, s, :], in_=v_h[s][0:128:4].rearrange("p t c -> t p c")),
            waits=[spv_tok[s], vd_tok[s - 2] if s >= 2 else None],
            inc=ld_vd[gsl], inc_n=16)
        vd_tok[s] = OP("vector", lambda e, s=s: e.memset(
            vd[:, s, :].rearrange("p (h c) -> p h c", c=65)[:, :, 64:65], 4.0),
            waits=[tok_vd0], inc=sem_v)
        if s + 1 < S:
            ln_transpose_tiles(s + 1, [])
        wprep_chunk(s, 5)

        xnT_use[xsl] = (sem_t, sem_t.count)


    ln_stage(0)
    ln_transpose_tiles(0, [0, 1, 2, 3])
    emit_scale_tiles(26)
    for s in range(S):
        if s + 1 < S:
            ln_stage(s + 1)
        proj_stage(s)

    p1_end = [(sem_a, sem_a.count), (sem_v, sem_v.count)]
    p1_all = p1_end + [(sem_t, sem_t.count)]
    for _sp in (sp_q, sp_k, sp_v, sp_xn, sp_w1, sp_w2, sp_h):
        p1_all += [(_sp[0], _sp[0].count), (_sp[1], _sp[1].count)]
    es_p1.close()

    es_a = ExitStack()
    pa = es_a.enter_context
    wo_sb = pa(nc.sbuf_tensor("wo_sb", [128, JT, D], BF16))
    wtmp_a = [pa(nc.sbuf_tensor(f"wtmpa{i}", [128, 1024], F32)) for i in range(2)]
    qT_a = [pa(nc.sbuf_tensor(f"qTa{i}", [128, JT, SEG], BF16)) for i in range(2)]
    kT_a = [pa(nc.sbuf_tensor(f"kTa{i}", [128, JT, SEG], BF16)) for i in range(2)]
    vp_a = [pa(nc.sbuf_tensor(f"vpa{i}", [128, 4, 16 * 65], BF16)) for i in range(2)]
    E_sb = [pa(nc.sbuf_tensor(f"E{i}", [128, 2048], BF16)) for i in range(2)]
    ao = [pa(nc.sbuf_tensor(f"ao{i}", [128, 4, D], BF16)) for i in range(2)]
    aoT = [pa(nc.sbuf_tensor(f"aoT{i}", [128, JT, SEG], BF16)) for i in range(2)]
    bob = pa(nc.sbuf_tensor("bob", [128, D], F32))
    r_sb = [pa(nc.sbuf_tensor(f"r{i}", [128, 2, 5], F32)) for i in range(2)]
    tmp_sb = [pa(nc.sbuf_tensor(f"tmp{i}", [128, 640], F32)) for i in range(2)]
    s_st = [pa(nc.sbuf_tensor(f"sst{i}", [128, D], F32)) for i in range(2)]

    last_wa_use = [None, None]
    tok_wo = None
    for dj in range(JT):
        sl = dj % 2
        tokl = OP("sync", lambda e, sl=sl, dj=dj: e.dma_start(
            out=wtmp_a[sl][:], in_=wo_h[dj * 128:(dj + 1) * 128, :]),
            waits=[last_wa_use[sl], last_w_use[sl]] + p1_all,
            inc=ld_w[sl], inc_n=16)
        tok_wo = OP("vector", lambda e, sl=sl, dj=dj: e.tensor_copy(
            out=wo_sb[:, dj, :], in_=wtmp_a[sl][:]), waits=[tokl], inc=sem_v)
        last_wa_use[sl] = tok_wo

    tok_bo = OP("sync", lambda e: e.dma_start(
        out=brow1[:], in_=bo_h.rearrange("(a c) -> a c", a=1)),
        waits=p1_all, inc=sem_ld, inc_n=16)
    bo2 = GRP("tensor", [
        (lambda e: e.matmul(out=psB[:, 1024:1536], lhsT=ones_r[0:1, :],
                            rhs=brow1[0:1, 0:512], start=True, stop=True),
         [tok_bo] + p1_all),
        (lambda e: e.matmul(out=psB[:, 1536:2048], lhsT=ones_r[0:1, :],
                            rhs=brow1[0:1, 512:1024], start=True, stop=True), []),
    ], inc=sem_t)
    vb2 = OP("vector", lambda e: e.tensor_copy(out=bob[:], in_=psB[:, 1024:2048]),
             waits=[bo2], inc=sem_v)

    qa_tok = [None] * S
    ka_tok = [None] * S
    va_tok = [None] * S
    qa_use = [None, None]
    ka_use = [None, None]
    va_use = [None, None]

    def prefetch_seg(s):
        sl = s % 2
        qa_tok[s] = OP("sync", lambda e, s=s, sl=sl: e.dma_start(
            out=qT_a[sl][:], in_=qT_h[s]),
            waits=[spq_tok[s], qa_use[sl]] + p1_all, inc=ld_qa[sl], inc_n=16)
        ka_tok[s] = OP("sync", lambda e, s=s, sl=sl: e.dma_start(
            out=kT_a[sl][:], in_=kT_h[s]),
            waits=[spk_tok[s], ka_use[sl]] + p1_all, inc=ld_ka[sl], inc_n=16)
        va_tok[s] = OP("sync", lambda e, s=s, sl=sl: e.dma_start(
            out=vp_a[sl][:], in_=v_h[s]),
            waits=[spv_tok[s], va_use[sl]] + p1_all, inc=ld_va[sl], inc_n=16)

    prefetch_seg(0)
    prefetch_seg(1)

    slot_done = {}
    loc_free = {}
    cross_free = {}
    exp_a_tok = {}
    exp_b_tok = {}
    av_tok = {}
    ao_read_tok = [None, None]
    aoT_read_tok = [None, None]
    sst_use = [None, None]
    red_last = None
    u_global = 0
    g_slot_ref = {"g": 0}
    pend = None
    pseudo_pend = []

    for s in range(S):
        sl = s % 2
        aslot = s % 2
        valid = [o for o in OFFS if 0 <= s + o < S]
        nv = len(valid)

        def stage1(hp, qt, u, g, s=s, sl=sl, valid=valid, nv=nv):
            ps = psA if (g % 2 == 0) else psB
            q0 = qt * 128
            w0 = [loc_free.get(g - 2), qa_tok[s], ka_tok[s], tok_wo, tok_zA, tok_zB]
            if g < 2:
                w0 += p1_end + [vb2]
            items = []
            for kc in range(4):
                for hx in range(2):
                    pb = 64 * hx
                    items.append((
                        lambda e, ps=ps, kc=kc, hx=hx, pb=pb, sl=sl, hp=hp, q0=q0: e.matmul(
                            out=ps[:, hx * 1024 + kc * 128:hx * 1024 + (kc + 1) * 128],
                            lhsT=kT_a[sl][pb:pb + 64, hp, kc * 128:(kc + 1) * 128],
                            rhs=qT_a[sl][pb:pb + 64, hp, q0:q0 + 128],
                            start=True, stop=True),
                        w0 if (kc == 0 and hx == 0) else []))
            sc_loc = GRP("tensor", items, inc=sem_t)
            items = []
            for j, o in enumerate(valid):
                so = s + o
                for hx in range(2):
                    pb = 64 * hx
                    items.append((
                        lambda e, ps=ps, j=j, so=so, hx=hx, pb=pb, sl=sl, hp=hp, q0=q0: e.matmul(
                            out=ps[:, hx * 1024 + 512 + j * 128:hx * 1024 + 512 + (j + 1) * 128],
                            lhsT=kdT[pb:pb + 64, so, hp, :],
                            rhs=qT_a[sl][pb:pb + 64, hp, q0:q0 + 128],
                            start=True, stop=True),
                        [kd_tok[so], cross_free.get(g - 2)] if (j == 0 and hx == 0)
                        else [kd_tok[so]]))
            sc_cr = GRP("tensor", items, inc=sem_t)
            exp_a_tok[u] = OP("scalar", lambda e, ps=ps, u=u: e.activation(
                out=E_sb[u % 2][:].rearrange("p (h c) -> p h c", h=2)[:, :, 0:512],
                in_=ps.rearrange("p (h c) -> p h c", h=2)[:, :, 0:512],
                func=AF.Exp, scale=0.125),
                waits=[sc_loc, av_tok.get(u - 2)], inc=sem_a)
            loc_free[g] = exp_a_tok[u]
            exp_b_tok[u] = OP("scalar", lambda e, ps=ps, u=u, nv=nv: e.activation(
                out=E_sb[u % 2][:].rearrange("p (h c) -> p h c", h=2)[:, :, 512:512 + 128 * nv],
                in_=ps.rearrange("p (h c) -> p h c", h=2)[:, :, 512:512 + 128 * nv],
                func=AF.Exp, scale=0.125),
                waits=[sc_cr], inc=sem_a)

        def stage2(hp, qt, u, g, s=s, sl=sl, aslot=aslot, valid=valid, nv=nv):
            nonlocal red_last
            ps = psA if (g % 2 == 0) else psB
            items = []
            for hx in range(2):
                vcol = 65 * (2 * hp + hx)
                base = hx * 1024 + 512
                for kc in range(4):
                    items.append((
                        lambda e, ps=ps, u=u, hx=hx, base=base, kc=kc, sl=sl, vcol=vcol: e.matmul(
                            out=ps[:, base:base + 65],
                            lhsT=E_sb[u % 2][:, hx * 1024 + kc * 128:hx * 1024 + (kc + 1) * 128],
                            rhs=vp_a[sl][:, kc, vcol:vcol + 65],
                            start=(kc == 0), stop=(kc == 3)),
                        [exp_a_tok[u], exp_b_tok[u], va_tok[s]]
                        if (hx == 0 and kc == 0) else []))
                for j, o in enumerate(valid):
                    so = s + o
                    blk = base + 72 * (1 + j)
                    items.append((
                        lambda e, ps=ps, u=u, hx=hx, j=j, so=so, blk=blk, vcol=vcol: e.matmul(
                            out=ps[:, blk:blk + 65],
                            lhsT=E_sb[u % 2][:, hx * 1024 + 512 + j * 128:hx * 1024 + 512 + (j + 1) * 128],
                            rhs=vd[:, so, vcol:vcol + 65],
                            start=True, stop=True),
                        [vd_tok[so]]))
            av_tok[u] = GRP("tensor", items, inc=sem_t)
            rsl = u % 2
            nc1 = 1 + nv
            tok_rc = OP("vector", lambda e, ps=ps, rsl=rsl, nc1=nc1: e.reciprocal(
                out=r_sb[rsl][:, :, 0:nc1],
                in_=ps.rearrange("p (h c) -> p h c", h=2)[:, :, 576:576 + 72 * (nc1 - 1) + 1:72]),
                waits=[av_tok[u]], inc=sem_v)
            tok_mul = OP("vector", lambda e, ps=ps, rsl=rsl, nc1=nc1: e.tensor_tensor(
                out=tmp_sb[rsl][:].rearrange("p (x c j) -> p x c j", x=2, j=64)[:, :, 0:nc1, :],
                in0=ps.rearrange("p (h q) -> p h q", h=2)[:, :, 512:512 + 72 * nc1]
                      .rearrange("p h (c j) -> p h c j", j=72)[:, :, :, 0:64],
                in1=r_sb[rsl][:, :, 0:nc1].broadcast_to([128, 2, nc1, 64]),
                op=ALU.mult), waits=[tok_rc], inc=sem_v)
            slot_done[g] = tok_mul
            cross_free[g] = tok_mul

            def _red(e, rsl=rsl, nc1=nc1, aslot=aslot, qt=qt, hp=hp):
                with nc.allow_low_precision(reason="bf16 attnout operand"):
                    return e.tensor_reduce(
                        out=ao[aslot][:, qt, 128 * hp:128 * hp + 128].rearrange(
                            "p (h j) -> p h j", h=2),
                        in_=tmp_sb[rsl][:].rearrange(
                            "p (x c j) -> p x j c", x=2, j=64)[:, :, :, 0:nc1],
                        axis=AX.X, op=ALU.add)
            red_last = OP("vector", _red, waits=[ao_read_tok[aslot], tok_mul],
                          inc=sem_v)

        idx = 0
        for hp in range(8):
            for qt in range(4):
                u = u_global
                u_global += 1
                g = g_slot_ref["g"]
                g_slot_ref["g"] += 1
                stage1(hp, qt, u, g)
                if pend is not None:
                    pend()
                pend = (lambda hp=hp, qt=qt, u=u, g=g: stage2(hp, qt, u, g))
                if pseudo_pend and idx in (8, 20):
                    if pend is not None:
                        pend()
                        pend = None
                    pseudo_pend.pop(0)()
                    pseudo_pend.pop(0)()
                idx += 1
        pend()
        pend = None
        qa_use[sl] = ka_use[sl] = va_use[sl] = (sem_t, sem_t.count)
        if s + 2 < S:
            prefetch_seg(s + 2)

        def make_pseudo(s=s, sl=sl, aslot=aslot, red_of_seg=red_last):
            st = {"pt_evac": None, "pt_last_mm": None, "o_last_mm": None}

            def tr_slot(half4):
                g = g_slot_ref["g"]
                g_slot_ref["g"] += 1
                ps = psA if (g % 2 == 0) else psB
                mm_toks = []
                first = True
                for jt_ in range(4 * half4, 4 * half4 + 4):
                    items = []
                    for tt in range(4):
                        w0 = []
                        if first:
                            w0 = [loc_free.get(g - 2), cross_free.get(g - 2),
                                  slot_done.get(g - 2), red_of_seg,
                                  aoT_read_tok[aslot]]
                            first = False
                        items.append((
                            lambda e, ps=ps, aslot=aslot, tt=tt, jt_=jt_: e.matmul(
                                out=ps[:, (jt_ % 4) * 512 + tt * 128:(jt_ % 4) * 512 + (tt + 1) * 128],
                                lhsT=ao[aslot][:, tt, jt_ * 128:(jt_ + 1) * 128],
                                rhs=id_bf[:], start=True, stop=True), w0))
                    mm_toks.append(GRP("tensor", items, inc=sem_t))
                ev = None
                for i, jt_ in enumerate(range(4 * half4, 4 * half4 + 4)):
                    ev = OP("scalar", lambda e, ps=ps, aslot=aslot, jt_=jt_: e.activation(
                        out=aoT[aslot][:, jt_, :],
                        in_=ps[:, (jt_ % 4) * 512:(jt_ % 4) * 512 + 512],
                        func=AF.Identity), waits=[mm_toks[i]], inc=sem_a)
                slot_done[g] = ev
                loc_free[g] = ev
                cross_free[g] = ev
                st["pt_evac"] = ev
                st["pt_last_mm"] = mm_toks[-1]
                if half4 == 1:
                    ao_read_tok[aslot] = st["pt_last_mm"]

            def op_slot(tg):
                g = g_slot_ref["g"]
                g_slot_ref["g"] += 1
                ps = psA if (g % 2 == 0) else psB
                dve_ev = None
                for ti, tt in enumerate((2 * tg, 2 * tg + 1)):
                    c0 = ti * 1024
                    items = []
                    for halfj in range(2):
                        for jt_ in range(JT):
                            w0 = []
                            if halfj == 0 and jt_ == 0 and ti == 0:
                                w0 = [loc_free.get(g - 2), cross_free.get(g - 2),
                                      slot_done.get(g - 2), st["pt_evac"]]
                            items.append((
                                lambda e, ps=ps, aslot=aslot, tt=tt, jt_=jt_, halfj=halfj, c0=c0: e.matmul(
                                    out=ps[:, c0 + halfj * 512:c0 + (halfj + 1) * 512],
                                    lhsT=aoT[aslot][:, jt_, tt * 128:(tt + 1) * 128],
                                    rhs=wo_sb[:, jt_, halfj * 512:(halfj + 1) * 512],
                                    start=(jt_ == 0), stop=(jt_ == JT - 1)), w0))
                    st["o_last_mm"] = GRP("tensor", items, inc=sem_t)
                    ssl = tt % 2
                    tok_oe = OP("vector", lambda e, ps=ps, c0=c0, ssl=ssl: e.tensor_tensor(
                        out=s_st[ssl][:], in0=ps[:, c0:c0 + 1024], in1=bob[:], op=ALU.add),
                        waits=[st["o_last_mm"], vb2, sst_use[ssl]], inc=sem_v)
                    dve_ev = tok_oe
                    r0 = s * SEG + tt * 128
                    sst_use[ssl] = OP("gpsimd", lambda e, ssl=ssl, r0=r0: e.dma_start(
                        out=s_h[r0:r0 + 128, :], in_=s_st[ssl][:]),
                        waits=[tok_oe], inc=sp_s[ssl], inc_n=16)
                slot_done[g] = dve_ev
                loc_free[g] = dve_ev
                cross_free[g] = dve_ev
                if tg == 1:
                    aoT_read_tok[aslot] = st["o_last_mm"]

            return [lambda: tr_slot(0), lambda: tr_slot(1),
                    lambda: op_slot(0), lambda: op_slot(1)]

        pseudo_pend = make_pseudo()
        if s == S - 1:
            while pseudo_pend:
                pseudo_pend.pop(0)()

    sp_s_final = [(sp_s[0], sp_s[0].count), (sp_s[1], sp_s[1].count)]
    a_end = [(sem_a, sem_a.count), (sem_v, sem_v.count)]
    a_all = a_end + [(sem_t, sem_t.count)]
    es_a.close()
    es_pa.close()

    es_f2 = ExitStack()
    f2 = es_f2.enter_context
    w2t_sb = f2(nc.sbuf_tensor("w2t_sb", [128, 32, 1024], BF16))
    hT_g = [f2(nc.sbuf_tensor(f"hTg{i}", [128, 32, SEG], BF16)) for i in range(2)]
    s_f = [f2(nc.sbuf_tensor(f"s_f{i}", [128, D], F32)) for i in range(2)]
    x_f = [f2(nc.sbuf_tensor(f"x_f{i}", [128, D], F32)) for i in range(2)]
    gp = [f2(nc.sbuf_tensor(f"gp{i}", [128, D], F32)) for i in range(2)]
    ga = [f2(nc.sbuf_tensor(f"ga{i}", [128, D], F32)) for i in range(2)]
    ot = [f2(nc.sbuf_tensor(f"ot{i}", [128, D], F32)) for i in range(2)]

    b2b = f2(nc.sbuf_tensor("b2b", [128, D], F32))
    tok_b2r = OP("sync", lambda e: e.dma_start(
        out=brow1[:], in_=b2_h.rearrange("(a c) -> a c", a=1)),
        waits=a_all, inc=sem_ld, inc_n=16)
    b22 = GRP("tensor", [
        (lambda e: e.matmul(out=psB[:, 1024:1536], lhsT=ones_r[0:1, :],
                            rhs=brow1[0:1, 0:512], start=True, stop=True),
         [tok_b2r] + a_all),
        (lambda e: e.matmul(out=psB[:, 1536:2048], lhsT=ones_r[0:1, :],
                            rhs=brow1[0:1, 512:1024], start=True, stop=True), []),
    ], inc=sem_t)
    vb3 = OP("vector", lambda e: e.tensor_copy(out=b2b[:], in_=psB[:, 1024:2048]),
             waits=[b22], inc=sem_v)

    tok_w2t_c = []
    for ch in range(4):
        tok_w2t_c.append(OP("sync", lambda e, ch=ch: e.dma_start(
            out=w2t_sb[:, ch * 8:(ch + 1) * 8, :], in_=w2t_h[:, ch * 8:(ch + 1) * 8, :]),
            waits=([w2t_last_sp,
                    (sp_w2[0], sp_w2[0].count), (sp_w2[1], sp_w2[1].count)]
                   + a_all) if ch == 0 else [],
            inc=ld_big, inc_n=16))
    tok_w2t = tok_w2t_c[3]

    hg_use = [None, None]
    sf_use = [None, None]
    xf_use = [None, None]
    gp_use = [None, None]
    ga_use = [None, None]
    ot_use = [None, None]
    o_ev = [None, None]

    for g in range(S):
        gl = g % 2
        tok_hg_c = []
        for ch in range(4):
            tok_hg_c.append(OP("sync", lambda e, g=g, gl=gl, ch=ch: e.dma_start(
                out=hT_g[gl][:, ch * 8:(ch + 1) * 8, :],
                in_=hT_h[g][:, ch * 8:(ch + 1) * 8, :]),
                waits=([hsp_tok[g][30], hsp_tok[g][31], hg_use[gl]] + a_all)
                      if ch == 0 else [],
                inc=ld_hg[gl], inc_n=16))
        tok_hg = tok_hg_c[0]
        for tt in range(4):
            ssl = tt % 2
            r0 = g * SEG + tt * 128
            tok_sf = OP("sync", lambda e, ssl=ssl, r0=r0: e.dma_start(
                out=s_f[ssl][:], in_=s_h[r0:r0 + 128, :]),
                waits=[sf_use[ssl]] + sp_s_final + a_all,
                inc=ld_sf[ssl], inc_n=16)
            tok_xf = OP("sync", lambda e, ssl=ssl, r0=r0: e.dma_start(
                out=x_f[ssl][:], in_=x_h[r0:r0 + 128, :]),
                waits=[xf_use[ssl]] + a_all, inc=ld_xf[ssl], inc_n=16)
            c0 = (tt % 2) * 1024
            items = []
            for fj in range(32):
                for halfj in range(2):
                    w0 = [o_ev[tt % 2]] if (fj == 0 and halfj == 0) else []
                    if fj % 8 == 0 and halfj == 0:
                        w0 = w0 + [tok_w2t_c[fj // 8], tok_hg_c[fj // 8]]
                    if fj == 0 and halfj == 0 and g == 0 and tt < 2:
                        w0 = w0 + a_end
                    items.append((
                        lambda e, fj=fj, halfj=halfj, c0=c0, gl=gl, tt=tt: e.matmul(
                            out=psA[:, c0 + halfj * 512:c0 + (halfj + 1) * 512],
                            lhsT=hT_g[gl][:, fj, tt * 128:(tt + 1) * 128],
                            rhs=w2t_sb[:, fj, halfj * 512:(halfj + 1) * 512],
                            start=(fj == 0), stop=(fj == 31)), w0))
            tok_f2 = GRP("tensor", items, inc=sem_t)
            tok_gp = OP("vector", lambda e, c0=c0, ssl=ssl: e.tensor_tensor(
                out=gp[ssl][:], in0=psA[:, c0:c0 + 1024], in1=b2b[:], op=ALU.add),
                waits=[tok_f2, vb3, ga_use[ssl]], inc=sem_v)
            o_ev[tt % 2] = tok_gp
            tok_ga = OP("scalar", lambda e, ssl=ssl: e.activation(
                out=ga[ssl][:], in_=gp[ssl][:], func=AF.Silu), waits=[tok_gp],
                inc=sem_a)
            tok_m = OP("vector", lambda e, ssl=ssl: e.tensor_tensor(
                out=gp[ssl][:], in0=ga[ssl][:], in1=s_f[ssl][:], op=ALU.mult),
                waits=[tok_ga, tok_sf, ot_use[ssl]], inc=sem_v)
            tok_o = OP("vector", lambda e, ssl=ssl: e.tensor_tensor(
                out=ot[ssl][:], in0=gp[ssl][:], in1=x_f[ssl][:], op=ALU.add),
                waits=[tok_xf, tok_m], inc=sem_v)
            ga_use[ssl] = tok_o
            sf_use[ssl] = tok_o
            xf_use[ssl] = tok_o
            tok_st = OP("gpsimd", lambda e, ssl=ssl, r0=r0: e.dma_start(
                out=out_h[r0:r0 + 128, :], in_=ot[ssl][:]),
                waits=[tok_o], inc=sem_out, inc_n=16)
            ot_use[ssl] = tok_st
        hg_use[gl] = (sem_t, sem_t.count)

    OP("gpsimd", lambda e: e.engine_nop(), waits=[(sem_out, sem_out.count)])

    es_f2.close()
    S_.emit()
    return nc


def _build():
    nc = bass.Bass("TRN2", target_bir_lowering=False,
                   detect_race_conditions=False)
    ctx = ExitStack()
    with ctx:
        build_program(nc, ctx)
    return nc


def kernel(L0_attn_x, L0_attn_Wq, L0_attn_bq, L0_attn_Wk, L0_attn_bk,
           L0_attn_Wv, L0_attn_bv, L0_attn_Wo, L0_attn_bo,
           L0_ffn_W1, L0_ffn_b1, L0_ffn_W2, L0_ffn_b2):
    x = np.ascontiguousarray(np.asarray(L0_attn_x, dtype=np.float32))
    base = {
        "wq": np.ascontiguousarray(np.asarray(L0_attn_Wq, np.float32)),
        "wk": np.ascontiguousarray(np.asarray(L0_attn_Wk, np.float32)),
        "wv": np.ascontiguousarray(np.asarray(L0_attn_Wv, np.float32)),
        "wo": np.ascontiguousarray(np.asarray(L0_attn_Wo, np.float32)),
        "bq": np.ascontiguousarray(np.asarray(L0_attn_bq, np.float32)),
        "bk": np.ascontiguousarray(np.asarray(L0_attn_bk, np.float32)),
        "bv": np.ascontiguousarray(np.asarray(L0_attn_bv, np.float32)),
        "bo": np.ascontiguousarray(np.asarray(L0_attn_bo, np.float32)),
        "w1": np.ascontiguousarray(np.asarray(L0_ffn_W1, np.float32)),
        "b1": np.ascontiguousarray(np.asarray(L0_ffn_b1, np.float32)),
        "w2": np.ascontiguousarray(np.asarray(L0_ffn_W2, np.float32)),
        "b2": np.ascontiguousarray(np.asarray(L0_ffn_b2, np.float32)),
    }
    nc = _build()
    in_maps = [dict(base, x=np.ascontiguousarray(x[b])) for b in range(8)]
    res = run_bass_kernel_spmd(nc, in_maps, core_ids=list(range(8)))
    return np.stack([res.results[b]["out"] for b in range(8)], axis=0)


# version 24
# speedup vs baseline: 1.0380x; 1.0120x over previous
import numpy as np
import ml_dtypes
from contextlib import ExitStack

import concourse.bass as bass
import concourse.mybir as mybir
from concourse.bass_utils import run_bass_kernel_spmd

F32 = mybir.dt.float32
BF16 = mybir.dt.bfloat16
AF = mybir.ActivationFunctionType
ALU = mybir.AluOpType
AX = mybir.AxisListType

T = 4096
D = 1024
H = 16
DK = 64
SEG = 512
S = 8
FF = 4096
JT = 8
W_ELEMS = float(FF * D)
EPS = 1e-5
OFFS = (-2, -1, 1, 2)


class SemTok:
    __slots__ = ("h", "count", "name")

    def __init__(self, h, name):
        self.h = h
        self.count = 0
        self.name = name


class Sched:
    ENGINES = ("tensor", "vector", "scalar", "sync", "gpsimd")

    def __init__(self, nc, ctx):
        self.nc = nc
        self.ctx = ctx
        self.prog = {e: [] for e in self.ENGINES}
        self._hi = {e: {} for e in self.ENGINES}

    def sem(self, name):
        h = self.ctx.enter_context(self.nc.semaphore(name))
        return SemTok(h, name)

    def op(self, engine, build, waits=(), inc=None, inc_n=1):
        wl = []
        hi = self._hi[engine]
        for tok in waits:
            if tok is None:
                continue
            sem, val = tok
            if val <= 0:
                continue
            if hi.get(id(sem), 0) >= val:
                continue
            hi[id(sem)] = val
            wl.append((sem.h, val))
        incs = []
        out_tok = None
        if inc is not None:
            inc.count += inc_n
            incs.append((inc.h, inc_n))
            out_tok = (inc, inc.count)

        def thunk(eng, wl=wl, incs=incs, build=build):
            for (h, v) in wl:
                eng.wait_ge(h, v)
            inst = build(eng)
            for (h, n) in incs:
                inst.then_inc(h, n)

        self.prog[engine].append(thunk)
        return out_tok

    def group(self, engine, items, inc=None, inc_n=1):
        tok = None
        for i, (build, waits) in enumerate(items):
            last = i == len(items) - 1
            tok = self.op(engine, build, waits=waits,
                          inc=inc if last else None, inc_n=inc_n)
        return tok

    def emit(self):
        block = self.ctx.enter_context(self.nc.Block())
        progs = self.prog

        @block.tensor
        def _(eng):
            for f in progs["tensor"]:
                f(eng)

        @block.vector
        def _(eng):
            for f in progs["vector"]:
                f(eng)

        @block.scalar
        def _(eng):
            for f in progs["scalar"]:
                f(eng)

        @block.sync
        def _(eng):
            for f in progs["sync"]:
                f(eng)

        @block.gpsimd
        def _(eng):
            for f in progs["gpsimd"]:
                f(eng)


def build_program(nc: bass.Bass, ctx: ExitStack):
    x_h = nc.dram_tensor("x", [T, D], F32, kind="ExternalInput").ap()
    wq_h = nc.dram_tensor("wq", [D, D], F32, kind="ExternalInput").ap()
    wk_h = nc.dram_tensor("wk", [D, D], F32, kind="ExternalInput").ap()
    wv_h = nc.dram_tensor("wv", [D, D], F32, kind="ExternalInput").ap()
    wo_h = nc.dram_tensor("wo", [D, D], F32, kind="ExternalInput").ap()
    bq_h = nc.dram_tensor("bq", [D], F32, kind="ExternalInput").ap()
    bk_h = nc.dram_tensor("bk", [D], F32, kind="ExternalInput").ap()
    bv_h = nc.dram_tensor("bv", [D], F32, kind="ExternalInput").ap()
    bo_h = nc.dram_tensor("bo", [D], F32, kind="ExternalInput").ap()
    w1_h = nc.dram_tensor("w1", [FF, D], F32, kind="ExternalInput").ap()
    b1_h = nc.dram_tensor("b1", [FF], F32, kind="ExternalInput").ap()
    w2_h = nc.dram_tensor("w2", [D, FF], F32, kind="ExternalInput").ap()
    b2_h = nc.dram_tensor("b2", [D], F32, kind="ExternalInput").ap()
    out_h = nc.dram_tensor("out", [T, D], F32, kind="ExternalOutput").ap()

    xnT_h = nc.dram_tensor("xnT_sc", [S, 128, JT, SEG], BF16, kind="Internal").ap()
    qT_h = nc.dram_tensor("qT_sc", [S, 128, JT, SEG], BF16, kind="Internal").ap()
    kT_h = nc.dram_tensor("kT_sc", [S, 128, JT, SEG], BF16, kind="Internal").ap()
    v_h = nc.dram_tensor("v_sc", [S, 128, 4, 16 * 65], BF16, kind="Internal").ap()
    s_h = nc.dram_tensor("s_sc", [T, D], F32, kind="Internal").ap()
    w1t_h = nc.dram_tensor("w1t_sc", [128, 32, 8, 128], BF16, kind="Internal").ap()
    w2t_h = nc.dram_tensor("w2t_sc", [128, 32, 1024], BF16, kind="Internal").ap()
    hT_h = nc.dram_tensor("hT_sc", [S, 128, 32, SEG], BF16, kind="Internal").ap()

    id_bf_h = nc.inline_tensor(np.eye(128, dtype=ml_dtypes.bfloat16), name="id_bf_h").ap()
    id_f32_h = nc.inline_tensor(np.eye(128, dtype=np.float32), name="id_f32_h").ap()
    ones_r_h = nc.inline_tensor(np.ones((1, 128), np.float32), name="ones_r_h").ap()
    ones_c_h = nc.inline_tensor(np.ones((128, 1), np.float32), name="ones_c_h").ap()

    sb = ctx.enter_context
    S_ = Sched(nc, ctx)
    OP = S_.op
    GRP = S_.group

    sem_t = S_.sem("t")
    sem_v = S_.sem("v")
    sem_a = S_.sem("a")
    sem_ld = S_.sem("ld")
    sem_out = S_.sem("outd")
    ld_x = [S_.sem(f"ldx{i}") for i in range(2)]
    ld_w = [S_.sem(f"ldw{i}") for i in range(4)]
    ld_qa = [S_.sem(f"ldqa{i}") for i in range(2)]
    ld_ka = [S_.sem(f"ldka{i}") for i in range(2)]
    ld_va = [S_.sem(f"ldva{i}") for i in range(2)]
    ld_vd = [S_.sem(f"ldvd{i}") for i in range(2)]
    ld_xg = [S_.sem(f"ldxg{i}") for i in range(2)]
    ld_hg = [S_.sem(f"ldhg{i}") for i in range(2)]
    ld_sf = [S_.sem(f"ldsf{i}") for i in range(2)]
    ld_xf = [S_.sem(f"ldxf{i}") for i in range(2)]
    ld_wc = [S_.sem(f"ldwc{i}") for i in range(4)]
    ld_big = S_.sem("ldbig")
    sp_q = [S_.sem(f"spq{i}") for i in range(2)]
    sp_k = [S_.sem(f"spk{i}") for i in range(2)]
    sp_v = [S_.sem(f"spv{i}") for i in range(2)]
    sp_xn = [S_.sem(f"spxn{i}") for i in range(2)]
    sp_s = [S_.sem(f"sps{i}") for i in range(2)]
    sp_w1 = [S_.sem(f"spw1{i}") for i in range(2)]
    sp_w2 = [S_.sem(f"spw2{i}") for i in range(2)]
    sp_h = [S_.sem(f"sph{i}") for i in range(2)]

    id_bf = sb(nc.sbuf_tensor("id_bf", [128, 128], BF16))
    id_f32 = sb(nc.sbuf_tensor("id_f32", [128, 128], F32))
    ones_r = sb(nc.sbuf_tensor("ones_r", [1, 128], F32))
    ones_c = sb(nc.sbuf_tensor("ones_c", [128, 1], F32))
    bqc = sb(nc.sbuf_tensor("bqc", [128, JT], F32))
    bkc = sb(nc.sbuf_tensor("bkc", [128, JT], F32))
    b1c = sb(nc.sbuf_tensor("b1c", [128, 32], F32))
    brow = sb(nc.sbuf_tensor("brow", [64, 128], F32))
    bvb = sb(nc.sbuf_tensor("bvb", [128, D], F32))
    brow1 = sb(nc.sbuf_tensor("brow1", [1, D], F32))
    th1 = sb(nc.sbuf_tensor("th1", [128, 1], F32))
    th1n = sb(nc.sbuf_tensor("th1n", [128, 1], F32))
    th2 = sb(nc.sbuf_tensor("th2", [128, 1], F32))
    th2n = sb(nc.sbuf_tensor("th2n", [128, 1], F32))
    scal = sb(nc.sbuf_tensor("scal", [1, 2], F32))
    eps_sb = sb(nc.sbuf_tensor("eps_sb", [128, 1], F32))

    psA = sb(nc.psum_tensor("psA", [128, 2048], F32))
    psB = sb(nc.psum_tensor("psB", [128, 2048], F32))

    for dst, src in ((id_bf[:], id_bf_h), (id_f32[:], id_f32_h),
                     (ones_r[:], ones_r_h), (ones_c[:], ones_c_h)):
        OP("sync", lambda e, d=dst, s=src: e.dma_start(out=d, in_=s),
           inc=sem_ld, inc_n=16)
    OP("sync", lambda e: e.dma_start(out=brow[0:8, :],
                                     in_=bq_h.rearrange("(t c) -> t c", t=8)),
       inc=sem_ld, inc_n=16)
    OP("sync", lambda e: e.dma_start(out=brow[32:40, :],
                                     in_=bk_h.rearrange("(t c) -> t c", t=8)),
       inc=sem_ld, inc_n=16)
    OP("sync", lambda e: e.dma_start(out=brow1[:],
                                     in_=bv_h.rearrange("(a c) -> a c", a=1)),
       inc=sem_ld, inc_n=16)
    tok_ld7 = (sem_ld, sem_ld.count)

    tok_zA = OP("vector", lambda e: e.memset(psA[:], 0.0), inc=sem_v)
    tok_eps = OP("vector", lambda e: e.memset(eps_sb[:], EPS), inc=sem_v)
    tok_zB = OP("vector", lambda e: e.memset(psB[:], 0.0), inc=sem_v)

    t1 = OP("tensor", lambda e: e.matmul(out=psB[:, 1536:1544], lhsT=brow[0:8, :],
                                         rhs=id_f32[0:8, 0:8], start=True, stop=True),
            waits=[tok_ld7, tok_zB], inc=sem_t)
    t2 = OP("tensor", lambda e: e.matmul(out=psB[:, 1544:1552], lhsT=brow[32:40, :],
                                         rhs=id_f32[32:40, 32:40], start=True, stop=True),
            inc=sem_t)
    tok_bqc = OP("vector", lambda e: e.tensor_copy(out=bqc[:], in_=psB[:, 1536:1544]),
                 waits=[t1], inc=sem_v)
    tok_bkc = OP("vector", lambda e: e.tensor_copy(out=bkc[:], in_=psB[:, 1544:1552]),
                 waits=[t2], inc=sem_v)
    bv2 = GRP("tensor", [
        (lambda e: e.matmul(out=psB[:, 1024:1536], lhsT=ones_r[0:1, :],
                            rhs=brow1[0:1, 0:512], start=True, stop=True),
         [tok_ld7, tok_zB]),
        (lambda e: e.matmul(out=psB[:, 1536 + 0:1536 + 512], lhsT=ones_r[0:1, :],
                            rhs=brow1[0:1, 512:1024], start=True, stop=True),
         [tok_bqc, tok_bkc]),
    ], inc=sem_t)
    vb1 = OP("vector", lambda e: e.tensor_copy(out=bvb[:], in_=psB[:, 1024:2048]),
             waits=[bv2], inc=sem_v)
    tok_b1row = OP("sync", lambda e: e.dma_start(
        out=brow[0:32, :], in_=b1_h.rearrange("(t c) -> t c", t=32)),
        waits=[t2], inc=sem_ld, inc_n=16)
    t3 = OP("tensor", lambda e: e.matmul(out=psB[:, 1536:1568], lhsT=brow[0:32, :],
                                         rhs=id_f32[0:32, 0:32], start=True, stop=True),
            waits=[tok_b1row, vb1, tok_bqc, tok_bkc], inc=sem_t)
    tok_b1c = OP("vector", lambda e: e.tensor_copy(out=b1c[:], in_=psB[:, 1536:1568]),
                 waits=[t3], inc=sem_v)
    tok_biases = tok_b1c

    es_pa = ExitStack()
    kdT = es_pa.enter_context(nc.sbuf_tensor("kdT", [128, S, JT, 128], BF16))
    vd = es_pa.enter_context(nc.sbuf_tensor("vd", [128, S, 16 * 65], BF16))

    es_p1 = ExitStack()
    p1 = es_p1.enter_context
    wq_sb = p1(nc.sbuf_tensor("wq_sb", [128, JT, D], BF16))
    wk_sb = p1(nc.sbuf_tensor("wk_sb", [128, JT, D], BF16))
    wv_sb = p1(nc.sbuf_tensor("wv_sb", [128, JT, D], BF16))
    wtmp = [p1(nc.sbuf_tensor(f"wtmp{i}", [128, 1024], F32)) for i in range(3)]
    x_sb = [p1(nc.sbuf_tensor(f"x_sb{i}", [128, 2, D], F32)) for i in range(2)]
    dump = p1(nc.sbuf_tensor("dump", [128, D], BF16))
    sg = [p1(nc.sbuf_tensor(f"sg{i}", [128, 4, 4], F32)) for i in range(2)]
    nw = [p1(nc.sbuf_tensor(f"nw{i}", [128, 32], F32)) for i in range(2)]
    xn_tm = [p1(nc.sbuf_tensor(f"xn_tm{i}", [128, D], BF16)) for i in range(4)]
    xnT_st = [p1(nc.sbuf_tensor(f"xnT_st{i}", [128, JT, SEG], BF16)) for i in range(2)]
    qk_st = [p1(nc.sbuf_tensor(f"qk_st{i}", [128, JT, SEG], BF16)) for i in range(2)]
    v_st = [p1(nc.sbuf_tensor(f"v_st{i}", [128, 4, 16 * 65], BF16)) for i in range(1)]
    wacc = p1(nc.sbuf_tensor("wacc", [128, 4], F32))
    wred = p1(nc.sbuf_tensor("wred", [128, 1], F32))
    wg2 = [p1(nc.sbuf_tensor(f"wg2{i}", [128, 1024], BF16)) for i in range(1)]
    wtern = [p1(nc.sbuf_tensor(f"wtern{i}", [128, 1024], BF16)) for i in range(3)]
    wstage = [p1(nc.sbuf_tensor(f"wstage{i}", [128, 8, 128], BF16)) for i in range(2)]

    last_w_use = [None, None, None, None]
    wi = 0
    tok_wqkv = None
    for (wsb, whbm) in ((wq_sb, wq_h), (wk_sb, wk_h), (wv_sb, wv_h)):
        for dj in range(JT):
            sl = wi % 3
            wi += 1
            tokl = OP("sync", lambda e, sl=sl, whbm=whbm, dj=dj: e.dma_start(
                out=wtmp[sl][:], in_=whbm[dj * 128:(dj + 1) * 128, :]),
                waits=[last_w_use[sl]], inc=ld_w[sl], inc_n=16)
            tok_wqkv = OP("vector", lambda e, sl=sl, wsb=wsb, dj=dj: e.tensor_copy(
                out=wsb[:, dj, :], in_=wtmp[sl][:]), waits=[tokl], inc=sem_v)
            last_w_use[sl] = tok_wqkv

    proj_ps_last = {0: None, 1: None, 2: None, 3: None}
    lnps = {"prev_ev": None}
    pslot = {"i": 0}
    ln_xn_tok = [[None] * 4 for _ in range(S)]
    spq_tok = [None] * S
    spk_tok = [None] * S
    spv_tok = [None] * S
    spxn_tok = [None] * S
    kd_tok = [None] * S
    vd_tok = [None] * S
    w1t_last_sp = None

    x_use = [None, None]
    xn_use = [None, None, None, None]
    xnT_use = [None, None]
    qk_use = [None, None]
    v_use = [None, None]
    stats_use = [None, None]
    w1g_use = [None]
    wstage_use = [None, None]

    w1t_sp_tok = [None] * 32
    wscale_idx = 0
    wtern_idx = 0
    tok_acc_init = OP("vector", lambda e: e.memset(wacc[:], 0.0), inc=sem_v)
    acc_last = {1: tok_acc_init, 2: tok_acc_init}
    acc_parity = {1: 0, 2: 0}
    tok_thr = [None, None]
    tern_state = {"w1_psB_free": None, "w2_psB_free": None,
                  "tern_read": None}

    def emit_scale_tiles(n, queue="sync"):
        nonlocal wscale_idx, wi
        for _ in range(n):
            i = wscale_idx
            if i >= 64:
                return
            wscale_idx += 1
            if i < 32:
                src = w1_h[i * 128:(i + 1) * 128, :]
                widx = 1
            else:
                j = i - 32
                dt, qq = j // 4, j % 4
                src = w2_h[dt * 128:(dt + 1) * 128, qq * 1024:(qq + 1) * 1024]
                widx = 2
            sl = wi % 3
            wi += 1
            tokl = OP(queue, lambda e, sl=sl, src=src: e.dma_start(
                out=wtmp[sl][:], in_=src), waits=[last_w_use[sl]],
                inc=ld_w[sl], inc_n=16)
            tokr = OP("scalar", lambda e, sl=sl: e.activation(
                out=dump[:], in_=wtmp[sl][:], func=AF.Abs,
                accum_out=wred[:]), waits=[tokl, acc_last[widx], tok_biases],
                inc=sem_a)
            last_w_use[sl] = tokr
            par = acc_parity[widx]
            a_src = 2 * (widx - 1) + par
            a_dst = 2 * (widx - 1) + (1 - par)
            acc_parity[widx] = 1 - par
            acc_last[widx] = OP("vector",
                                lambda e, a_src=a_src, a_dst=a_dst: e.tensor_tensor(
                                    out=wacc[:, a_dst:a_dst + 1],
                                    in0=wacc[:, a_src:a_src + 1],
                                    in1=wred[:], op=ALU.add),
                                waits=[tokr], inc=sem_v)

    def finalize_scales(which):
        for widx, (th, thn) in ((1, (th1, th1n)), (2, (th2, th2n))):
            if widx != which:
                continue
            col = 2 * (widx - 1) + acc_parity[widx]
            oc = 1568 + 2 * widx
            tokm = OP("tensor", lambda e, col=col, oc=oc: e.matmul(
                out=psB[0:1, oc:oc + 1], lhsT=wacc[:, col:col + 1],
                rhs=ones_c[:, 0:1], start=True, stop=True),
                waits=[acc_last[widx], tok_b1c, tok_thr[0]], inc=sem_t)
            tokc = OP("vector", lambda e, widx=widx, oc=oc: e.tensor_scalar(
                out=scal[0:1, widx - 1:widx], in0=psB[0:1, oc:oc + 1],
                scalar1=0.5 / W_ELEMS, scalar2=0.5e-8, op0=ALU.mult, op1=ALU.max),
                waits=[tokm], inc=sem_v)
            tokb = OP("tensor", lambda e, widx=widx, oc=oc: e.matmul(
                out=psB[:, oc + 8:oc + 9], lhsT=ones_r[0:1, :],
                rhs=scal[0:1, widx - 1:widx], start=True, stop=True),
                waits=[tokc], inc=sem_t)
            tokp = OP("vector", lambda e, th=th, oc=oc: e.tensor_copy(
                out=th[:], in_=psB[:, oc + 8:oc + 9]), waits=[tokb], inc=sem_v)
            tok_thr[widx - 1] = OP("vector", lambda e, th=th, thn=thn: e.tensor_scalar(
                out=thn[:], in0=th[:], scalar1=-1.0, scalar2=None, op0=ALU.mult),
                waits=[tokp], inc=sem_v)

    tern_pend = []

    def _w1_tern_partB(ft, gsl, tokt):
        nonlocal w1t_last_sp
        ssl = ft % 2
        ev_parts = []
        for b4 in range(2):
            psl = pslot["i"] % 4
            pslot["i"] += 1
            c0 = psl * 512
            items = []
            for i4 in range(4):
                dt = b4 * 4 + i4
                items.append((
                    lambda e, gsl=gsl, dt=dt, i4=i4, c0=c0: e.matmul(
                        out=psA[:, c0 + i4 * 128:c0 + (i4 + 1) * 128],
                        lhsT=wtern[gsl][:, dt * 128:(dt + 1) * 128], rhs=id_bf[:],
                        start=True, stop=True),
                    [tokt, wstage_use[ssl], proj_ps_last[psl], tok_zA,
                     (sem_ld, 16)] if i4 == 0 else []))
            tokmm = GRP("tensor", items, inc=sem_t)
            tern_state["tern_read"] = tokmm
            toke = OP("scalar", lambda e, ssl=ssl, b4=b4, c0=c0: e.activation(
                out=wstage[ssl][:, 4 * b4:4 * b4 + 4, :], in_=psA[:, c0:c0 + 512],
                func=AF.Copy), waits=[tokmm], inc=sem_a)
            proj_ps_last[psl] = toke
            ev_parts.append(toke)
        w1t_last_sp = OP("gpsimd", lambda e, ssl=ssl, ft=ft: e.dma_start(
            out=w1t_h[:, ft, :, :], in_=wstage[ssl][:]),
            waits=ev_parts, inc=sp_w1[ssl], inc_n=16)
        w1t_sp_tok[ft] = w1t_last_sp
        wstage_use[ssl] = w1t_last_sp

    def emit_w1_tern_tiles(n):
        nonlocal wtern_idx, wi
        for _ in range(n):
            ft = wtern_idx
            if ft >= 32:
                while tern_pend:
                    _w1_tern_partB(*tern_pend.pop(0))
                return
            wtern_idx += 1
            sl = wi % 3
            wi += 1
            gsl = ft % 3
            tokl = OP("sync", lambda e, sl=sl, ft=ft: e.dma_start(
                out=wtmp[sl][:], in_=w1_h[ft * 128:(ft + 1) * 128, :]),
                waits=[last_w_use[sl]], inc=ld_w[sl], inc_n=16)
            tg2 = OP("vector", lambda e, sl=sl: e.tensor_scalar(
                out=wg2[0][:], in0=wtmp[sl][:], scalar1=th1n[:, 0:1], scalar2=None,
                op0=ALU.is_lt), waits=[tokl, tok_thr[0], w1g_use[0]], inc=sem_v)
            tokt = OP("vector", lambda e, sl=sl, gsl=gsl: e.scalar_tensor_tensor(
                out=wtern[gsl][:], in0=wtmp[sl][:], scalar=th1[:, 0:1],
                in1=wg2[0][:], op0=ALU.is_gt, op1=ALU.subtract),
                waits=[tg2, tern_state.get("tern_read")], inc=sem_v)
            last_w_use[sl] = tokt
            w1g_use[0] = tokt
            tern_pend.append((ft, gsl, tokt))
            if len(tern_pend) > 2:
                _w1_tern_partB(*tern_pend.pop(0))

    xng = [p1(nc.sbuf_tensor(f"xng{i}", [128, JT, SEG], BF16)) for i in range(2)]
    wstr = [p1(nc.sbuf_tensor(f"wstr{i}", [128, 8, 128], BF16)) for i in range(4)]
    hst = [p1(nc.sbuf_tensor(f"hst{i}", [128, SEG], BF16)) for i in range(2)]

    w2_fj = 0
    w2t_last_sp = None

    w2_pend = []

    def _w2_tern_partB(fj, gsl, tokt):
        nonlocal w2t_last_sp
        ssl = fj % 2
        ev_parts = []
        for b4 in range(2):
            psl = pslot["i"] % 4
            pslot["i"] += 1
            c0 = psl * 512
            items = []
            for i4 in range(4):
                dt = b4 * 4 + i4
                items.append((
                    lambda e, gsl=gsl, dt=dt, i4=i4, c0=c0: e.matmul(
                        out=psA[:, c0 + i4 * 128:c0 + (i4 + 1) * 128],
                        lhsT=wtern[gsl][:, dt * 128:(dt + 1) * 128], rhs=id_bf[:],
                        start=True, stop=True),
                    [tokt, tok_thr[0], tok_thr[1], wstage_use[ssl],
                     proj_ps_last[psl], tok_zA] if i4 == 0 else []))
            tokmm = GRP("tensor", items, inc=sem_t)
            tern_state["tern_read"] = tokmm
            toke = OP("scalar", lambda e, ssl=ssl, b4=b4, c0=c0: e.activation(
                out=wstage[ssl][:, 4 * b4:4 * b4 + 4, :], in_=psA[:, c0:c0 + 512],
                func=AF.Copy), waits=[tokmm], inc=sem_a)
            proj_ps_last[psl] = toke
            ev_parts.append(toke)
        w2t_last_sp = OP("gpsimd", lambda e, ssl=ssl, fj=fj: e.dma_start(
            out=w2t_h[:, fj, :], in_=wstage[ssl][:]),
            waits=ev_parts, inc=sp_w2[ssl], inc_n=16)
        wstage_use[ssl] = w2t_last_sp

    def emit_w2_tern(n):
        nonlocal w2_fj, wi
        for _ in range(n):
            fj = w2_fj
            if fj >= 32:
                while w2_pend:
                    _w2_tern_partB(*w2_pend.pop(0))
                return
            w2_fj += 1
            sl = wi % 3
            wi += 1
            tokl = OP("sync", lambda e, sl=sl, fj=fj: e.dma_start(
                out=wtmp[sl][:].rearrange("p (a b) -> p a b", a=8),
                in_=w2_h[:, fj * 128:(fj + 1) * 128].rearrange(
                    "(dt p) c -> p dt c", dt=8)),
                waits=[last_w_use[sl]], inc=ld_w[sl], inc_n=16)
            gsl = fj % 3
            tg2 = OP("vector", lambda e, sl=sl: e.tensor_scalar(
                out=wg2[0][:], in0=wtmp[sl][:], scalar1=th2n[:, 0:1], scalar2=None,
                op0=ALU.is_lt), waits=[tokl, tok_thr[1], w1g_use[0]], inc=sem_v)
            tokt = OP("vector", lambda e, sl=sl, gsl=gsl: e.scalar_tensor_tensor(
                out=wtern[gsl][:], in0=wtmp[sl][:], scalar=th2[:, 0:1],
                in1=wg2[0][:], op0=ALU.is_gt, op1=ALU.subtract),
                waits=[tg2, tern_state.get("tern_read")], inc=sem_v)
            last_w_use[sl] = tokt
            w1g_use[0] = tokt
            w2_pend.append((fj, gsl, tokt))
            if len(w2_pend) > 2:
                _w2_tern_partB(*w2_pend.pop(0))

    f1_step = [0]
    f1_xng_tok = [None]
    f1_gl = [0]
    xng_use = [None, None]
    wstr_use = [None, None, None, None]
    h_ev = [None, None, None, None]
    hst_use = [None, None]
    hsp_tok = [[None] * 32 for _ in range(S)]

    def emit_f1_steps(n):
        for _ in range(n):
            step = f1_step[0]
            if step >= 256:
                return
            f1_step[0] = step + 1
            g, ft = step // 32, step % 32
            if ft == 0:
                gl = g % 2
                f1_gl[0] = gl
                f1_xng_tok[0] = OP("sync", lambda e, g=g, gl=gl: e.dma_start(
                    out=xng[gl][:], in_=xnT_h[g]),
                    waits=[spxn_tok[g], xng_use[gl]], inc=ld_xg[gl], inc_n=16)
            gl = f1_gl[0]
            wsl = step % 4
            tok_w = OP("sync", lambda e, wsl=wsl, ft=ft: e.dma_start(
                out=wstr[wsl][:], in_=w1t_h[:, ft, :, :]),
                waits=[w1t_sp_tok[ft], wstr_use[wsl]], inc=ld_wc[wsl], inc_n=16)
            psl = step % 4
            c0 = psl * 512
            if h_ev[psl] is None:
                h_ev[psl] = (tok_zB, tok_zB, vb1, tok_thr[1])[psl]
            items = []
            for dj in range(JT):
                w0 = []
                if dj == 0:
                    w0 = [tok_w, f1_xng_tok[0], h_ev[psl]]
                items.append((
                    lambda e, wsl=wsl, dj=dj, c0=c0, gl=gl: e.matmul(
                        out=psB[:, c0:c0 + 512], lhsT=wstr[wsl][:, dj, :],
                        rhs=xng[gl][:, dj, :],
                        start=(dj == 0), stop=(dj == JT - 1)), w0))
            tokmm = GRP("tensor", items, inc=sem_t)
            wstr_use[wsl] = tokmm
            hsl = step % 2
            tok_g = OP("scalar", lambda e, ft=ft, c0=c0, hsl=hsl: e.activation(
                out=hst[hsl][:], in_=psB[:, c0:c0 + 512], func=AF.Gelu,
                bias=b1c[:, ft:ft + 1]),
                waits=[tokmm, hst_use[hsl]], inc=sem_a)
            h_ev[psl] = tok_g
            tok_sp = OP("gpsimd", lambda e, g=g, ft=ft, hsl=hsl: e.dma_start(
                out=hT_h[g][:, ft, :], in_=hst[hsl][:]),
                waits=[tok_g], inc=sp_h[hsl], inc_n=16)
            hsp_tok[g][ft] = tok_sp
            hst_use[hsl] = tok_sp
            if ft == 31:
                xng_use[g % 2] = (sem_t, sem_t.count)

    def wprep_chunk(s, sixth):
        if s == 0:
            emit_scale_tiles(1)
            if sixth == 5:
                finalize_scales(1)
        elif s == 1:
            emit_w1_tern_tiles(3 if sixth < 4 else 2)
            emit_scale_tiles(4 if sixth < 2 else 3)
        elif s == 2:
            emit_w1_tern_tiles(3 if sixth < 4 else 2)
            emit_scale_tiles(2)
            if sixth == 5:
                finalize_scales(2)
        else:
            if s == 3 and sixth == 0:
                emit_w1_tern_tiles(1)
            emit_w2_tern(2 if sixth == 0 else 1)
            emit_f1_steps(9 if sixth < 3 else 8)
            if s == 7 and sixth == 5:
                emit_w2_tern(32)
                emit_f1_steps(256)

    xnT_done = [None] * S

    def ln_stage(s):
        xsl = s % 2
        sp = s % 2
        xh_tok = []
        for hs in range(2):
            sl = hs
            r0 = s * SEG + hs * 256
            tokx = OP("sync", lambda e, sl=sl, r0=r0: e.dma_start(
                out=x_sb[sl][:],
                in_=x_h[r0:r0 + 256, :].rearrange("(n p) c -> p n c", n=2)),
                waits=[x_use[sl]], inc=ld_x[sl], inc_n=16)
            xh_tok.append(tokx)
        tok_sq = None
        tok_rd = None
        for tt in range(4):
            hs, half = tt // 2, tt % 2
            xt = x_sb[hs][:, half, :]
            tok_sq = OP("scalar", lambda e, xt=xt, sp=sp, tt=tt: e.activation(
                out=dump[:], in_=xt, func=AF.Square,
                accum_out=sg[sp][:, tt, 1:2]),
                waits=[xh_tok[hs], stats_use[sp], tok_biases], inc=sem_a)
            tok_rd = OP("vector", lambda e, xt=xt, sp=sp, tt=tt: e.reduce_sum(
                out=sg[sp][:, tt, 0:1], in_=xt, axis=AX.X),
                waits=[xh_tok[hs], stats_use[sp]], inc=sem_v)
        m4 = OP("vector", lambda e, sp=sp: e.tensor_scalar(
            out=nw[sp][:, 0:4], in0=sg[sp][:, :, 0], scalar1=1.0 / D,
            scalar2=None, op0=ALU.mult), waits=[tok_rd, tok_sq], inc=sem_v)
        e4 = OP("vector", lambda e, sp=sp: e.tensor_scalar(
            out=nw[sp][:, 4:8], in0=sg[sp][:, :, 1], scalar1=1.0 / D,
            scalar2=None, op0=ALU.mult), inc=sem_v)
        mm4 = OP("vector", lambda e, sp=sp: e.tensor_tensor(
            out=nw[sp][:, 8:12], in0=nw[sp][:, 0:4], in1=nw[sp][:, 0:4],
            op=ALU.mult), waits=[m4], inc=sem_v)
        v4 = OP("vector", lambda e, sp=sp: e.tensor_tensor(
            out=sg[sp][:, :, 2], in0=nw[sp][:, 4:8], in1=nw[sp][:, 8:12],
            op=ALU.subtract), waits=[e4, mm4], inc=sem_v)
        vh4 = OP("vector", lambda e, sp=sp: e.tensor_scalar(
            out=nw[sp][:, 16:20], in0=sg[sp][:, :, 2], scalar1=0.5,
            scalar2=0.5 * EPS, op0=ALU.mult, op1=ALU.add), waits=[v4], inc=sem_v)
        td4 = OP("vector", lambda e, sp=sp: e.tensor_scalar(
            out=nw[sp][:, 20:24], in0=sg[sp][:, :, 2], scalar1=0.5,
            scalar2=0.5 + 0.5 * EPS, op0=ALU.mult, op1=ALU.add), inc=sem_v)
        tok_y = OP("vector", lambda e, sp=sp: e.reciprocal(
            out=nw[sp][:, 24:28], in_=nw[sp][:, 20:24]), waits=[td4], inc=sem_v)
        yc = [24, 28]
        for it in range(4):
            ca, cb = yc[it % 2], yc[(it + 1) % 2]
            t1 = OP("vector", lambda e, sp=sp, ca=ca: e.tensor_tensor(
                out=nw[sp][:, 8:12], in0=nw[sp][:, ca:ca + 4],
                in1=nw[sp][:, ca:ca + 4], op=ALU.mult), waits=[tok_y], inc=sem_v)
            t2 = OP("vector", lambda e, sp=sp: e.tensor_tensor(
                out=nw[sp][:, 12:16], in0=nw[sp][:, 8:12], in1=nw[sp][:, 16:20],
                op=ALU.mult), waits=[t1, vh4], inc=sem_v)
            t3 = OP("vector", lambda e, sp=sp: e.tensor_scalar(
                out=nw[sp][:, 8:12], in0=nw[sp][:, 12:16], scalar1=-1.0,
                scalar2=1.5, op0=ALU.mult, op1=ALU.add), waits=[t2], inc=sem_v)
            tok_y = OP("vector", lambda e, sp=sp, ca=ca, cb=cb: e.tensor_tensor(
                out=nw[sp][:, cb:cb + 4], in0=nw[sp][:, ca:ca + 4],
                in1=nw[sp][:, 8:12], op=ALU.mult), waits=[t3], inc=sem_v)
        yfin = yc[0]
        ng4 = OP("vector", lambda e, sp=sp, yfin=yfin: e.tensor_tensor(
            out=nw[sp][:, 12:16], in0=nw[sp][:, 0:4], in1=nw[sp][:, yfin:yfin + 4],
            op=ALU.mult), waits=[tok_y], inc=sem_v)
        tok_nm = OP("vector", lambda e, sp=sp: e.tensor_scalar(
            out=nw[sp][:, 8:12], in0=nw[sp][:, 12:16], scalar1=-1.0,
            scalar2=None, op0=ALU.mult), waits=[ng4], inc=sem_v)
        for tt in range(4):
            hs, half = tt // 2, tt % 2
            xt = x_sb[hs][:, half, :]
            xsl2 = tt
            tok_xn = OP("scalar", lambda e, xt=xt, sp=sp, tt=tt, xsl2=xsl2, yfin=yfin: e.activation(
                out=xn_tm[xsl2][:], in_=xt, func=AF.Identity,
                scale=nw[sp][:, yfin + tt:yfin + tt + 1],
                bias=nw[sp][:, 8 + tt:9 + tt]),
                waits=[tok_nm, xn_use[xsl2]], inc=sem_a)
            ln_xn_tok[s][tt] = tok_xn
            stats_use[sp] = tok_xn

    def ln_transpose_tiles(s, tts):
        xsl = s % 2
        for tt in tts:
            xsl2 = tt
            tok_xn = ln_xn_tok[s][tt]
            for g4 in range(2):
                psl = pslot["i"] % 4
                pslot["i"] += 1
                c0 = psl * 512
                items = []
                for j4 in range(4):
                    dj = g4 * 4 + j4
                    items.append((
                        lambda e, dj=dj, j4=j4, xsl2=xsl2, c0=c0: e.matmul(
                            out=psA[:, c0 + j4 * 128:c0 + (j4 + 1) * 128],
                            lhsT=xn_tm[xsl2][:, dj * 128:(dj + 1) * 128], rhs=id_bf[:],
                            start=True, stop=True),
                        [tok_xn, xnT_use[xsl], tok_zA, proj_ps_last[psl],
                         spxn_tok[s - 2] if s >= 2 else None] if j4 == 0 else []))
                tokmm = GRP("tensor", items, inc=sem_t)
                lnps["prev_ev"] = OP("scalar",
                                     lambda e, xsl=xsl, g4=g4, tt=tt, c0=c0: e.activation(
                                         out=xnT_st[xsl][:, g4 * 4:(g4 + 1) * 4,
                                                         tt * 128:(tt + 1) * 128],
                                         in_=psA[:, c0:c0 + 512], func=AF.Copy),
                                     waits=[tokmm], inc=sem_a)
                proj_ps_last[psl] = lnps["prev_ev"]
            xn_use[tt] = (sem_t, sem_t.count)
            if tt == 1:
                x_use[0] = lnps["prev_ev"]
            if tt == 3:
                x_use[1] = lnps["prev_ev"]
        if tts and tts[-1] == 3:
            tok_xnT = lnps["prev_ev"]
            xnT_done[s] = tok_xnT
            spxn_tok[s] = OP("gpsimd", lambda e, xsl=xsl, s=s: e.dma_start(
                out=xnT_h[s], in_=xnT_st[xsl][:]), waits=[tok_xnT],
                inc=sp_xn[xsl], inc_n=16)

    def proj_stage(s):
        xsl = s % 2
        tok_xnT = xnT_done[s]
        for which in range(2):
            wsb = (wq_sb, wk_sb)[which]
            bcol = (bqc, bkc)[which]
            sp_sem = (sp_q, sp_k)[which]
            sp_arr = (spq_tok, spk_tok)[which]
            dst_h = (qT_h, kT_h)[which]
            qsl = which
            ev_toks = []
            for jt in range(JT):
                psl = pslot["i"] % 4
                pslot["i"] += 1
                c0 = psl * 512
                items = []
                for dj in range(JT):
                    w0 = [tok_xnT, tok_wqkv, qk_use[qsl], tok_zA,
                          proj_ps_last[psl]] if dj == 0 else []
                    items.append((
                        lambda e, wsb=wsb, dj=dj, jt=jt, c0=c0, xsl=xsl: e.matmul(
                            out=psA[:, c0:c0 + 512],
                            lhsT=wsb[:, dj, jt * 128:(jt + 1) * 128],
                            rhs=xnT_st[xsl][:, dj, :],
                            start=(dj == 0), stop=(dj == JT - 1)), w0))
                tokmm = GRP("tensor", items, inc=sem_t)
                tok_ev = OP("vector",
                            lambda e, qsl=qsl, jt=jt, c0=c0, bcol=bcol: e.tensor_scalar(
                                out=qk_st[qsl][:, jt, :], in0=psA[:, c0:c0 + 512],
                                scalar1=bcol[:, jt:jt + 1], scalar2=None,
                                op0=ALU.add),
                            waits=[tokmm], inc=sem_v)
                proj_ps_last[psl] = tok_ev
                ev_toks.append(tok_ev)
                if jt == 3:
                    wprep_chunk(s, which * 2)
            extra = None
            if which == 1:
                kd_tok[s] = OP("vector", lambda e, qsl=qsl, s=s: e.tensor_copy(
                    out=kdT[:, s, :, :], in_=qk_st[qsl][:, :, 0:512:4]),
                    waits=[ev_toks[-1]], inc=sem_v)
                extra = kd_tok[s]
            sp_arr[s] = OP("gpsimd", lambda e, qsl=qsl, s=s, dst_h=dst_h: e.dma_start(
                out=dst_h[s], in_=qk_st[qsl][:]),
                waits=[ev_toks[-1], extra], inc=sp_sem[qsl], inc_n=16)
            qk_use[qsl] = sp_arr[s]
            if s + 1 < S:
                ln_transpose_tiles(s + 1, [2 * which, 2 * which + 1])
            wprep_chunk(s, which * 2 + 1)

        vsl = 0
        ev_last = None
        for tt in range(4):
            for halfj in range(2):
                psl = pslot["i"] % 4
                pslot["i"] += 1
                c0 = psl * 512
                items = []
                for dj in range(JT):
                    w0 = [tok_xnT, tok_wqkv, v_use[vsl], vb1, tok_zA,
                          proj_ps_last[psl]] if dj == 0 else []
                    items.append((
                        lambda e, dj=dj, tt=tt, halfj=halfj, c0=c0, xsl=xsl: e.matmul(
                            out=psA[:, c0:c0 + 512],
                            lhsT=xnT_st[xsl][:, dj, tt * 128:(tt + 1) * 128],
                            rhs=wv_sb[:, dj, halfj * 512:(halfj + 1) * 512],
                            start=(dj == 0), stop=(dj == JT - 1)), w0))
                tokmm = GRP("tensor", items, inc=sem_t)
                ev_last = OP("vector",
                             lambda e, vsl=vsl, tt=tt, halfj=halfj, c0=c0: e.tensor_tensor(
                                 out=v_st[vsl][:, tt, :].rearrange(
                                     "p (h c) -> p h c", c=65)[:, halfj * 8:(halfj + 1) * 8, 0:64],
                                 in0=psA[:, c0:c0 + 512].rearrange("p (h c) -> p h c", c=64),
                                 in1=bvb[:, halfj * 512:(halfj + 1) * 512].rearrange(
                                     "p (h c) -> p h c", c=64),
                                 op=ALU.add), waits=[tokmm], inc=sem_v)
                proj_ps_last[psl] = ev_last
                if tt == 1 and halfj == 1:
                    wprep_chunk(s, 4)
        tok_ones = OP("vector", lambda e, vsl=vsl: e.memset(
            v_st[vsl][:].rearrange("p t (h c) -> p t h c", c=65)[:, :, :, 64:65], 1.0),
            waits=[ev_last], inc=sem_v)
        spv_tok[s] = OP("gpsimd", lambda e, vsl=vsl, s=s: e.dma_start(
            out=v_h[s], in_=v_st[vsl][:]), waits=[tok_ones], inc=sp_v[vsl], inc_n=16)
        v_use[vsl] = spv_tok[s]
        gsl = s % 2
        tok_vd0 = OP("sync", lambda e, s=s: e.dma_start(
            out=vd[:, s, :], in_=v_h[s][0:128:4].rearrange("p t c -> t p c")),
            waits=[spv_tok[s], vd_tok[s - 2] if s >= 2 else None],
            inc=ld_vd[gsl], inc_n=16)
        vd_tok[s] = OP("vector", lambda e, s=s: e.memset(
            vd[:, s, :].rearrange("p (h c) -> p h c", c=65)[:, :, 64:65], 4.0),
            waits=[tok_vd0], inc=sem_v)
        if s + 1 < S:
            ln_transpose_tiles(s + 1, [])
        wprep_chunk(s, 5)

        xnT_use[xsl] = (sem_t, sem_t.count)


    ln_stage(0)
    ln_transpose_tiles(0, [0, 1, 2, 3])
    emit_scale_tiles(26)
    for s in range(S):
        if s + 1 < S:
            ln_stage(s + 1)
        proj_stage(s)

    p1_end = [(sem_a, sem_a.count), (sem_v, sem_v.count)]
    p1_all = p1_end + [(sem_t, sem_t.count)]
    for _sp in (sp_q, sp_k, sp_v, sp_xn, sp_w1, sp_w2, sp_h):
        p1_all += [(_sp[0], _sp[0].count), (_sp[1], _sp[1].count)]
    es_p1.close()

    es_a = ExitStack()
    pa = es_a.enter_context
    wo_sb = pa(nc.sbuf_tensor("wo_sb", [128, JT, D], BF16))
    wtmp_a = [pa(nc.sbuf_tensor(f"wtmpa{i}", [128, 1024], F32)) for i in range(2)]
    qT_a = [pa(nc.sbuf_tensor(f"qTa{i}", [128, JT, SEG], BF16)) for i in range(2)]
    kT_a = [pa(nc.sbuf_tensor(f"kTa{i}", [128, JT, SEG], BF16)) for i in range(2)]
    vp_a = [pa(nc.sbuf_tensor(f"vpa{i}", [128, 4, 16 * 65], BF16)) for i in range(2)]
    E_sb = [pa(nc.sbuf_tensor(f"E{i}", [128, 2048], BF16)) for i in range(2)]
    ao = [pa(nc.sbuf_tensor(f"ao{i}", [128, 4, D], BF16)) for i in range(2)]
    aoT = [pa(nc.sbuf_tensor(f"aoT{i}", [128, JT, SEG], BF16)) for i in range(2)]
    bob = pa(nc.sbuf_tensor("bob", [128, D], F32))
    r_sb = [pa(nc.sbuf_tensor(f"r{i}", [128, 2, 5], F32)) for i in range(2)]
    tmp_sb = [pa(nc.sbuf_tensor(f"tmp{i}", [128, 640], F32)) for i in range(2)]
    s_st = [pa(nc.sbuf_tensor(f"sst{i}", [128, D], F32)) for i in range(2)]

    last_wa_use = [None, None]
    tok_wo = None
    for dj in range(JT):
        sl = dj % 2
        tokl = OP("sync", lambda e, sl=sl, dj=dj: e.dma_start(
            out=wtmp_a[sl][:], in_=wo_h[dj * 128:(dj + 1) * 128, :]),
            waits=[last_wa_use[sl], last_w_use[sl]] + p1_all,
            inc=ld_w[sl], inc_n=16)
        tok_wo = OP("vector", lambda e, sl=sl, dj=dj: e.tensor_copy(
            out=wo_sb[:, dj, :], in_=wtmp_a[sl][:]), waits=[tokl], inc=sem_v)
        last_wa_use[sl] = tok_wo

    tok_bo = OP("sync", lambda e: e.dma_start(
        out=brow1[:], in_=bo_h.rearrange("(a c) -> a c", a=1)),
        waits=p1_all, inc=sem_ld, inc_n=16)
    bo2 = GRP("tensor", [
        (lambda e: e.matmul(out=psB[:, 1024:1536], lhsT=ones_r[0:1, :],
                            rhs=brow1[0:1, 0:512], start=True, stop=True),
         [tok_bo] + p1_all),
        (lambda e: e.matmul(out=psB[:, 1536:2048], lhsT=ones_r[0:1, :],
                            rhs=brow1[0:1, 512:1024], start=True, stop=True), []),
    ], inc=sem_t)
    vb2 = OP("vector", lambda e: e.tensor_copy(out=bob[:], in_=psB[:, 1024:2048]),
             waits=[bo2], inc=sem_v)

    qa_tok = [None] * S
    ka_tok = [None] * S
    va_tok = [None] * S
    qa_use = [None, None]
    ka_use = [None, None]
    va_use = [None, None]

    def prefetch_seg(s):
        sl = s % 2
        qa_tok[s] = OP("sync", lambda e, s=s, sl=sl: e.dma_start(
            out=qT_a[sl][:], in_=qT_h[s]),
            waits=[spq_tok[s], qa_use[sl]] + p1_all, inc=ld_qa[sl], inc_n=16)
        ka_tok[s] = OP("sync", lambda e, s=s, sl=sl: e.dma_start(
            out=kT_a[sl][:], in_=kT_h[s]),
            waits=[spk_tok[s], ka_use[sl]] + p1_all, inc=ld_ka[sl], inc_n=16)
        va_tok[s] = OP("sync", lambda e, s=s, sl=sl: e.dma_start(
            out=vp_a[sl][:], in_=v_h[s]),
            waits=[spv_tok[s], va_use[sl]] + p1_all, inc=ld_va[sl], inc_n=16)

    prefetch_seg(0)
    prefetch_seg(1)

    slot_done = {}
    loc_free = {}
    cross_free = {}
    exp_a_tok = {}
    exp_b_tok = {}
    av_tok = {}
    ao_read_tok = [None, None]
    aoT_read_tok = [None, None]
    sst_use = [None, None]
    red_last = None
    u_global = 0
    g_slot_ref = {"g": 0}
    pend = None
    pseudo_pend = []

    for s in range(S):
        sl = s % 2
        aslot = s % 2
        valid = [o for o in OFFS if 0 <= s + o < S]
        nv = len(valid)

        sc_loc_pend = {}

        def stage1a(hp, qt, u, g, s=s, sl=sl, valid=valid, nv=nv):
            ps = psA if (g % 2 == 0) else psB
            q0 = qt * 128
            w0 = [loc_free.get(g - 2), qa_tok[s], ka_tok[s], tok_wo, tok_zA, tok_zB]
            if g < 2:
                w0 += p1_end + [vb2]
            items = []
            for kc in range(4):
                for hx in range(2):
                    pb = 64 * hx
                    items.append((
                        lambda e, ps=ps, kc=kc, hx=hx, pb=pb, sl=sl, hp=hp, q0=q0: e.matmul(
                            out=ps[:, hx * 1024 + kc * 128:hx * 1024 + (kc + 1) * 128],
                            lhsT=kT_a[sl][pb:pb + 64, hp, kc * 128:(kc + 1) * 128],
                            rhs=qT_a[sl][pb:pb + 64, hp, q0:q0 + 128],
                            start=True, stop=True),
                        w0 if (kc == 0 and hx == 0) else []))
            sc_loc_pend[u] = GRP("tensor", items, inc=sem_t)

        def stage1b(hp, qt, u, g, s=s, sl=sl, valid=valid, nv=nv):
            ps = psA if (g % 2 == 0) else psB
            q0 = qt * 128
            sc_loc = sc_loc_pend.pop(u)
            items = []
            for j, o in enumerate(valid):
                so = s + o
                for hx in range(2):
                    pb = 64 * hx
                    items.append((
                        lambda e, ps=ps, j=j, so=so, hx=hx, pb=pb, sl=sl, hp=hp, q0=q0: e.matmul(
                            out=ps[:, hx * 1024 + 512 + j * 128:hx * 1024 + 512 + (j + 1) * 128],
                            lhsT=kdT[pb:pb + 64, so, hp, :],
                            rhs=qT_a[sl][pb:pb + 64, hp, q0:q0 + 128],
                            start=True, stop=True),
                        [kd_tok[so], cross_free.get(g - 2)] if (j == 0 and hx == 0)
                        else [kd_tok[so]]))
            sc_cr = GRP("tensor", items, inc=sem_t)
            exp_a_tok[u] = OP("scalar", lambda e, ps=ps, u=u: e.activation(
                out=E_sb[u % 2][:].rearrange("p (h c) -> p h c", h=2)[:, :, 0:512],
                in_=ps.rearrange("p (h c) -> p h c", h=2)[:, :, 0:512],
                func=AF.Exp, scale=0.125),
                waits=[sc_loc, av_tok.get(u - 2)], inc=sem_a)
            loc_free[g] = exp_a_tok[u]
            exp_b_tok[u] = OP("scalar", lambda e, ps=ps, u=u, nv=nv: e.activation(
                out=E_sb[u % 2][:].rearrange("p (h c) -> p h c", h=2)[:, :, 512:512 + 128 * nv],
                in_=ps.rearrange("p (h c) -> p h c", h=2)[:, :, 512:512 + 128 * nv],
                func=AF.Exp, scale=0.125),
                waits=[sc_cr], inc=sem_a)

        def stage2(hp, qt, u, g, s=s, sl=sl, aslot=aslot, valid=valid, nv=nv):
            nonlocal red_last
            ps = psA if (g % 2 == 0) else psB
            items = []
            for hx in range(2):
                vcol = 65 * (2 * hp + hx)
                base = hx * 1024 + 512
                for kc in range(4):
                    items.append((
                        lambda e, ps=ps, u=u, hx=hx, base=base, kc=kc, sl=sl, vcol=vcol: e.matmul(
                            out=ps[:, base:base + 65],
                            lhsT=E_sb[u % 2][:, hx * 1024 + kc * 128:hx * 1024 + (kc + 1) * 128],
                            rhs=vp_a[sl][:, kc, vcol:vcol + 65],
                            start=(kc == 0), stop=(kc == 3)),
                        [exp_a_tok[u], exp_b_tok[u], va_tok[s]]
                        if (hx == 0 and kc == 0) else []))
                for j, o in enumerate(valid):
                    so = s + o
                    blk = base + 72 * (1 + j)
                    items.append((
                        lambda e, ps=ps, u=u, hx=hx, j=j, so=so, blk=blk, vcol=vcol: e.matmul(
                            out=ps[:, blk:blk + 65],
                            lhsT=E_sb[u % 2][:, hx * 1024 + 512 + j * 128:hx * 1024 + 512 + (j + 1) * 128],
                            rhs=vd[:, so, vcol:vcol + 65],
                            start=True, stop=True),
                        [vd_tok[so]]))
            av_tok[u] = GRP("tensor", items, inc=sem_t)
            rsl = u % 2
            nc1 = 1 + nv
            tok_rc = OP("vector", lambda e, ps=ps, rsl=rsl, nc1=nc1: e.reciprocal(
                out=r_sb[rsl][:, :, 0:nc1],
                in_=ps.rearrange("p (h c) -> p h c", h=2)[:, :, 576:576 + 72 * (nc1 - 1) + 1:72]),
                waits=[av_tok[u]], inc=sem_v)
            tok_mul = OP("vector", lambda e, ps=ps, rsl=rsl, nc1=nc1: e.tensor_tensor(
                out=tmp_sb[rsl][:].rearrange("p (x c j) -> p x c j", x=2, j=64)[:, :, 0:nc1, :],
                in0=ps.rearrange("p (h q) -> p h q", h=2)[:, :, 512:512 + 72 * nc1]
                      .rearrange("p h (c j) -> p h c j", j=72)[:, :, :, 0:64],
                in1=r_sb[rsl][:, :, 0:nc1].broadcast_to([128, 2, nc1, 64]),
                op=ALU.mult), waits=[tok_rc], inc=sem_v)
            slot_done[g] = tok_mul
            cross_free[g] = tok_mul

            def _red(e, rsl=rsl, nc1=nc1, aslot=aslot, qt=qt, hp=hp):
                with nc.allow_low_precision(reason="bf16 attnout operand"):
                    return e.tensor_reduce(
                        out=ao[aslot][:, qt, 128 * hp:128 * hp + 128].rearrange(
                            "p (h j) -> p h j", h=2),
                        in_=tmp_sb[rsl][:].rearrange(
                            "p (x c j) -> p x j c", x=2, j=64)[:, :, :, 0:nc1],
                        axis=AX.X, op=ALU.add)
            red_last = OP("vector", _red, waits=[ao_read_tok[aslot], tok_mul],
                          inc=sem_v)

        idx = 0
        for hp in range(8):
            for qt in range(4):
                u = u_global
                u_global += 1
                g = g_slot_ref["g"]
                g_slot_ref["g"] += 1
                stage1a(hp, qt, u, g)
                if pend is not None:
                    pend()
                stage1b(hp, qt, u, g)
                pend = (lambda hp=hp, qt=qt, u=u, g=g: stage2(hp, qt, u, g))
                if pseudo_pend and idx in (8, 20):
                    if pend is not None:
                        pend()
                        pend = None
                    pseudo_pend.pop(0)()
                    pseudo_pend.pop(0)()
                idx += 1
        pend()
        pend = None
        qa_use[sl] = ka_use[sl] = va_use[sl] = (sem_t, sem_t.count)
        if s + 2 < S:
            prefetch_seg(s + 2)

        def make_pseudo(s=s, sl=sl, aslot=aslot, red_of_seg=red_last):
            st = {"pt_evac": None, "pt_last_mm": None, "o_last_mm": None}

            def tr_slot(half4):
                g = g_slot_ref["g"]
                g_slot_ref["g"] += 1
                ps = psA if (g % 2 == 0) else psB
                mm_toks = []
                first = True
                for jt_ in range(4 * half4, 4 * half4 + 4):
                    items = []
                    for tt in range(4):
                        w0 = []
                        if first:
                            w0 = [loc_free.get(g - 2), cross_free.get(g - 2),
                                  slot_done.get(g - 2), red_of_seg,
                                  aoT_read_tok[aslot]]
                            first = False
                        items.append((
                            lambda e, ps=ps, aslot=aslot, tt=tt, jt_=jt_: e.matmul(
                                out=ps[:, (jt_ % 4) * 512 + tt * 128:(jt_ % 4) * 512 + (tt + 1) * 128],
                                lhsT=ao[aslot][:, tt, jt_ * 128:(jt_ + 1) * 128],
                                rhs=id_bf[:], start=True, stop=True), w0))
                    mm_toks.append(GRP("tensor", items, inc=sem_t))
                ev = None
                for i, jt_ in enumerate(range(4 * half4, 4 * half4 + 4)):
                    ev = OP("scalar", lambda e, ps=ps, aslot=aslot, jt_=jt_: e.activation(
                        out=aoT[aslot][:, jt_, :],
                        in_=ps[:, (jt_ % 4) * 512:(jt_ % 4) * 512 + 512],
                        func=AF.Identity), waits=[mm_toks[i]], inc=sem_a)
                slot_done[g] = ev
                loc_free[g] = ev
                cross_free[g] = ev
                st["pt_evac"] = ev
                st["pt_last_mm"] = mm_toks[-1]
                if half4 == 1:
                    ao_read_tok[aslot] = st["pt_last_mm"]

            def op_slot(tg):
                g = g_slot_ref["g"]
                g_slot_ref["g"] += 1
                ps = psA if (g % 2 == 0) else psB
                dve_ev = None
                for ti, tt in enumerate((2 * tg, 2 * tg + 1)):
                    c0 = ti * 1024
                    items = []
                    for halfj in range(2):
                        for jt_ in range(JT):
                            w0 = []
                            if halfj == 0 and jt_ == 0 and ti == 0:
                                w0 = [loc_free.get(g - 2), cross_free.get(g - 2),
                                      slot_done.get(g - 2), st["pt_evac"]]
                            items.append((
                                lambda e, ps=ps, aslot=aslot, tt=tt, jt_=jt_, halfj=halfj, c0=c0: e.matmul(
                                    out=ps[:, c0 + halfj * 512:c0 + (halfj + 1) * 512],
                                    lhsT=aoT[aslot][:, jt_, tt * 128:(tt + 1) * 128],
                                    rhs=wo_sb[:, jt_, halfj * 512:(halfj + 1) * 512],
                                    start=(jt_ == 0), stop=(jt_ == JT - 1)), w0))
                    st["o_last_mm"] = GRP("tensor", items, inc=sem_t)
                    ssl = tt % 2
                    tok_oe = OP("vector", lambda e, ps=ps, c0=c0, ssl=ssl: e.tensor_tensor(
                        out=s_st[ssl][:], in0=ps[:, c0:c0 + 1024], in1=bob[:], op=ALU.add),
                        waits=[st["o_last_mm"], vb2, sst_use[ssl]], inc=sem_v)
                    dve_ev = tok_oe
                    r0 = s * SEG + tt * 128
                    sst_use[ssl] = OP("gpsimd", lambda e, ssl=ssl, r0=r0: e.dma_start(
                        out=s_h[r0:r0 + 128, :], in_=s_st[ssl][:]),
                        waits=[tok_oe], inc=sp_s[ssl], inc_n=16)
                slot_done[g] = dve_ev
                loc_free[g] = dve_ev
                cross_free[g] = dve_ev
                if tg == 1:
                    aoT_read_tok[aslot] = st["o_last_mm"]

            return [lambda: tr_slot(0), lambda: tr_slot(1),
                    lambda: op_slot(0), lambda: op_slot(1)]

        pseudo_pend = make_pseudo()
        if s == S - 1:
            while pseudo_pend:
                pseudo_pend.pop(0)()

    sp_s_final = [(sp_s[0], sp_s[0].count), (sp_s[1], sp_s[1].count)]
    a_end = [(sem_a, sem_a.count), (sem_v, sem_v.count)]
    a_all = a_end + [(sem_t, sem_t.count)]
    es_a.close()
    es_pa.close()

    es_f2 = ExitStack()
    f2 = es_f2.enter_context
    w2t_sb = f2(nc.sbuf_tensor("w2t_sb", [128, 32, 1024], BF16))
    hT_g = [f2(nc.sbuf_tensor(f"hTg{i}", [128, 32, SEG], BF16)) for i in range(2)]
    s_f = [f2(nc.sbuf_tensor(f"s_f{i}", [128, D], F32)) for i in range(2)]
    x_f = [f2(nc.sbuf_tensor(f"x_f{i}", [128, D], F32)) for i in range(2)]
    gp = [f2(nc.sbuf_tensor(f"gp{i}", [128, D], F32)) for i in range(2)]
    ga = [f2(nc.sbuf_tensor(f"ga{i}", [128, D], F32)) for i in range(2)]
    ot = [f2(nc.sbuf_tensor(f"ot{i}", [128, D], F32)) for i in range(2)]

    b2b = f2(nc.sbuf_tensor("b2b", [128, D], F32))
    tok_b2r = OP("sync", lambda e: e.dma_start(
        out=brow1[:], in_=b2_h.rearrange("(a c) -> a c", a=1)),
        waits=a_all, inc=sem_ld, inc_n=16)
    b22 = GRP("tensor", [
        (lambda e: e.matmul(out=psB[:, 1024:1536], lhsT=ones_r[0:1, :],
                            rhs=brow1[0:1, 0:512], start=True, stop=True),
         [tok_b2r] + a_all),
        (lambda e: e.matmul(out=psB[:, 1536:2048], lhsT=ones_r[0:1, :],
                            rhs=brow1[0:1, 512:1024], start=True, stop=True), []),
    ], inc=sem_t)
    vb3 = OP("vector", lambda e: e.tensor_copy(out=b2b[:], in_=psB[:, 1024:2048]),
             waits=[b22], inc=sem_v)

    tok_w2t_c = []
    for ch in range(4):
        tok_w2t_c.append(OP("sync", lambda e, ch=ch: e.dma_start(
            out=w2t_sb[:, ch * 8:(ch + 1) * 8, :], in_=w2t_h[:, ch * 8:(ch + 1) * 8, :]),
            waits=([w2t_last_sp,
                    (sp_w2[0], sp_w2[0].count), (sp_w2[1], sp_w2[1].count)]
                   + a_all) if ch == 0 else [],
            inc=ld_big, inc_n=16))
    tok_w2t = tok_w2t_c[3]

    hg_use = [None, None]
    sf_use = [None, None]
    xf_use = [None, None]
    gp_use = [None, None]
    ga_use = [None, None]
    ot_use = [None, None]
    o_ev = [None, None]

    for g in range(S):
        gl = g % 2
        tok_hg_c = []
        for ch in range(4):
            tok_hg_c.append(OP("sync", lambda e, g=g, gl=gl, ch=ch: e.dma_start(
                out=hT_g[gl][:, ch * 8:(ch + 1) * 8, :],
                in_=hT_h[g][:, ch * 8:(ch + 1) * 8, :]),
                waits=([hsp_tok[g][30], hsp_tok[g][31], hg_use[gl]] + a_all)
                      if ch == 0 else [],
                inc=ld_hg[gl], inc_n=16))
        tok_hg = tok_hg_c[0]
        for tt in range(4):
            ssl = tt % 2
            r0 = g * SEG + tt * 128
            tok_sf = OP("sync", lambda e, ssl=ssl, r0=r0: e.dma_start(
                out=s_f[ssl][:], in_=s_h[r0:r0 + 128, :]),
                waits=[sf_use[ssl]] + sp_s_final + a_all,
                inc=ld_sf[ssl], inc_n=16)
            tok_xf = OP("sync", lambda e, ssl=ssl, r0=r0: e.dma_start(
                out=x_f[ssl][:], in_=x_h[r0:r0 + 128, :]),
                waits=[xf_use[ssl]] + a_all, inc=ld_xf[ssl], inc_n=16)
            c0 = (tt % 2) * 1024
            items = []
            for fj in range(32):
                for halfj in range(2):
                    w0 = [o_ev[tt % 2]] if (fj == 0 and halfj == 0) else []
                    if fj % 8 == 0 and halfj == 0:
                        w0 = w0 + [tok_w2t_c[fj // 8], tok_hg_c[fj // 8]]
                    if fj == 0 and halfj == 0 and g == 0 and tt < 2:
                        w0 = w0 + a_end
                    items.append((
                        lambda e, fj=fj, halfj=halfj, c0=c0, gl=gl, tt=tt: e.matmul(
                            out=psA[:, c0 + halfj * 512:c0 + (halfj + 1) * 512],
                            lhsT=hT_g[gl][:, fj, tt * 128:(tt + 1) * 128],
                            rhs=w2t_sb[:, fj, halfj * 512:(halfj + 1) * 512],
                            start=(fj == 0), stop=(fj == 31)), w0))
            tok_f2 = GRP("tensor", items, inc=sem_t)
            tok_gp = OP("vector", lambda e, c0=c0, ssl=ssl: e.tensor_tensor(
                out=gp[ssl][:], in0=psA[:, c0:c0 + 1024], in1=b2b[:], op=ALU.add),
                waits=[tok_f2, vb3, ga_use[ssl]], inc=sem_v)
            o_ev[tt % 2] = tok_gp
            tok_ga = OP("scalar", lambda e, ssl=ssl: e.activation(
                out=ga[ssl][:], in_=gp[ssl][:], func=AF.Silu), waits=[tok_gp],
                inc=sem_a)
            tok_m = OP("vector", lambda e, ssl=ssl: e.tensor_tensor(
                out=gp[ssl][:], in0=ga[ssl][:], in1=s_f[ssl][:], op=ALU.mult),
                waits=[tok_ga, tok_sf, ot_use[ssl]], inc=sem_v)
            tok_o = OP("vector", lambda e, ssl=ssl: e.tensor_tensor(
                out=ot[ssl][:], in0=gp[ssl][:], in1=x_f[ssl][:], op=ALU.add),
                waits=[tok_xf, tok_m], inc=sem_v)
            ga_use[ssl] = tok_o
            sf_use[ssl] = tok_o
            xf_use[ssl] = tok_o
            tok_st = OP("gpsimd", lambda e, ssl=ssl, r0=r0: e.dma_start(
                out=out_h[r0:r0 + 128, :], in_=ot[ssl][:]),
                waits=[tok_o], inc=sem_out, inc_n=16)
            ot_use[ssl] = tok_st
        hg_use[gl] = (sem_t, sem_t.count)

    OP("gpsimd", lambda e: e.engine_nop(), waits=[(sem_out, sem_out.count)])

    es_f2.close()
    S_.emit()
    return nc


def _build():
    nc = bass.Bass("TRN2", target_bir_lowering=False,
                   detect_race_conditions=False)
    ctx = ExitStack()
    with ctx:
        build_program(nc, ctx)
    return nc


def kernel(L0_attn_x, L0_attn_Wq, L0_attn_bq, L0_attn_Wk, L0_attn_bk,
           L0_attn_Wv, L0_attn_bv, L0_attn_Wo, L0_attn_bo,
           L0_ffn_W1, L0_ffn_b1, L0_ffn_W2, L0_ffn_b2):
    x = np.ascontiguousarray(np.asarray(L0_attn_x, dtype=np.float32))
    base = {
        "wq": np.ascontiguousarray(np.asarray(L0_attn_Wq, np.float32)),
        "wk": np.ascontiguousarray(np.asarray(L0_attn_Wk, np.float32)),
        "wv": np.ascontiguousarray(np.asarray(L0_attn_Wv, np.float32)),
        "wo": np.ascontiguousarray(np.asarray(L0_attn_Wo, np.float32)),
        "bq": np.ascontiguousarray(np.asarray(L0_attn_bq, np.float32)),
        "bk": np.ascontiguousarray(np.asarray(L0_attn_bk, np.float32)),
        "bv": np.ascontiguousarray(np.asarray(L0_attn_bv, np.float32)),
        "bo": np.ascontiguousarray(np.asarray(L0_attn_bo, np.float32)),
        "w1": np.ascontiguousarray(np.asarray(L0_ffn_W1, np.float32)),
        "b1": np.ascontiguousarray(np.asarray(L0_ffn_b1, np.float32)),
        "w2": np.ascontiguousarray(np.asarray(L0_ffn_W2, np.float32)),
        "b2": np.ascontiguousarray(np.asarray(L0_ffn_b2, np.float32)),
    }
    nc = _build()
    in_maps = [dict(base, x=np.ascontiguousarray(x[b])) for b in range(8)]
    res = run_bass_kernel_spmd(nc, in_maps, core_ids=list(range(8)))
    return np.stack([res.results[b]["out"] for b in range(8)], axis=0)
